# version 16
# baseline (speedup 1.0000x reference)
import numpy as np
import contextlib
import concourse.bass as bass
import concourse.mybir as mybir
from concourse.bass_utils import run_bass_kernel_spmd

F32 = mybir.dt.float32
BF16 = mybir.dt.bfloat16
F16 = mybir.dt.float16
I32 = mybir.dt.int32
AF = mybir.ActivationFunctionType
ALU = mybir.AluOpType
ds = bass.ds

H = 4096
DIN = 8192
NST = 16
RDT = 256
NCORE = 8
CT = 512
NCHUNK = 16
NDT = 8
LN_EPS = 1e-5
RMS_EPS = 1e-6
RG = [list(range(NCORE))]


def build_program(debug=False):
    nc = bass.Bass("TRN2", target_bir_lowering=False)

    x0T = nc.dram_tensor("x0T", [H, 1024], F32, kind="ExternalInput").ap()
    winx = nc.dram_tensor("winx", [1024, 4096], F32, kind="ExternalInput").ap()
    wing = nc.dram_tensor("wing", [1024, 4096], F32, kind="ExternalInput").ap()
    wxT = nc.dram_tensor("wxT", [128, 8 * 288], F32, kind="ExternalInput").ap()
    wdtT = nc.dram_tensor("wdtT", [128, 2048], F32, kind="ExternalInput").ap()
    wout = nc.dram_tensor("wout", [128, 32768], F32, kind="ExternalInput").ap()
    q3c = nc.dram_tensor("q3c", [512, 4096], I32, kind="ExternalInput").ap()
    rngc = nc.dram_tensor("rngc", [512, 256], F16, kind="ExternalInput").ap()
    biasc = nc.dram_tensor("biasc", [128, 32], F16, kind="ExternalInput").ap()
    alog = nc.dram_tensor("alog", [128, 128], F32, kind="ExternalInput").ap()
    convw = nc.dram_tensor("convw", [128, 32], F32, kind="ExternalInput").ap()
    convb = nc.dram_tensor("convb", [128, 8], F32, kind="ExternalInput").ap()
    bdt = nc.dram_tensor("bdt", [128, 8], F32, kind="ExternalInput").ap()
    dvec = nc.dram_tensor("dvec", [128, 8], F32, kind="ExternalInput").ap()
    oT = nc.dram_tensor("oT", [H, 1024], F32, kind="ExternalOutput").ap()

    winx_bf = nc.dram_tensor("winx_bf", [1024, 4096], BF16)
    wing_bf = nc.dram_tensor("wing_bf", [1024, 4096], BF16)
    woutp_bf = nc.dram_tensor("woutp_bf", [128, 32768], BF16)
    wout_all = nc.dram_tensor("wout_all", [8 * 128, 32768], BF16)
    hsln_part = nc.dram_tensor("hsln_part", [H, 1024], BF16)
    hsln_all = nc.dram_tensor("hsln_all", [8 * H, 1024], BF16)
    ssmpart = nc.dram_tensor("ssmpart", [NCHUNK, 288, CT], F32)
    ssmred = nc.dram_tensor("ssmred", [NCHUNK, 288, CT], F32)
    wffn_part = nc.dram_tensor("wffn_part", [128, 16384], BF16)
    wffn_all = nc.dram_tensor("wffn_all", [8 * 128, 16384], BF16)

    yag_e = nc.dram_tensor("yag_e", [8 * 8, 1024, CT], BF16)
    yag_o = nc.dram_tensor("yag_o", [8 * 8, 1024, CT], BF16)
    ybf_e = nc.dram_tensor("ybf_e", [8, 1024, CT], BF16)
    ybf_o = nc.dram_tensor("ybf_o", [8, 1024, CT], BF16)
    x1d = nc.dram_tensor("x1d", [2, H, CT], F32)
    dbgu = nc.dram_tensor("dbgu", [NCHUNK, 1024, CT], BF16)
    dbgg = nc.dram_tensor("dbgg", [NCHUNK, 1024, CT], BF16)
    dbgp = nc.dram_tensor("dbgp", [2, 1024, CT], F32)
    dbgr = nc.dram_tensor("dbgr", [4096, CT], BF16)
    dbgw = nc.dram_tensor("dbgw", [128, 4096], BF16)

    es = contextlib.ExitStack()
    with es:
        ab = es.enter_context(nc.sbuf_tensor("ab", [128, 49152], BF16))
        af = es.enter_context(nc.sbuf_tensor("af", [128, 20992], F32))
        id_bf = es.enter_context(nc.sbuf_tensor("id_bf", [128, 128], BF16))
        id_f32 = es.enter_context(nc.sbuf_tensor("id_f32", [128, 128], F32))
        sel_b = es.enter_context(nc.sbuf_tensor("sel_b", [16, 2048], BF16))
        rowa = es.enter_context(nc.sbuf_tensor("rowa", [1, 512], F32))
        rowb = es.enter_context(nc.sbuf_tensor("rowb", [1, 512], F32))
        rowc = es.enter_context(nc.sbuf_tensor("rowc", [1, 512], F32))
        rowd = es.enter_context(nc.sbuf_tensor("rowd", [1, 512], F32))
        rowe = es.enter_context(nc.sbuf_tensor("rowe", [1, 512], F32))
        ones1r = es.enter_context(nc.sbuf_tensor("ones1r", [1, 128], F32))
        ones16r = es.enter_context(nc.sbuf_tensor("ones16r", [1, 16], F32))
        ones128 = es.enter_context(nc.sbuf_tensor("ones128", [128, 1], F32))
        ones16 = es.enter_context(nc.sbuf_tensor("ones16", [16, 1], F32))
        wxT_sb = es.enter_context(nc.sbuf_tensor("wxT_sb", [128, 8 * 288], BF16))
        wdtT_sb = es.enter_context(nc.sbuf_tensor("wdtT_sb", [128, 2048], BF16))
        A_sb = es.enter_context(nc.sbuf_tensor("A_sb", [128, 128], F32))

        convw_sb = es.enter_context(nc.sbuf_tensor("convw_sb", [128, 32], F32))
        convb_sb = es.enter_context(nc.sbuf_tensor("convb_sb", [128, 8], F32))
        bdt_sb = es.enter_context(nc.sbuf_tensor("bdt_sb", [128, 8], F32))
        dvec_sb = es.enter_context(nc.sbuf_tensor("dvec_sb", [128, 8], F32))
        biasc_sb = es.enter_context(nc.sbuf_tensor("biasc_sb", [128, 32], F32))
        carry = es.enter_context(nc.sbuf_tensor("carry", [128, 128], F32))
        halo = es.enter_context(nc.sbuf_tensor("halo", [128, 24], F32))
        epsc = es.enter_context(nc.sbuf_tensor("epsc", [1, 4], F32))
        one128 = es.enter_context(nc.sbuf_tensor("one128", [128, 1], F32))
        sclT = es.enter_context(nc.sbuf_tensor("sclT", [128, 128], BF16))
        mnnT = es.enter_context(nc.sbuf_tensor("mnnT", [128, 128], BF16))

        ps_m0 = es.enter_context(nc.psum_tensor("ps_m0", [128, 512], F32))
        ps_m1 = es.enter_context(nc.psum_tensor("ps_m1", [128, 512], F32))
        ps_a = es.enter_context(nc.psum_tensor("ps_a", [128, 512], F32))
        ps_b = es.enter_context(nc.psum_tensor("ps_b", [128, 512], F32))
        ps_c = es.enter_context(nc.psum_tensor("ps_c", [128, 512], F32))
        ps_d = es.enter_context(nc.psum_tensor("ps_d", [128, 512], F32))
        ps_e = es.enter_context(nc.psum_tensor("ps_e", [128, 512], F32))
        ps_f = es.enter_context(nc.psum_tensor("ps_f", [128, 512], F32))

        sem_ld = es.enter_context(nc.semaphore("sem_ld"))
        sem_g = es.enter_context(nc.semaphore("sem_g"))
        sem_cc = es.enter_context(nc.semaphore("sem_cc"))
        sem_t = es.enter_context(nc.semaphore("sem_t"))
        sem_v = es.enter_context(nc.semaphore("sem_v"))
        sem_s = es.enter_context(nc.semaphore("sem_s"))
        sem_lp = es.enter_context(nc.semaphore("sem_lp"))
        sem_gp = es.enter_context(nc.semaphore("sem_gp"))

        RHS_O = 0
        WM_O = 16384
        XBF_O = 24576
        GBF_O = 28672
        TSN_O = 32768
        YST_O = 33792
        HST_O = 34816
        BN_O = 35840
        DQ_O = 36864
        Y_O = 0
        XLN_O = 0
        WOUT_O = 32768
        WFF_O = 32768
        WNAT_O = 16384

        DT_O = 0
        DA_O = 4096
        DTU_O = 6144
        SAB_O = 8192
        YA_O = 10240
        TB_O = 14336
        STG_O = 15360
        SQ_O = 16896
        T0_O = 17920
        ACC_O = 18944
        BCR_O = 19968
        XCAT_O = 0
        XI_O = 0
        X1_O = 14336
        SCR_O = 15360

        def abf(off, n, p=128):
            return ab[0:p, off:off + n]

        def af32(off, n, p=128):
            return af[0:p, off:off + n]

        prog = {k: [] for k in ("sync", "vector", "scalar", "tensor", "gpsimd")}

        class S:
            __slots__ = ("h", "n")

            def __init__(self, h):
                self.h = h
                self.n = 0

        Sld, Sg, Scc, St_, Sv, Ss = (S(sem_ld), S(sem_g), S(sem_cc),
                                     S(sem_t), S(sem_v), S(sem_s))
        Slp, Sgp = S(sem_lp), S(sem_gp)

        def EM(e, fn, sem=None, k=1):
            if sem is None:
                prog[e].append(fn)
                return None
            sem.n += k
            prog[e].append(lambda eng, fn=fn, s=sem, k=k: fn(eng).then_inc(s.h, k))
            return sem.n

        def WT(e, sem, v):
            if v > 0:
                prog[e].append(lambda eng, s=sem, v=v: eng.wait_ge(s.h, v))

        def CERT(queue):
            qs, ps = (Sld, Slp) if queue == "sync" else (Sg, Sgp)
            n = qs.n
            ps.n += 1
            prog[queue].append(
                lambda eng, n=n, q=qs, p=ps: eng.wait_ge(q.h, n).then_inc(p.h, 1))
            return ps.n

        hold = {}

        EM("gpsimd", lambda e: e.memset(id_bf[:], 0.0))
        EM("gpsimd", lambda e: e.affine_select(
            out=id_bf[:], in_=id_bf[:], compare_op=ALU.not_equal, fill=1.0,
            base=0, pattern=[[-1, 128]], channel_multiplier=1))
        EM("gpsimd", lambda e: e.memset(id_f32[:], 0.0))
        v_ident = EM("gpsimd", lambda e: e.affine_select(
            out=id_f32[:], in_=id_f32[:], compare_op=ALU.not_equal, fill=1.0,
            base=0, pattern=[[-1, 128]], channel_multiplier=1), Sgp, 1)
        P0 = abf(DQ_O, 4096)
        EM("gpsimd", lambda e: e.memset(P0, 1.0))
        EM("gpsimd", lambda e: e.affine_select(
            out=P0, in_=P0, compare_op=ALU.is_ge, fill=0.0,
            base=0, pattern=[[1, 4096]], channel_multiplier=-32))
        v_p0 = EM("gpsimd", lambda e: e.affine_select(
            out=P0, in_=P0, compare_op=ALU.is_ge, fill=0.0,
            base=31, pattern=[[-1, 4096]], channel_multiplier=32), Sgp, 1)

        EM("gpsimd", lambda e: e.memset(sel_b[:], 0.0))
        v_sel = EM("gpsimd", lambda e: e.affine_select(
            out=sel_b[:].rearrange("g (j m) -> g j m", m=128),
            in_=sel_b[:].rearrange("g (j m) -> g j m", m=128),
            compare_op=ALU.not_equal, fill=1.0, base=0,
            pattern=[[1, 16], [0, 128]], channel_multiplier=-1), Sgp, 1)
        EM("vector", lambda e: e.memset(ones128[:], 1.0))
        EM("vector", lambda e: e.memset(ones16[:], 1.0))
        EM("vector", lambda e: e.memset(ones1r[:], 1.0))
        EM("vector", lambda e: e.memset(ones16r[:], 1.0))
        EM("vector", lambda e: e.memset(carry[:], 0.0))
        EM("vector", lambda e: e.memset(epsc[0:1, 0:1], 1e-5))
        EM("vector", lambda e: e.memset(epsc[0:1, 1:2], 1e-6))
        EM("vector", lambda e: e.memset(epsc[0:1, 2:3], 1.0))
        EM("vector", lambda e: e.memset(one128[:], 1.0))
        v_pre_v = EM("vector", lambda e: e.memset(halo[:], 0.0), Sv, 1)
        WT("scalar", Sv, v_pre_v)

        for (dst, src) in ((convw_sb[:], convw), (convb_sb[:], convb),
                           (bdt_sb[:], bdt), (dvec_sb[:], dvec), (af32(1408, 128), alog)):
            EM("sync", lambda e, d=dst, s=src: e.dma_start(out=d, in_=s), Sld, 16)
        CERT("sync")
        v_consts = Sld.n
        WT("scalar", Slp, Slp.n)
        v_expA = EM("scalar", lambda e: e.activation(
            out=A_sb[:], in_=af32(1408, 128), func=AF.Exp), Ss, 1)
        WT("vector", Ss, v_expA)
        v_negA = EM("vector", lambda e: e.tensor_scalar(
            out=A_sb[:], in0=A_sb[:], scalar1=-1.0, scalar2=None, op0=ALU.mult), Sv, 1)
        WT("scalar", Sv, v_negA)

        v_cwx = EM("gpsimd", lambda e: e.dma_start(out=wxT_sb[:], in_=wxT), Sg, 16)
        v_cwdt = EM("gpsimd", lambda e: e.dma_start(out=wdtT_sb[:], in_=wdtT), Sg, 16)
        v_cbias = EM("gpsimd", lambda e: e.dma_start(out=biasc_sb[:], in_=biasc), Sg, 16)
        v_cwinx = EM("gpsimd", lambda e: e.dma_start(out=winx_bf.ap(), in_=winx), Sg, 16)
        v_cwing = EM("gpsimd", lambda e: e.dma_start(out=wing_bf.ap(), in_=wing), Sg, 16)
        v_cwout = EM("gpsimd", lambda e: e.dma_start(out=woutp_bf.ap(), in_=wout), Sg, 16)
        CERT("gpsimd")

        def _rank(e):
            hold["rank"] = e.cc_rank(replica_groups=RG)
        EM("sync", _rank)

        xi = [af32(XI_O, 512), af32(XI_O + 512, 512)]
        sqp = [af32(SQ_O, 512), af32(SQ_O + 512, 512)]
        hst = [abf(HST_O, 512), abf(HST_O + 512, 512)]
        v_hst_st = [0, 0]
        v_ln_tile = [0] * 32
        v_nm_last = 0
        for hf in range(2):
            col = hf * 512
            for kk in range(32):
                sl = kk % 2
                if hf > 0 or kk >= 2:
                    WT("sync", St_, v_ln_tile[kk - 2] if kk >= 2 else v_ln_tile[31])
                if hf == 1 and kk < 2:
                    WT("sync", Sv, v_nm_last)
                v_ld = EM("sync", lambda e, s=xi[sl], kk=kk, col=col: e.dma_start(
                    out=s, in_=x0T[kk * 128:(kk + 1) * 128, col:col + 512]), Sld, 16)
                CERT("sync")
                WT("scalar", Slp, Slp.n)
                if hf > 0 or kk >= 2:
                    WT("scalar", St_, v_ln_tile[kk - 2] if kk >= 2 else v_ln_tile[31])
                v_sq = EM("scalar", lambda e, s=xi[sl], q=sqp[sl]: e.activation(
                    out=q, in_=s, func=AF.Square), Ss, 1)
                WT("tensor", Slp, Slp.n)
                EM("tensor", lambda e, s=xi[sl], kk=kk: e.matmul(
                    ps_d[0:1, :], ones128[:], s, start=(kk == 0), stop=(kk == 31),
                    skip_group_check=True))
                WT("tensor", Ss, v_sq)
                v_ln_tile[kk] = EM("tensor", lambda e, q=sqp[sl], kk=kk: e.matmul(
                    ps_e[0:1, :], ones128[:], q, start=(kk == 0), stop=(kk == 31),
                    skip_group_check=True), St_, 1)
            WT("scalar", St_, v_ln_tile[31])
            EM("scalar", lambda e: e.activation(
                out=rowa[:], in_=ps_d[0:1, :], func=AF.Copy, scale=1.0 / H))
            v_mean = EM("scalar", lambda e: e.activation(
                out=rowd[:], in_=rowa[:], func=AF.Square), Ss, 1)
            WT("vector", Ss, v_mean)
            v_var = EM("vector", lambda e: e.scalar_tensor_tensor(
                out=rowe[:], in0=ps_e[0:1, :], scalar=1.0 / H,
                in1=rowd[:], op0=ALU.mult, op1=ALU.subtract), Sv, 1)
            WT("scalar", Sv, v_var)
            EM("scalar", lambda e: e.activation(
                out=rowd[:], in_=rowe[:], func=AF.Ln, bias=epsc[0:1, 0:1]))
            v_rstd = EM("scalar", lambda e: e.activation(
                out=rowb[:], in_=rowd[:], func=AF.Exp, scale=-0.5), Ss, 1)
            WT("tensor", Ss, v_rstd)
            EM("tensor", lambda e: e.matmul(
                ps_c[:], ones1r[:], rowa[:], start=True, stop=True))
            v_bc = EM("tensor", lambda e: e.matmul(
                ps_d[:], ones1r[:], rowb[:], start=True, stop=True), St_, 1)
            WT("sync", St_, v_ln_tile[31])
            for kk in range(32):
                sl = kk % 2
                WT("sync", Sv, Sv.n)
                v_ld = EM("sync", lambda e, s=xi[sl], kk=kk, col=col: e.dma_start(
                    out=s, in_=x0T[kk * 128:(kk + 1) * 128, col:col + 512]), Sld, 16)
                CERT("sync")
                WT("vector", Slp, Slp.n)
                if kk == 0:
                    WT("vector", St_, v_bc)
                WT("vector", Sgp, Sgp.n)
                EM("vector", lambda e, s=xi[sl], q=sqp[sl]: e.tensor_tensor(
                    out=q, in0=s, in1=ps_c[:], op=ALU.subtract))
                v_nm = EM("vector", lambda e, q=sqp[sl], h=hst[sl]: e.tensor_tensor(
                    out=h, in0=q, in1=ps_d[:], op=ALU.mult), Sv, 1)
                WT("gpsimd", Sv, v_nm)
                v_hst_st[sl] = EM("gpsimd", lambda e, h=hst[sl], kk=kk, col=col: e.dma_start(
                    out=hsln_part.ap()[kk * 128:(kk + 1) * 128, col:col + 512], in_=h), Sg, 16)
                CERT("gpsimd")
            v_nm_last = Sv.n
        WT("gpsimd", Sg, Sg.n)
        v_ag_hsln = EM("gpsimd", lambda e: e.collective_compute(
            "AllGather", ALU.bypass, replica_groups=RG,
            ins=[hsln_part.ap()], outs=[hsln_all.ap()]), Scc, 1)
        WT("gpsimd", Scc, Scc.n)
        WT("gpsimd", Sg, Sg.n)
        v_ag_wout = EM("gpsimd", lambda e: e.collective_compute(
            "AllGather", ALU.bypass, replica_groups=RG,
            ins=[woutp_bf.ap()], outs=[wout_all.ap()]), Scc, 1)
        WT("gpsimd", Scc, Scc.n)

        wnat = [abf(WNAT_O + j * 128, 128) for j in range(4)]
        wttile = [abf(WNAT_O + 512 + j * 128, 128) for j in range(4)]
        q3f = af32(0, 512)
        q3i = af[0:128, 512:1024].bitcast(I32)
        rngt = af[0:128, 1024:1152].bitcast(F16)
        scl = af32(1152, 128)
        mnn = af32(1280, 128)
        WT("sync", Sv, v_nm_last)
        WT("tensor", Sv, v_nm_last)
        v_wq_st = 0
        v_q3_rd = 0
        for ot in range(4):
            v_rng = EM("sync", lambda e, ot=ot: e.dma_start(
                out=rngt, in_=rngc[ot * 128:(ot + 1) * 128, :]), Sld, 16)
            CERT("sync")
            WT("vector", Slp, Slp.n)
            rv = rngt.rearrange("p (g two) -> p g two", two=2)
            EM("vector", lambda e, rv=rv: e.tensor_tensor(
                out=scl, in0=rv[:, :, 1], in1=rv[:, :, 0], op=ALU.subtract))
            v_sc = EM("vector", lambda e, rv=rv: e.tensor_copy(
                out=mnn, in_=rv[:, :, 0]), Sv, 1)
            WT("tensor", Sv, v_sc)
            if ot == 0:
                WT("tensor", Sgp, Sgp.n)
            v_t1 = EM("tensor", lambda e: e.transpose(
                ps_e[0:128, 0:128], scl, id_f32[:]), St_, 1)
            WT("vector", St_, v_t1)
            v_e1 = EM("vector", lambda e: e.tensor_copy(
                out=sclT[:], in_=ps_e[0:128, 0:128]), Sv, 1)
            WT("tensor", Sv, v_e1)
            v_t2 = EM("tensor", lambda e: e.transpose(
                ps_e[0:128, 0:128], mnn, id_f32[:]), St_, 1)
            WT("vector", St_, v_t2)
            v_e2 = EM("vector", lambda e: e.tensor_copy(
                out=mnnT[:], in_=ps_e[0:128, 0:128]), Sv, 1)
            for istr in range(8):
                WT("tensor", Sv, v_e2)
                if ot == 0 and istr == 0:
                    WT("tensor", Sgp, Sgp.n)
                WT("tensor", Sv, v_q3_rd)
                EM("tensor", lambda e, istr=istr: e.matmul(
                    ps_c[:], sclT[:], P0[:, istr * 512:(istr + 1) * 512],
                    start=True, stop=True))
                v_mm = EM("tensor", lambda e, istr=istr: e.matmul(
                    ps_d[:], mnnT[:], P0[:, istr * 512:(istr + 1) * 512],
                    start=True, stop=True), St_, 1)
                WT("sync", Sv, v_q3_rd)
                v_q3 = EM("sync", lambda e, ot=ot, istr=istr: e.dma_start(
                    out=q3i, in_=q3c[ot * 128:(ot + 1) * 128, istr * 512:(istr + 1) * 512]),
                    Sld, 16)
                CERT("sync")
                WT("vector", Slp, Slp.n)
                EM("vector", lambda e: e.tensor_scalar(
                    out=q3f, in0=q3i, scalar1=1.0 / 7.0, scalar2=None, op0=ALU.mult))
                WT("vector", St_, v_mm)
                EM("vector", lambda e: e.tensor_tensor(
                    out=q3f, in0=q3f, in1=ps_c[:], op=ALU.mult))
                WT("vector", Sgp, Sgp.n)
                v_wn = EM("vector", lambda e: e.tensor_tensor(
                    out=ab[0:128, WNAT_O:WNAT_O + 512], in0=q3f, in1=ps_d[:],
                    op=ALU.add), Sv, 1)
                v_q3_rd = v_wn
                WT("tensor", Sv, v_wn)
                for j in range(4):
                    v_tt = EM("tensor", lambda e, j=j: e.transpose(
                        ps_f[0:128, 0:64].bitcast(BF16), wnat[j], id_bf[:]), St_, 1)
                    WT("scalar", St_, v_tt)
                    v_ev = EM("scalar", lambda e, j=j: e.activation(
                        out=wttile[j], in_=ps_f[0:128, 0:64].bitcast(BF16), func=AF.Copy), Ss, 1)
                    WT("tensor", Ss, v_ev)
                    WT("gpsimd", Ss, v_ev)
                    v_wq_st = EM("gpsimd", lambda e, ot=ot, istr=istr, j=j: e.dma_start(
                        out=wffn_part.ap().rearrange(
                            "p (o k f) -> p o k f", o=4, k=32)[:, ot, istr * 4 + j, :],
                        in_=wttile[j]), Sg, 16)
                    CERT("gpsimd")
        WT("gpsimd", Sg, Sg.n)
        v_ag_wffn = EM("gpsimd", lambda e: e.collective_compute(
            "AllGather", ALU.bypass, replica_groups=RG,
            ins=[wffn_part.ap()], outs=[wffn_all.ap()]), Scc, 1)
        WT("gpsimd", Scc, Scc.n)
        v_deq_done = Sg.n

        WT("tensor", Sgp, Sgp.n)
        for psw in (ps_m0, ps_m1):
            for kk in range(4):
                EM("tensor", lambda e, psw=psw, kk=kk: e.matmul(
                    psw[0:128, 0:128], id_bf[:], id_bf[:],
                    start=(kk == 0), stop=(kk == 3)))
        hsv = hsln_all.ap().rearrange("(blk h) t -> blk h t", h=H)
        winx_v = winx_bf.ap().rearrange("(m p) f -> m p f", p=128)
        wing_v = wing_bf.ap().rearrange("(m p) f -> m p f", p=128)
        rhs_t = [abf(RHS_O + kk * 512, 512) for kk in range(32)]
        wm = [abf(WM_O, 4096), abf(WM_O + 4096, 4096)]
        xbf_pp = [[abf(XBF_O + d * 512, 512) for d in range(NDT)],
                  [abf(40960 + d * 512, 512) for d in range(NDT)]]
        gbf_pp = [[abf(GBF_O + d * 512, 512) for d in range(NDT)],
                  [abf(45056 + d * 512, 512) for d in range(NDT)]]
        tsn = [abf(TSN_O, 512), abf(TSN_O + 512, 512)]
        yst = [abf(YST_O, 512), abf(YST_O + 512, 512)]
        bN = ab[0:16, BN_O:BN_O + 512]
        cN = ab[0:16, BN_O + 512:BN_O + 1024]
        dt_t = [af32(DT_O + d * 512, 512) for d in range(NDT)]
        dtu_t = [af32(DTU_O + d * 512, 512) for d in range(NDT)]
        ya_t = [af32(YA_O + d * 512, 512) for d in range(NDT)]
        tb_t = [af32(TB_O, 512), af32(TB_O + 512, 512)]
        dAb_t = [abf(DQ_O + j * 512, 512) for j in range(4)]
        tbb_t = [abf(DQ_O + 2048, 512), abf(DQ_O + 2560, 512)]
        tsb_t = [abf(DQ_O + 3072, 512), abf(DQ_O + 3584, 512)]
        tcb_t = [abf(HST_O, 512), abf(HST_O + 512, 512)]
        dub_t = [af[0:128, DTU_O:DTU_O + 2048].bitcast(BF16)[:, d * 512:(d + 1) * 512]
                 for d in range(NDT)]
        bcb_t = [af[0:128, DA_O:DA_O + 1024].bitcast(BF16)[:, j * 512:(j + 1) * 512]
                 for j in range(4)]
        ts0 = af32(T0_O, 512)
        ts1 = af32(T0_O + 512, 512)
        accv = [af32(ACC_O, 512), af32(ACC_O + 512, 512)]
        xcat = [af32(XCAT_O, 515), af32(XCAT_O + 515, 515)]
        stg = [af32(STG_O, 512), af32(STG_O + 512, 512), af32(STG_O + 1024, 512, 32)]
        braw = af[0:16, BCR_O:BCR_O + 512]
        craw = af[0:16, BCR_O + 512:BCR_O + 1024]
        bsq = af[0:16, 5120:5632]
        csq = af[0:16, 5632:6144]

        last_rhs_use = [0] * 32
        v_wm_free = [0, 0]
        v_ssm_store = [0, 0, 0]
        v_dt = [0] * NDT
        v_yst = [0, 0]
        v_scan_use = {}
        v_bc_read = [0, 0]

        v_dtu_prev = 0
        v_evac_prev = [0, 0]
        v_ssm_ev = [0, 0, 0]
        v_ar_of = {}
        v_conv_prev = [0, 0]

        def emit_A(k):
            nonlocal v_evac_prev, v_conv_prev
            blk, half = k // 2, k % 2
            col = half * 512
            xbf = xbf_pp[k % 2]
            gbf = gbf_pp[k % 2]
            if k in (0, 8):
                WT("scalar", Sv, Sv.n)
                EM("scalar", lambda e: e.memzero(halo[:]))

            if k == 0:
                WT("sync", Scc, v_ag_hsln)
                WT("sync", Sgp, Sgp.n)
            v_rhs = [0] * 32
            for kk in range(32):
                WT("sync", St_, last_rhs_use[kk])
                v_rhs[kk] = EM("sync", lambda e, kk=kk, blk=blk, col=col: e.dma_start(
                    out=rhs_t[kk], in_=hsv[blk, kk * 128:(kk + 1) * 128, col:col + 512]),
                    Sld, 16)
            CERT("sync")

            if k == 0:
                WT("sync", Sgp, Sgp.n)
                WT("sync", Sgp, Sgp.n)
            v_mm_done = [0] * 16
            v_evac = [0] * 16
            v_silu = [0] * 16
            v_conv_done = [0] * 8
            for m in range(16):
                sl = m % 2
                wv = winx_v if m < 8 else wing_v
                mi = m if m < 8 else m - 8
                WT("sync", St_, v_wm_free[sl])
                v_wm = EM("sync", lambda e, wv=wv, mi=mi, sl=sl: e.dma_start(
                    out=wm[sl], in_=wv[mi]), Sld, 16)
                CERT("sync")
                psm = ps_m0 if sl == 0 else ps_m1
                WT("tensor", Slp, Slp.n)
                if m >= 2:
                    WT("tensor", Ss, v_evac[m - 2])
                else:
                    WT("tensor", Ss, v_evac_prev[m])
                for kk in range(32):
                    if m == 0:
                        WT("tensor", Slp, Slp.n)
                    last = (kk == 31)
                    v = EM("tensor", lambda e, psm=psm, sl=sl, kk=kk: e.matmul(
                        psm[:], ab[0:128, WM_O + sl * 4096 + kk * 128:WM_O + sl * 4096 + (kk + 1) * 128],
                        rhs_t[kk], start=(kk == 0), stop=(kk == 31)),
                        St_ if last else None, 1)
                    if last:
                        v_mm_done[m] = v
                v_wm_free[sl] = v_mm_done[m]
                if debug and k == 0 and m == 0:
                    WT("gpsimd", St_, v_mm_done[0])
                    for kk in range(32):
                        EM("gpsimd", lambda e, kk=kk: e.dma_start(
                            out=dbgr.ap()[kk * 128:(kk + 1) * 128, :], in_=rhs_t[kk]), Sg, 16)
                    EM("gpsimd", lambda e: e.dma_start(
                        out=dbgw.ap(), in_=wm[0]), Sg, 16)
                WT("scalar", St_, v_mm_done[m])
                if m < 8:
                    xc = xcat[m % 2]
                    if m >= 2:
                        WT("scalar", Sv, v_conv_done[m - 2])
                    else:
                        WT("scalar", Sv, max(v_conv_prev[m], v_dtu_prev if m == 0 else 0))
                    v_evac[m] = EM("scalar", lambda e, psm=psm, xc=xc: e.activation(
                        out=xc[:, 3:515], in_=psm[:], func=AF.Copy), Ss, 1)
                    if debug and k < 2:
                        WT("gpsimd", Ss, v_evac[m])
                        EM("gpsimd", lambda e, xc=xc, m=m, k=k: e.dma_start(
                            out=dbgp.ap()[k, m * 128:(m + 1) * 128, :], in_=xc[:, 3:515]),
                            Sg, 16)
                        CERT("gpsimd")
                        WT("scalar", Sgp, Sgp.n)
                    WT("vector", Ss, v_evac[m])
                    if m >= 2:
                        WT("vector", Ss, v_silu[m - 2])
                    EM("vector", lambda e, xc=xc, m=m: e.tensor_copy(
                        out=xc[:, 0:3], in_=halo[:, m * 3:(m + 1) * 3]))
                    av = accv[m % 2]
                    EM("vector", lambda e, xc=xc, av=av, m=m: e.tensor_scalar(
                        out=av, in0=xc[:, 3:515],
                        scalar1=convw_sb[:, m * 4 + 3:m * 4 + 4],
                        scalar2=convb_sb[:, m:m + 1], op0=ALU.mult, op1=ALU.add))
                    for j in range(3):
                        EM("vector", lambda e, xc=xc, av=av, m=m, j=j: e.scalar_tensor_tensor(
                            out=av, in0=xc[:, j:j + 512],
                            scalar=convw_sb[:, m * 4 + j:m * 4 + j + 1],
                            in1=av, op0=ALU.mult, op1=ALU.add))
                    v_conv_done[m] = EM("vector", lambda e, xc=xc, m=m: e.tensor_copy(
                        out=halo[:, m * 3:(m + 1) * 3], in_=xc[:, 512:515]), Sv, 1)
                    WT("scalar", Sv, v_conv_done[m])
                    v_silu[m] = EM("scalar", lambda e, av=av, m=m: e.activation(
                        out=xbf[m], in_=av, func=AF.Silu), Ss, 1)
                    v_evac[m] = v_silu[m]
                else:
                    dtl = m - 8
                    v_evac[m] = EM("scalar", lambda e, psm=psm, dtl=dtl: e.activation(
                        out=gbf[dtl], in_=psm[:], func=AF.Silu), Ss, 1)
                    v_silu[m] = v_evac[m]
            for kk in range(32):
                last_rhs_use[kk] = v_mm_done[15]
            v_evac_prev = [v_evac[14], v_evac[15]]
            v_conv_prev = [v_conv_done[6], v_conv_done[7]]
            v_x_ready = Ss.n
            if debug:
                WT("gpsimd", Ss, v_x_ready)
                for d in range(NDT):
                    EM("gpsimd", lambda e, d=d, k=k: e.dma_start(
                        out=dbgu.ap()[k, d * 128:(d + 1) * 128, :], in_=xbf[d]), Sg, 16)
                    EM("gpsimd", lambda e, d=d, k=k: e.dma_start(
                        out=dbgg.ap()[k, d * 128:(d + 1) * 128, :], in_=gbf[d]), Sg, 16)

            if k == 0:
                WT("tensor", Sgp, Sgp.n)
            psl = [ps_a, ps_b, ps_c]
            v_wx = [0, 0, 0]
            for ms in range(3):
                msz = 128 if ms < 2 else 32
                for kk in range(NDT):
                    if kk == 0:
                        WT("tensor", Ss, v_x_ready)
                        WT("tensor", Ss, v_ssm_ev[ms])
                    last = (kk == NDT - 1)
                    v = EM("tensor", lambda e, ms=ms, msz=msz, kk=kk: e.matmul(
                        psl[ms][0:msz, :],
                        wxT_sb[:, kk * 288 + ms * 128:kk * 288 + ms * 128 + msz],
                        xbf[kk], start=(kk == 0), stop=(kk == NDT - 1)),
                        St_ if last else None, 1)
                    if last:
                        v_wx[ms] = v
            for ms in range(3):
                msz = 128 if ms < 2 else 32
                WT("scalar", St_, v_wx[ms])
                WT("scalar", Sgp, Sgp.n)
                v_ssm_ev[ms] = EM("scalar", lambda e, ms=ms, msz=msz: e.activation(
                    out=stg[ms] if ms == 2 else stg[ms][0:msz, :],
                    in_=psl[ms][0:msz, :], func=AF.Copy), Ss, 1)
                WT("gpsimd", Ss, v_ssm_ev[ms])
                v_ssm_store[ms] = EM("gpsimd", lambda e, ms=ms, msz=msz, k=k: e.dma_start(
                    out=ssmpart.ap()[k, ms * 128:ms * 128 + msz, :],
                    in_=stg[ms] if ms == 2 else stg[ms][0:msz, :]), Sg, 16)
            CERT("gpsimd")
            WT("gpsimd", Sg, Sg.n)
            WT("gpsimd", Scc, Scc.n)
            v_ar_of[k] = EM("gpsimd", lambda e, k=k: e.collective_compute(
                "AllReduce", ALU.add, replica_groups=RG,
                ins=[ssmpart.ap()[k]], outs=[ssmred.ap()[k]]), Scc, 1)

        def emit_C(k):
            nonlocal v_dtu_prev
            blk, half = k // 2, k % 2
            col = half * 512
            xbf = xbf_pp[k % 2]
            gbf = gbf_pp[k % 2]
            if k in (0, 8):
                WT("scalar", Sv, Sv.n)
                EM("scalar", lambda e: e.memzero(carry[:]))
            WT("sync", Scc, v_ar_of[k])
            EM("sync", lambda e, k=k: e.dma_start(out=ts0, in_=ssmred.ap()[k, 0:128, :]), Sld, 16)
            v_l1 = EM("sync", lambda e, k=k: e.dma_start(out=ts1, in_=ssmred.ap()[k, 128:256, :]), Sld, 16)
            EM("sync", lambda e, k=k: e.dma_start(out=braw, in_=ssmred.ap()[k, 256:272, :]), Sld, 16)
            v_l3 = EM("sync", lambda e, k=k: e.dma_start(out=craw, in_=ssmred.ap()[k, 272:288, :]), Sld, 16)
            CERT("sync")

            WT("scalar", Slp, Slp.n)
            EM("scalar", lambda e: e.activation(out=sqp[0], in_=ts0, func=AF.Square))
            EM("scalar", lambda e: e.activation(out=sqp[1], in_=ts1, func=AF.Square))
            WT("scalar", Slp, Slp.n)
            EM("scalar", lambda e: e.activation(out=bsq, in_=braw, func=AF.Square))
            v_sqbc = EM("scalar", lambda e: e.activation(out=csq, in_=craw, func=AF.Square), Ss, 1)
            WT("tensor", Ss, v_sqbc)
            EM("tensor", lambda e: e.matmul(ps_d[0:1, :], ones128[:], sqp[0], start=True, stop=False))
            EM("tensor", lambda e: e.matmul(ps_d[0:1, :], ones128[:], sqp[1], start=False, stop=True))
            EM("tensor", lambda e: e.matmul(ps_e[0:1, :], ones16[:], bsq, start=True, stop=True))
            v_strms = EM("tensor", lambda e: e.matmul(ps_f[0:1, :], ones16[:], csq, start=True, stop=True), St_, 1)
            WT("scalar", St_, v_strms)
            EM("scalar", lambda e: e.activation(out=rowd[:], in_=ps_d[0:1, :], func=AF.Ln, scale=1.0 / RDT, bias=epsc[0:1, 1:2]))
            EM("scalar", lambda e: e.activation(out=rowa[:], in_=rowd[:], func=AF.Exp, scale=-0.5))
            EM("scalar", lambda e: e.activation(out=rowd[:], in_=ps_e[0:1, :], func=AF.Ln, scale=1.0 / NST, bias=epsc[0:1, 1:2]))
            EM("scalar", lambda e: e.activation(out=rowb[:], in_=rowd[:], func=AF.Exp, scale=-0.5))
            EM("scalar", lambda e: e.activation(out=rowd[:], in_=ps_f[0:1, :], func=AF.Ln, scale=1.0 / NST, bias=epsc[0:1, 1:2]))
            v_rst = EM("scalar", lambda e: e.activation(out=rowc[:], in_=rowd[:], func=AF.Exp, scale=-0.5), Ss, 1)
            WT("tensor", Ss, v_rst)
            v_tb = EM("tensor", lambda e: e.matmul(
                ps_d[:], ones1r[:], rowa[:], start=True, stop=True), St_, 1)
            WT("vector", St_, v_tb)
            EM("vector", lambda e: e.tensor_tensor(out=tsn[0], in0=ts0, in1=ps_d[:], op=ALU.mult))
            v_tsn = EM("vector", lambda e: e.tensor_tensor(out=tsn[1], in0=ts1, in1=ps_d[:], op=ALU.mult), Sv, 1)
            EM("tensor", lambda e: e.matmul(
                ps_e[0:16, :], ones16r[:], rowb[:], start=True, stop=True))
            v_cbm = EM("tensor", lambda e: e.matmul(
                ps_f[0:16, :], ones16r[:], rowc[:], start=True, stop=True), St_, 1)
            WT("vector", St_, v_cbm)
            EM("vector", lambda e: e.tensor_tensor(out=bN, in0=braw, in1=ps_e[0:16, :], op=ALU.mult))
            v_bcn = EM("vector", lambda e: e.tensor_tensor(out=cN, in0=craw, in1=ps_f[0:16, :], op=ALU.mult), Sv, 1)

            if k == 0:
                WT("tensor", Sgp, Sgp.n)
            WT("scalar", Sgp, Sgp.n)
            v_dtadd = [0] * NDT
            for d in range(NDT):
                psd = ps_a if (d % 2 == 0) else ps_b
                if d == 0:
                    WT("tensor", Sv, v_tsn)
                if d >= 2:
                    WT("tensor", Ss, v_dt[d - 2])
                for rr in range(2):
                    last = (rr == 1)
                    v = EM("tensor", lambda e, psd=psd, d=d, rr=rr: e.matmul(
                        psd[:], wdtT_sb[:, rr * 1024 + d * 128:rr * 1024 + (d + 1) * 128],
                        tsn[rr], start=(rr == 0), stop=(rr == 1)),
                        St_ if last else None, 1)
                WT("scalar", St_, v)
                if d >= 2:
                    WT("scalar", Sv, v_dtadd[d - 2])
                else:
                    WT("scalar", Sv, v_dtu_prev)
                sa = af32(SAB_O + (d % 2) * 1024, 512)
                sb = af32(SAB_O + (d % 2) * 1024 + 512, 512)
                EM("scalar", lambda e, psd=psd, d=d, sa=sa: e.activation(
                    out=sa, in_=psd[:], func=AF.Abs, bias=bdt_sb[:, d:d + 1]))
                EM("scalar", lambda e, sa=sa, sb=sb: e.activation(
                    out=sb, in_=sa, func=AF.Exp, scale=-1.0))
                EM("scalar", lambda e, sa=sa, sb=sb: e.activation(
                    out=sa, in_=sb, func=AF.Ln, bias=one128[:]))
                v_dt[d] = EM("scalar", lambda e, psd=psd, d=d, sb=sb: e.activation(
                    out=sb, in_=psd[:], func=AF.Relu, bias=bdt_sb[:, d:d + 1]), Ss, 1)
                WT("vector", Ss, v_dt[d])
                EM("vector", lambda e, d=d, sa=sa, sb=sb: e.tensor_tensor(
                    out=dt_t[d], in0=sa, in1=sb, op=ALU.add))
                v_dtadd[d] = EM("vector", lambda e, d=d, xb=xbf[d]: e.tensor_tensor(
                    out=dub_t[d], in0=dt_t[d], in1=xb, op=ALU.mult), Sv, 1)
                if d == NDT - 1:
                    v_dtu_prev = v_dtadd[d]

            v_bc_ev = [0, 0]
            for n in range(NST):
                par = n % 2
                psB = ps_c if par == 0 else ps_e
                psC = ps_d if par == 0 else ps_f
                bcB = bcb_t[2 * par]
                bcC = bcb_t[2 * par + 1]
                if n == 0:
                    WT("tensor", Sv, v_bcn)
                else:
                    WT("tensor", Sv, v_bc_read[par])
                EM("tensor", lambda e, n=n, psB=psB: e.matmul(
                    psB[:], sel_b[:, n * 128:(n + 1) * 128], bN, start=True, stop=True))
                v_bcast = EM("tensor", lambda e, n=n, psC=psC: e.matmul(
                    psC[:], sel_b[:, n * 128:(n + 1) * 128], cN, start=True, stop=True), St_, 1)
                WT("scalar", St_, v_bcast)
                if n >= 2:
                    WT("scalar", Sv, v_bc_read[par])
                else:
                    WT("scalar", Sv, v_bc_read[par])
                EM("scalar", lambda e, psB=psB, bcB=bcB: e.activation(
                    out=bcB, in_=psB[:], func=AF.Copy))
                v_bc_ev[par] = EM("scalar", lambda e, psC=psC, bcC=bcC: e.activation(
                    out=bcC, in_=psC[:], func=AF.Copy), Ss, 1)
                WT("vector", Ss, v_bc_ev[par])
                for d in range(NDT):
                    g = n * NDT + d
                    j = g % 4
                    pr = g % 2
                    if g >= 4:
                        WT("scalar", Sv, v_scan_use.get((k, g - 4), 0))
                    else:
                        WT("scalar", Sv, v_dtadd[NDT - 1])
                    v_da = EM("scalar", lambda e, d=d, n=n, j=j: e.activation(
                        out=dAb_t[j], in_=dt_t[d], func=AF.Exp,
                        scale=A_sb[:, d * 16 + n:d * 16 + n + 1]), Ss, 1)
                    EM("vector", lambda e, d=d, pr=pr, bcB=bcB: e.tensor_tensor(
                        out=tbb_t[pr], in0=dub_t[d], in1=bcB, op=ALU.mult))
                    WT("vector", Ss, v_da)
                    cidx = d * 16 + n
                    EM("vector", lambda e, j=j, pr=pr, cidx=cidx: e.tensor_tensor_scan(
                        out=tsb_t[pr], data0=dAb_t[j], data1=tbb_t[pr],
                        initial=carry[:, cidx:cidx + 1], op0=ALU.mult, op1=ALU.add))
                    v_cy = EM("vector", lambda e, pr=pr, bcC=bcC: e.tensor_tensor(
                        out=tcb_t[pr], in0=tsb_t[pr], in1=bcC, op=ALU.mult), Sv, 1)
                    v_scan_use[(k, g)] = v_cy
                    if n == 0:
                        EM("vector", lambda e, d=d, pr=pr: e.tensor_copy(
                            out=ya_t[d], in_=tcb_t[pr]))
                    else:
                        EM("vector", lambda e, d=d, pr=pr: e.tensor_tensor(
                            out=ya_t[d], in0=ya_t[d], in1=tcb_t[pr], op=ALU.add))
                    EM("vector", lambda e, pr=pr, cidx=cidx: e.tensor_copy(
                        out=carry[:, cidx:cidx + 1], in_=tsb_t[pr][:, 511:512]))
                    if d == NDT - 1:
                        v_bc_read[par] = v_cy

            for d in range(NDT):
                pr = d % 2
                EM("vector", lambda e, d=d, pr=pr: e.tensor_scalar(
                    out=tb_t[pr], in0=xbf[d], scalar1=dvec_sb[:, d:d + 1],
                    scalar2=None, op0=ALU.mult))
                EM("vector", lambda e, d=d, pr=pr: e.tensor_tensor(
                    out=ya_t[d], in0=ya_t[d], in1=tb_t[pr], op=ALU.add))
                WT("vector", Sgp, Sgp.n)
                v_y3 = EM("vector", lambda e, d=d, pr=pr: e.tensor_tensor(
                    out=yst[pr], in0=ya_t[d], in1=gbf[d], op=ALU.mult), Sv, 1)
                WT("gpsimd", Sv, v_y3)
                ybf_t = ybf_e if k % 2 == 0 else ybf_o
                v_yst[pr] = EM("gpsimd", lambda e, d=d, k=k, pr=pr, yb=ybf_t: e.dma_start(
                    out=yb.ap()[k // 2, d * 128:(d + 1) * 128, :], in_=yst[pr]), Sg, 16)
                CERT("gpsimd")
        emit_A(0)
        for k in range(1, NCHUNK):
            emit_A(k)
            emit_C(k - 1)
            if k - 1 == 14:
                WT("gpsimd", Sg, Sg.n)
                WT("gpsimd", Scc, Scc.n)
                v_agy_e = EM("gpsimd", lambda e: e.collective_compute(
                    "AllGather", ALU.bypass, replica_groups=RG,
                    ins=[ybf_e.ap()], outs=[yag_e.ap()]), Scc, 1)
        emit_C(NCHUNK - 1)
        WT("gpsimd", Sg, Sg.n)
        WT("gpsimd", Scc, Scc.n)
        v_agy_o = EM("gpsimd", lambda e: e.collective_compute(
            "AllGather", ALU.bypass, replica_groups=RG,
            ins=[ybf_o.ap()], outs=[yag_o.ap()]), Scc, 1)

        y_t = [abf(Y_O + kk * 512, 512) for kk in range(64)]
        wo_sl = [abf(WOUT_O, 4096), abf(WOUT_O + 4096, 4096)]
        xln_t = [abf(XLN_O + kk * 512, 512) for kk in range(32)]
        wf_sl = [abf(WFF_O, 4096), abf(WFF_O + 4096, 4096)]
        x1p = [af32(X1_O, 512), af32(X1_O + 512, 512)]
        scr = af32(SCR_O, 512)
        x0p = [af32(XI_O, 512), af32(XI_O + 512, 512)]
        otp = [af32(ACC_O, 512), af32(ACC_O + 512, 512)]
        yagv_e = yag_e.ap().rearrange("(r k) ch t -> r k ch t", k=8)
        yagv_o = yag_o.ap().rearrange("(r k) ch t -> r k ch t", k=8)
        wov = wout_all.ap().rearrange("(r p) (m k f) -> r p m k f", p=128, m=32, k=8)
        wfv = wffn_all.ap().rearrange("(r p) (o k f) -> r p o k f", p=128, o=4, k=32)


        v_wo_free = [0, 0]
        v_wf_free = [0, 0]
        v_ost = [0, 0]
        v_x1s_arr = [0] * 32
        v_sq_arr = [0] * 32
        for sub in range(2):
            col = sub * 512
            v_y = [0] * 8
            WT("sync", Scc, v_agy_e if sub == 0 else v_agy_o)
            if sub == 1:
                WT("sync", St_, St_.n)
            for g in range(8):
                yv = yagv_e if sub == 0 else yagv_o
                def _yld(e, g=g, yv=yv):
                    return e.dma_start(
                        out=ab[0:128, Y_O + g * 4096:Y_O + (g + 1) * 4096]
                        .rearrange("p (kk t) -> p kk t", t=512),
                        in_=yv[g, ds(hold["rank"], 1), :, :]
                        .squeeze(0).rearrange("(kk p) t -> p kk t", p=128))
                v_y[g] = EM("sync", _yld, Sld, 16)
            CERT("sync")
            if sub == 0:
                WT("sync", Sv, Sv.n)
                WT("sync", Ss, Ss.n)
                WT("sync", Sgp, Sgp.n)
            v_amm = [0] * 32
            v_ev = [0] * 32
            v_half_done = [0, 0]
            for m in range(32):
                sl = m % 2
                v_wo = [0, 0]
                for hf2 in range(2):
                    WT("sync", St_, v_half_done[hf2])
                    v_wo[hf2] = EM("sync", lambda e, m=m, hf2=hf2: e.dma_start(
                        out=ab[0:128, WOUT_O + hf2 * 4096:WOUT_O + (hf2 + 1) * 4096]
                        .rearrange("p (r q) -> p r q", q=1024),
                        in_=wov[4 * hf2:4 * (hf2 + 1), :, m].rearrange("r p k f -> p r (k f)")),
                        Sld, 16)
                if m >= 2:
                    WT("sync", Sv, v_ev[m - 2])
                v_x0 = EM("sync", lambda e, m=m, col=col, sl=sl: e.dma_start(
                    out=x0p[sl], in_=x0T[m * 128:(m + 1) * 128, col:col + 512]), Sld, 16)
                CERT("sync")
                psm = ps_m0 if sl == 0 else ps_m1
                if m >= 2:
                    WT("tensor", Sv, v_ev[m - 2])
                for kk in range(64):
                    if kk == 0:
                        WT("tensor", Slp, Slp.n)
                    if kk == 32:
                        WT("tensor", Slp, Slp.n)
                    if m == 0 and kk % 8 == 0:
                        WT("tensor", Slp, Slp.n)
                    last = (kk == 63)
                    v = EM("tensor", lambda e, psm=psm, kk=kk: e.matmul(
                        psm[:], ab[0:128, WOUT_O + (kk // 32) * 4096 + (kk % 32) * 128:WOUT_O + (kk // 32) * 4096 + (kk % 32) * 128 + 128],
                        y_t[kk], start=(kk == 0), stop=(kk == 63)),
                        St_ if last else None, 1)
                    if last:
                        v_amm[m] = v
                v_half_done[0] = v_amm[m]
                v_half_done[1] = v_amm[m]
                WT("vector", St_, v_amm[m])
                WT("vector", Slp, Slp.n)
                pr = m % 2
                if m >= 2:
                    WT("vector", Ss, v_sq_arr[m - 2])
                    WT("vector", Sgp, Sgp.n)
                v_x1 = EM("vector", lambda e, psm=psm, pr=pr: e.tensor_tensor(
                    out=x1p[pr], in0=x0p[pr], in1=psm[:], op=ALU.add), Sv, 1)
                v_ev[m] = v_x1
                WT("scalar", Sv, v_x1)
                v_sq_arr[m] = EM("scalar", lambda e, pr=pr: e.activation(
                    out=scr, in_=x1p[pr], func=AF.Square), Ss, 1)
                WT("tensor", Sv, v_x1)
                WT("tensor", Ss, v_sq_arr[m])
                EM("tensor", lambda e, pr=pr, m=m: e.matmul(
                    ps_d[0:1, :], ones128[:], x1p[pr], start=(m == 0), stop=(m == 31),
                    skip_group_check=True))
                v = EM("tensor", lambda e, m=m: e.matmul(
                    ps_e[0:1, :], ones128[:], scr, start=(m == 0), stop=(m == 31),
                    skip_group_check=True), St_, 1)
                if m == 31:
                    v_ln2stats = v
                WT("gpsimd", Sv, v_x1)
                v_x1s_arr[m] = EM("gpsimd", lambda e, m=m, sub=sub, pr=pr: e.dma_start(
                    out=x1d.ap()[sub, m * 128:(m + 1) * 128, :], in_=x1p[pr]), Sg, 16)
                CERT("gpsimd")

            WT("scalar", St_, v_ln2stats)
            EM("scalar", lambda e: e.activation(out=rowa[:], in_=ps_d[0:1, :], func=AF.Copy, scale=1.0 / H))
            v_m2 = EM("scalar", lambda e: e.activation(out=rowd[:], in_=rowa[:], func=AF.Square), Ss, 1)
            WT("vector", Ss, v_m2)
            v_v2 = EM("vector", lambda e: e.scalar_tensor_tensor(
                out=rowe[:], in0=ps_e[0:1, :], scalar=1.0 / H,
                in1=rowd[:], op0=ALU.mult, op1=ALU.subtract), Sv, 1)
            WT("scalar", Sv, v_v2)
            EM("scalar", lambda e: e.activation(out=rowd[:], in_=rowe[:], func=AF.Ln, bias=epsc[0:1, 0:1]))
            v_r2 = EM("scalar", lambda e: e.activation(out=rowb[:], in_=rowd[:], func=AF.Exp, scale=-0.5), Ss, 1)
            WT("tensor", Ss, v_r2)
            EM("tensor", lambda e: e.matmul(ps_c[:], ones1r[:], rowa[:], start=True, stop=True))
            v_bc2 = EM("tensor", lambda e: e.matmul(ps_d[:], ones1r[:], rowb[:], start=True, stop=True), St_, 1)

            v_xln = [0] * 32
            WT("sync", Sgp, Sgp.n)
            WT("sync", Ss, v_sq_arr[31])
            for kk in range(32):
                pr = kk % 2
                if kk >= 2:
                    WT("sync", Sv, v_xln[kk - 2])
                v_lx = EM("sync", lambda e, kk=kk, sub=sub, pr=pr: e.dma_start(
                    out=x1p[pr], in_=x1d.ap()[sub, kk * 128:(kk + 1) * 128, :]), Sld, 16)
                CERT("sync")
                WT("vector", Slp, Slp.n)
                if kk == 0:
                    WT("vector", St_, v_bc2)
                EM("vector", lambda e, pr=pr: e.tensor_tensor(
                    out=scr, in0=x1p[pr], in1=ps_c[:], op=ALU.subtract))
                v_xln[kk] = EM("vector", lambda e, kk=kk: e.tensor_tensor(
                    out=xln_t[kk], in0=scr, in1=ps_d[:], op=ALU.mult), Sv, 1)

            if sub == 0:
                WT("sync", Scc, v_ag_wffn)
                WT("sync", Sgp, Sgp.n)
            v_fmm = [0] * 32
            v_fev = [0] * 32
            for m in range(32):
                sl = m % 2
                if m == 0:
                    WT("sync", St_, v_amm[31])
                WT("sync", St_, v_wf_free[sl])
                v_wf = EM("sync", lambda e, m=m, sl=sl: e.dma_start(
                    out=wf_sl[sl],
                    in_=wfv[m // 4, :, m % 4].rearrange("p k f -> p (k f)")), Sld, 16)
                if m >= 2:
                    WT("sync", Sv, v_fev[m - 2])
                else:
                    WT("sync", Sv, v_xln[31])
                v_x1r = EM("sync", lambda e, m=m, sub=sub, sl=sl: e.dma_start(
                    out=x1p[sl], in_=x1d.ap()[sub, m * 128:(m + 1) * 128, :]), Sld, 16)
                CERT("sync")
                psm = ps_m0 if sl == 0 else ps_m1
                WT("tensor", Slp, Slp.n)
                if m >= 2:
                    WT("tensor", Sv, v_fev[m - 2])
                for kk in range(32):
                    if m == 0:
                        WT("tensor", Sv, v_xln[kk])
                    last = (kk == 31)
                    v = EM("tensor", lambda e, psm=psm, sl=sl, kk=kk: e.matmul(
                        psm[:], ab[0:128, WFF_O + sl * 4096 + kk * 128:WFF_O + sl * 4096 + (kk + 1) * 128],
                        xln_t[kk], start=(kk == 0), stop=(kk == 31)),
                        St_ if last else None, 1)
                    if last:
                        v_fmm[m] = v
                v_wf_free[sl] = v_fmm[m]
                WT("vector", St_, v_fmm[m])
                WT("vector", Slp, Slp.n)
                pr = m % 2
                WT("vector", Sgp, Sgp.n)
                v_x2 = EM("vector", lambda e, psm=psm, pr=pr, m=m: e.scalar_tensor_tensor(
                    out=otp[pr], in0=psm[:], scalar=biasc_sb[:, m:m + 1],
                    in1=x1p[pr], op0=ALU.add, op1=ALU.add), Sv, 1)
                v_fev[m] = v_x2
                WT("gpsimd", Sv, v_x2)
                v_ost[pr] = EM("gpsimd", lambda e, m=m, col=col, pr=pr: e.dma_start(
                    out=oT[m * 128:(m + 1) * 128, col:col + 512], in_=otp[pr]), Sg, 16)
                CERT("gpsimd")

        if debug:
            dbg = {
                "dbg_hsln": (hsln_all, [8 * H, 1024], BF16),
                "dbg_ssm": (ssmred, [NCHUNK, 288, CT], F32),
                "dbg_ssmp": (ssmpart, [NCHUNK, 288, CT], F32),
                "dbg_u": (dbgu, [NCHUNK, 1024, CT], BF16),
                "dbg_p": (dbgp, [2, 1024, CT], F32),
                "dbg_r": (dbgr, [4096, CT], BF16),
                "dbg_w": (dbgw, [128, 4096], BF16),
                "dbg_g": (dbgg, [NCHUNK, 1024, CT], BF16),
                "dbg_ye": (ybf_e, [8, 1024, CT], BF16),
                "dbg_yage": (yag_e, [64, 1024, CT], BF16),
                "dbg_yo": (ybf_o, [8, 1024, CT], BF16),
                "dbg_x1": (x1d, [2, H, CT], F32),
                "dbg_wffn": (wffn_all, [8 * 128, 16384], BF16),
                "dbg_wout": (wout_all, [8 * 128, 32768], BF16),
            }
            for name, (t, shp, dt_) in dbg.items():
                o = nc.dram_tensor(name, shp, dt_, kind="ExternalOutput").ap()
                EM("gpsimd", lambda e, o=o, t=t: e.dma_start(out=o, in_=t.ap()), Sg, 16)
        WT("gpsimd", Sg, Sg.n)

        block = es.enter_context(nc.Block())
        block.sync(lambda eng: [f(eng) for f in prog["sync"]])
        block.vector(lambda eng: [f(eng) for f in prog["vector"]])
        block.scalar(lambda eng: [f(eng) for f in prog["scalar"]])
        block.tensor(lambda eng: [f(eng) for f in prog["tensor"]])
        block.gpsimd(lambda eng: [f(eng) for f in prog["gpsimd"]])

    return nc


_NC_CACHE = {}


def _host_layout(inputs):
    x = np.asarray(inputs["L0_attn_hidden_states"])
    W_in = np.asarray(inputs["L0_attn_W_in"])
    conv_w = np.asarray(inputs["L0_attn_conv_w"])
    conv_b = np.asarray(inputs["L0_attn_conv_b"])
    W_x = np.asarray(inputs["L0_attn_W_x"])
    W_dt = np.asarray(inputs["L0_attn_W_dt"])
    b_dt = np.asarray(inputs["L0_attn_b_dt"])
    A_log = np.asarray(inputs["L0_attn_A_log"])
    Dv = np.asarray(inputs["L0_attn_D"])
    W_out = np.asarray(inputs["L0_attn_W_out"])
    q3 = np.asarray(inputs["L0_ffn_weight_q3"])
    rng = np.asarray(inputs["L0_ffn_weight_ranges"])
    bias = np.asarray(inputs["L0_ffn_bias"])

    xf = x.reshape(8192, 4096)
    q3_2d = q3.reshape(4096, 4096)
    in_maps = []
    for c in range(NCORE):
        cs, ce = c * 1024, (c + 1) * 1024
        d = {}
        d["x0T"] = np.ascontiguousarray(xf[cs:ce].T)
        d["winx"] = np.ascontiguousarray(
            W_in[cs:ce].reshape(8, 128, 32, 128).transpose(0, 3, 2, 1)).reshape(1024, 4096)
        d["wing"] = np.ascontiguousarray(
            W_in[DIN + cs:DIN + ce].reshape(8, 128, 32, 128).transpose(0, 3, 2, 1)).reshape(1024, 4096)
        d["wxT"] = np.ascontiguousarray(
            W_x[:, cs:ce].T.reshape(8, 128, 288).transpose(1, 0, 2)).reshape(128, 8 * 288)
        d["wdtT"] = np.ascontiguousarray(
            W_dt[cs:ce].T.reshape(2, 128, 1024).transpose(1, 0, 2)).reshape(128, 2048)
        d["wout"] = np.ascontiguousarray(
            W_out[:, cs:ce].reshape(32, 128, 8, 128).transpose(3, 0, 2, 1)).reshape(128, 32768)
        d["q3c"] = np.ascontiguousarray(q3_2d[c * 512:(c + 1) * 512])
        d["rngc"] = np.ascontiguousarray(rng[c * 65536:(c + 1) * 65536].reshape(512, 256))
        d["biasc"] = np.ascontiguousarray(bias.reshape(32, 128).T)
        d["alog"] = np.ascontiguousarray(
            A_log[cs:ce].reshape(8, 128, 16).transpose(1, 0, 2)).reshape(128, 128)
        d["convw"] = np.ascontiguousarray(
            conv_w[cs:ce].reshape(8, 128, 4).transpose(1, 0, 2)).reshape(128, 32)
        d["convb"] = np.ascontiguousarray(conv_b[cs:ce].reshape(8, 128).T)
        d["bdt"] = np.ascontiguousarray(b_dt[cs:ce].reshape(8, 128).T)
        d["dvec"] = np.ascontiguousarray(Dv[cs:ce].reshape(8, 128).T)
        in_maps.append(d)
    return in_maps


def kernel(**inputs):
    in_maps = _host_layout(inputs)
    if "nc" not in _NC_CACHE:
        _NC_CACHE["nc"] = build_program()
    nc = _NC_CACHE["nc"]
    res = run_bass_kernel_spmd(nc, in_maps, core_ids=list(range(NCORE)))
    out = np.empty((8192, 4096), dtype=np.float32)
    for c in range(NCORE):
        out[c * 1024:(c + 1) * 1024] = res.results[c]["oT"].T
    return out.reshape(2, 4096, 4096)


# version 17
# speedup vs baseline: 1.0166x; 1.0166x over previous
import numpy as np
import contextlib
import concourse.bass as bass
import concourse.mybir as mybir
from concourse.bass_utils import run_bass_kernel_spmd

F32 = mybir.dt.float32
BF16 = mybir.dt.bfloat16
F16 = mybir.dt.float16
I32 = mybir.dt.int32
AF = mybir.ActivationFunctionType
ALU = mybir.AluOpType
ds = bass.ds

H = 4096
DIN = 8192
NST = 16
RDT = 256
NCORE = 8
CT = 512
NCHUNK = 16
NDT = 8
LN_EPS = 1e-5
RMS_EPS = 1e-6
RG = [list(range(NCORE))]


def build_program(debug=False):
    nc = bass.Bass("TRN2", target_bir_lowering=False)

    x0T = nc.dram_tensor("x0T", [H, 1024], F32, kind="ExternalInput").ap()
    winx = nc.dram_tensor("winx", [1024, 4096], F32, kind="ExternalInput").ap()
    wing = nc.dram_tensor("wing", [1024, 4096], F32, kind="ExternalInput").ap()
    wxT = nc.dram_tensor("wxT", [128, 8 * 288], F32, kind="ExternalInput").ap()
    wdtT = nc.dram_tensor("wdtT", [128, 2048], F32, kind="ExternalInput").ap()
    wout = nc.dram_tensor("wout", [128, 32768], F32, kind="ExternalInput").ap()
    q3c = nc.dram_tensor("q3c", [512, 4096], I32, kind="ExternalInput").ap()
    rngc = nc.dram_tensor("rngc", [512, 256], F16, kind="ExternalInput").ap()
    biasc = nc.dram_tensor("biasc", [128, 32], F16, kind="ExternalInput").ap()
    alog = nc.dram_tensor("alog", [128, 128], F32, kind="ExternalInput").ap()
    convw = nc.dram_tensor("convw", [128, 32], F32, kind="ExternalInput").ap()
    convb = nc.dram_tensor("convb", [128, 8], F32, kind="ExternalInput").ap()
    bdt = nc.dram_tensor("bdt", [128, 8], F32, kind="ExternalInput").ap()
    dvec = nc.dram_tensor("dvec", [128, 8], F32, kind="ExternalInput").ap()
    oT = nc.dram_tensor("oT", [H, 1024], F32, kind="ExternalOutput").ap()

    winx_bf = nc.dram_tensor("winx_bf", [1024, 4096], BF16)
    wing_bf = nc.dram_tensor("wing_bf", [1024, 4096], BF16)
    woutp_bf = nc.dram_tensor("woutp_bf", [128, 32768], BF16)
    wout_all = nc.dram_tensor("wout_all", [8 * 128, 32768], BF16)
    hsln_part = nc.dram_tensor("hsln_part", [H, 1024], BF16)
    hsln_all = nc.dram_tensor("hsln_all", [8 * H, 1024], BF16)
    ssmpart = nc.dram_tensor("ssmpart", [NCHUNK, 288, CT], F32)
    ssmred = nc.dram_tensor("ssmred", [NCHUNK, 288, CT], F32)
    wffn_part = nc.dram_tensor("wffn_part", [128, 16384], BF16)
    wffn_all = nc.dram_tensor("wffn_all", [8 * 128, 16384], BF16)

    yag_e = nc.dram_tensor("yag_e", [8 * 8, 1024, CT], BF16)
    yag_o = nc.dram_tensor("yag_o", [8 * 8, 1024, CT], BF16)
    ybf_e = nc.dram_tensor("ybf_e", [8, 1024, CT], BF16)
    ybf_o = nc.dram_tensor("ybf_o", [8, 1024, CT], BF16)
    x1d = nc.dram_tensor("x1d", [2, H, CT], F32)
    dbgu = nc.dram_tensor("dbgu", [NCHUNK, 1024, CT], BF16)
    dbgg = nc.dram_tensor("dbgg", [NCHUNK, 1024, CT], BF16)
    dbgp = nc.dram_tensor("dbgp", [2, 1024, CT], F32)
    dbgr = nc.dram_tensor("dbgr", [4096, CT], BF16)
    dbgw = nc.dram_tensor("dbgw", [128, 4096], BF16)

    es = contextlib.ExitStack()
    with es:
        ab = es.enter_context(nc.sbuf_tensor("ab", [128, 49152], BF16))
        af = es.enter_context(nc.sbuf_tensor("af", [128, 20992], F32))
        id_bf = es.enter_context(nc.sbuf_tensor("id_bf", [128, 128], BF16))
        id_f32 = es.enter_context(nc.sbuf_tensor("id_f32", [128, 128], F32))
        sel_b = es.enter_context(nc.sbuf_tensor("sel_b", [16, 2048], BF16))
        rowa = es.enter_context(nc.sbuf_tensor("rowa", [1, 512], F32))
        rowb = es.enter_context(nc.sbuf_tensor("rowb", [1, 512], F32))
        rowc = es.enter_context(nc.sbuf_tensor("rowc", [1, 512], F32))
        rowd = es.enter_context(nc.sbuf_tensor("rowd", [1, 512], F32))
        rowe = es.enter_context(nc.sbuf_tensor("rowe", [1, 512], F32))
        ones1r = es.enter_context(nc.sbuf_tensor("ones1r", [1, 128], F32))
        ones16r = es.enter_context(nc.sbuf_tensor("ones16r", [1, 16], F32))
        ones128 = es.enter_context(nc.sbuf_tensor("ones128", [128, 1], F32))
        ones16 = es.enter_context(nc.sbuf_tensor("ones16", [16, 1], F32))
        wxT_sb = es.enter_context(nc.sbuf_tensor("wxT_sb", [128, 8 * 288], BF16))
        wdtT_sb = es.enter_context(nc.sbuf_tensor("wdtT_sb", [128, 2048], BF16))
        A_sb = es.enter_context(nc.sbuf_tensor("A_sb", [128, 128], F32))

        convw_sb = es.enter_context(nc.sbuf_tensor("convw_sb", [128, 32], F32))
        convb_sb = es.enter_context(nc.sbuf_tensor("convb_sb", [128, 8], F32))
        bdt_sb = es.enter_context(nc.sbuf_tensor("bdt_sb", [128, 8], F32))
        dvec_sb = es.enter_context(nc.sbuf_tensor("dvec_sb", [128, 8], F32))
        biasc_sb = es.enter_context(nc.sbuf_tensor("biasc_sb", [128, 32], F32))
        carry = es.enter_context(nc.sbuf_tensor("carry", [128, 128], F32))
        halo = es.enter_context(nc.sbuf_tensor("halo", [128, 24], F32))
        epsc = es.enter_context(nc.sbuf_tensor("epsc", [1, 4], F32))
        one128 = es.enter_context(nc.sbuf_tensor("one128", [128, 1], F32))
        sclT = es.enter_context(nc.sbuf_tensor("sclT", [128, 128], BF16))
        mnnT = es.enter_context(nc.sbuf_tensor("mnnT", [128, 128], BF16))

        ps_m0 = es.enter_context(nc.psum_tensor("ps_m0", [128, 512], F32))
        ps_m1 = es.enter_context(nc.psum_tensor("ps_m1", [128, 512], F32))
        ps_a = es.enter_context(nc.psum_tensor("ps_a", [128, 512], F32))
        ps_b = es.enter_context(nc.psum_tensor("ps_b", [128, 512], F32))
        ps_c = es.enter_context(nc.psum_tensor("ps_c", [128, 512], F32))
        ps_d = es.enter_context(nc.psum_tensor("ps_d", [128, 512], F32))
        ps_e = es.enter_context(nc.psum_tensor("ps_e", [128, 512], F32))
        ps_f = es.enter_context(nc.psum_tensor("ps_f", [128, 512], F32))

        sem_ld = es.enter_context(nc.semaphore("sem_ld"))
        sem_g = es.enter_context(nc.semaphore("sem_g"))
        sem_cc = es.enter_context(nc.semaphore("sem_cc"))
        sem_t = es.enter_context(nc.semaphore("sem_t"))
        sem_v = es.enter_context(nc.semaphore("sem_v"))
        sem_s = es.enter_context(nc.semaphore("sem_s"))
        sem_lp = es.enter_context(nc.semaphore("sem_lp"))
        sem_gp = es.enter_context(nc.semaphore("sem_gp"))

        RHS_O = 0
        WM_O = 16384
        XBF_O = 24576
        GBF_O = 28672
        TSN_O = 32768
        YST_O = 33792
        HST_O = 34816
        BN_O = 35840
        DQ_O = 36864
        Y_O = 0
        XLN_O = 0
        WOUT_O = 32768
        WFF_O = 32768
        WNAT_O = 16384

        DT_O = 0
        DA_O = 4096
        DTU_O = 6144
        SAB_O = 8192
        YA_O = 10240
        TB_O = 14336
        STG_O = 15360
        SQ_O = 16896
        T0_O = 17920
        ACC_O = 18944
        BCR_O = 19968
        XCAT_O = 0
        XI_O = 0
        X1_O = 14336
        SCR_O = 15360

        def abf(off, n, p=128):
            return ab[0:p, off:off + n]

        def af32(off, n, p=128):
            return af[0:p, off:off + n]

        prog = {k: [] for k in ("sync", "vector", "scalar", "tensor", "gpsimd")}

        class S:
            __slots__ = ("h", "n")

            def __init__(self, h):
                self.h = h
                self.n = 0

        Sld, Sg, Scc, St_, Sv, Ss = (S(sem_ld), S(sem_g), S(sem_cc),
                                     S(sem_t), S(sem_v), S(sem_s))
        Slp, Sgp = S(sem_lp), S(sem_gp)

        def EM(e, fn, sem=None, k=1):
            if sem is None:
                prog[e].append(fn)
                return None
            sem.n += k
            prog[e].append(lambda eng, fn=fn, s=sem, k=k: fn(eng).then_inc(s.h, k))
            return sem.n

        def WT(e, sem, v):
            if v > 0:
                prog[e].append(lambda eng, s=sem, v=v: eng.wait_ge(s.h, v))

        def CERT(queue):
            qs, ps = (Sld, Slp) if queue == "sync" else (Sg, Sgp)
            n = qs.n
            ps.n += 1
            prog[queue].append(
                lambda eng, n=n, q=qs, p=ps: eng.wait_ge(q.h, n).then_inc(p.h, 1))
            return ps.n

        hold = {}

        EM("gpsimd", lambda e: e.memset(id_bf[:], 0.0))
        EM("gpsimd", lambda e: e.affine_select(
            out=id_bf[:], in_=id_bf[:], compare_op=ALU.not_equal, fill=1.0,
            base=0, pattern=[[-1, 128]], channel_multiplier=1))
        EM("gpsimd", lambda e: e.memset(id_f32[:], 0.0))
        v_ident = EM("gpsimd", lambda e: e.affine_select(
            out=id_f32[:], in_=id_f32[:], compare_op=ALU.not_equal, fill=1.0,
            base=0, pattern=[[-1, 128]], channel_multiplier=1), Sgp, 1)
        P0 = abf(DQ_O, 4096)
        EM("gpsimd", lambda e: e.memset(P0, 1.0))
        EM("gpsimd", lambda e: e.affine_select(
            out=P0, in_=P0, compare_op=ALU.is_ge, fill=0.0,
            base=0, pattern=[[1, 4096]], channel_multiplier=-32))
        v_p0 = EM("gpsimd", lambda e: e.affine_select(
            out=P0, in_=P0, compare_op=ALU.is_ge, fill=0.0,
            base=31, pattern=[[-1, 4096]], channel_multiplier=32), Sgp, 1)

        EM("gpsimd", lambda e: e.memset(sel_b[:], 0.0))
        v_sel = EM("gpsimd", lambda e: e.affine_select(
            out=sel_b[:].rearrange("g (j m) -> g j m", m=128),
            in_=sel_b[:].rearrange("g (j m) -> g j m", m=128),
            compare_op=ALU.not_equal, fill=1.0, base=0,
            pattern=[[1, 16], [0, 128]], channel_multiplier=-1), Sgp, 1)
        EM("vector", lambda e: e.memset(ones128[:], 1.0))
        EM("vector", lambda e: e.memset(ones16[:], 1.0))
        EM("vector", lambda e: e.memset(ones1r[:], 1.0))
        EM("vector", lambda e: e.memset(ones16r[:], 1.0))
        EM("vector", lambda e: e.memset(carry[:], 0.0))
        EM("vector", lambda e: e.memset(epsc[0:1, 0:1], 1e-5))
        EM("vector", lambda e: e.memset(epsc[0:1, 1:2], 1e-6))
        EM("vector", lambda e: e.memset(epsc[0:1, 2:3], 1.0))
        EM("vector", lambda e: e.memset(one128[:], 1.0))
        v_pre_v = EM("vector", lambda e: e.memset(halo[:], 0.0), Sv, 1)
        WT("scalar", Sv, v_pre_v)

        for (dst, src) in ((convw_sb[:], convw), (convb_sb[:], convb),
                           (bdt_sb[:], bdt), (dvec_sb[:], dvec), (af32(1408, 128), alog)):
            EM("sync", lambda e, d=dst, s=src: e.dma_start(out=d, in_=s), Sld, 16)
        CERT("sync")
        v_consts = Sld.n
        WT("scalar", Slp, Slp.n)
        v_expA = EM("scalar", lambda e: e.activation(
            out=A_sb[:], in_=af32(1408, 128), func=AF.Exp), Ss, 1)
        WT("vector", Ss, v_expA)
        v_negA = EM("vector", lambda e: e.tensor_scalar(
            out=A_sb[:], in0=A_sb[:], scalar1=-1.0, scalar2=None, op0=ALU.mult), Sv, 1)
        WT("scalar", Sv, v_negA)

        v_cwx = EM("gpsimd", lambda e: e.dma_start(out=wxT_sb[:], in_=wxT), Sg, 16)
        v_cwdt = EM("gpsimd", lambda e: e.dma_start(out=wdtT_sb[:], in_=wdtT), Sg, 16)
        v_cbias = EM("gpsimd", lambda e: e.dma_start(out=biasc_sb[:], in_=biasc), Sg, 16)
        v_cwinx = EM("gpsimd", lambda e: e.dma_start(out=winx_bf.ap(), in_=winx), Sg, 16)
        v_cwing = EM("gpsimd", lambda e: e.dma_start(out=wing_bf.ap(), in_=wing), Sg, 16)
        v_cwout = EM("gpsimd", lambda e: e.dma_start(out=woutp_bf.ap(), in_=wout), Sg, 16)
        CERT("gpsimd")

        def _rank(e):
            hold["rank"] = e.cc_rank(replica_groups=RG)
        EM("sync", _rank)

        xi = [af32(XI_O, 512), af32(XI_O + 512, 512)]
        sqp = [af32(SQ_O, 512), af32(SQ_O + 512, 512)]
        hst = [abf(HST_O, 512), abf(HST_O + 512, 512)]
        v_hst_st = [0, 0]
        v_ln_tile = [0] * 32
        v_nm_last = 0
        for hf in range(2):
            col = hf * 512
            for kk in range(32):
                sl = kk % 2
                if hf > 0 or kk >= 2:
                    WT("sync", St_, v_ln_tile[kk - 2] if kk >= 2 else v_ln_tile[31])
                if hf == 1 and kk < 2:
                    WT("sync", Sv, v_nm_last)
                v_ld = EM("sync", lambda e, s=xi[sl], kk=kk, col=col: e.dma_start(
                    out=s, in_=x0T[kk * 128:(kk + 1) * 128, col:col + 512]), Sld, 16)
                CERT("sync")
                WT("scalar", Slp, Slp.n)
                if hf > 0 or kk >= 2:
                    WT("scalar", St_, v_ln_tile[kk - 2] if kk >= 2 else v_ln_tile[31])
                v_sq = EM("scalar", lambda e, s=xi[sl], q=sqp[sl]: e.activation(
                    out=q, in_=s, func=AF.Square), Ss, 1)
                WT("tensor", Slp, Slp.n)
                EM("tensor", lambda e, s=xi[sl], kk=kk: e.matmul(
                    ps_d[0:1, :], ones128[:], s, start=(kk == 0), stop=(kk == 31),
                    skip_group_check=True))
                WT("tensor", Ss, v_sq)
                v_ln_tile[kk] = EM("tensor", lambda e, q=sqp[sl], kk=kk: e.matmul(
                    ps_e[0:1, :], ones128[:], q, start=(kk == 0), stop=(kk == 31),
                    skip_group_check=True), St_, 1)
            WT("scalar", St_, v_ln_tile[31])
            EM("scalar", lambda e: e.activation(
                out=rowa[:], in_=ps_d[0:1, :], func=AF.Copy, scale=1.0 / H))
            v_mean = EM("scalar", lambda e: e.activation(
                out=rowd[:], in_=rowa[:], func=AF.Square), Ss, 1)
            WT("vector", Ss, v_mean)
            v_var = EM("vector", lambda e: e.scalar_tensor_tensor(
                out=rowe[:], in0=ps_e[0:1, :], scalar=1.0 / H,
                in1=rowd[:], op0=ALU.mult, op1=ALU.subtract), Sv, 1)
            WT("scalar", Sv, v_var)
            EM("scalar", lambda e: e.activation(
                out=rowd[:], in_=rowe[:], func=AF.Ln, bias=epsc[0:1, 0:1]))
            v_rstd = EM("scalar", lambda e: e.activation(
                out=rowb[:], in_=rowd[:], func=AF.Exp, scale=-0.5), Ss, 1)
            WT("tensor", Ss, v_rstd)
            EM("tensor", lambda e: e.matmul(
                ps_c[:], ones1r[:], rowa[:], start=True, stop=True))
            v_bc = EM("tensor", lambda e: e.matmul(
                ps_d[:], ones1r[:], rowb[:], start=True, stop=True), St_, 1)
            WT("sync", St_, v_ln_tile[31])
            for kk in range(32):
                sl = kk % 2
                WT("sync", Sv, Sv.n)
                v_ld = EM("sync", lambda e, s=xi[sl], kk=kk, col=col: e.dma_start(
                    out=s, in_=x0T[kk * 128:(kk + 1) * 128, col:col + 512]), Sld, 16)
                CERT("sync")
                WT("vector", Slp, Slp.n)
                if kk == 0:
                    WT("vector", St_, v_bc)
                WT("vector", Sgp, Sgp.n)
                EM("vector", lambda e, s=xi[sl], q=sqp[sl]: e.tensor_tensor(
                    out=q, in0=s, in1=ps_c[:], op=ALU.subtract))
                v_nm = EM("vector", lambda e, q=sqp[sl], h=hst[sl]: e.tensor_tensor(
                    out=h, in0=q, in1=ps_d[:], op=ALU.mult), Sv, 1)
                WT("gpsimd", Sv, v_nm)
                v_hst_st[sl] = EM("gpsimd", lambda e, h=hst[sl], kk=kk, col=col: e.dma_start(
                    out=hsln_part.ap()[kk * 128:(kk + 1) * 128, col:col + 512], in_=h), Sg, 16)
                CERT("gpsimd")
            v_nm_last = Sv.n
        WT("gpsimd", Sg, Sg.n)
        v_ag_hsln = EM("gpsimd", lambda e: e.collective_compute(
            "AllGather", ALU.bypass, replica_groups=RG,
            ins=[hsln_part.ap()], outs=[hsln_all.ap()]), Scc, 1)
        WT("gpsimd", Scc, Scc.n)
        WT("gpsimd", Sg, Sg.n)
        v_ag_wout = EM("gpsimd", lambda e: e.collective_compute(
            "AllGather", ALU.bypass, replica_groups=RG,
            ins=[woutp_bf.ap()], outs=[wout_all.ap()]), Scc, 1)
        WT("gpsimd", Scc, Scc.n)

        wnat = [abf(WNAT_O + j * 128, 128) for j in range(4)]
        wttile = [abf(WNAT_O + 512 + j * 128, 128) for j in range(4)]
        q3f = af32(0, 512)
        q3i = af[0:128, 512:1024].bitcast(I32)
        rngt = af[0:128, 1024:1152].bitcast(F16)
        scl = af32(1152, 128)
        mnn = af32(1280, 128)
        WT("sync", Sv, v_nm_last)
        WT("tensor", Sv, v_nm_last)
        v_wq_st = 0
        v_q3_rd = 0
        for ot in range(4):
            v_rng = EM("sync", lambda e, ot=ot: e.dma_start(
                out=rngt, in_=rngc[ot * 128:(ot + 1) * 128, :]), Sld, 16)
            CERT("sync")
            WT("vector", Slp, Slp.n)
            rv = rngt.rearrange("p (g two) -> p g two", two=2)
            EM("vector", lambda e, rv=rv: e.tensor_tensor(
                out=scl, in0=rv[:, :, 1], in1=rv[:, :, 0], op=ALU.subtract))
            v_sc = EM("vector", lambda e, rv=rv: e.tensor_copy(
                out=mnn, in_=rv[:, :, 0]), Sv, 1)
            WT("tensor", Sv, v_sc)
            if ot == 0:
                WT("tensor", Sgp, Sgp.n)
            v_t1 = EM("tensor", lambda e: e.transpose(
                ps_e[0:128, 0:128], scl, id_f32[:]), St_, 1)
            WT("vector", St_, v_t1)
            v_e1 = EM("vector", lambda e: e.tensor_copy(
                out=sclT[:], in_=ps_e[0:128, 0:128]), Sv, 1)
            WT("tensor", Sv, v_e1)
            v_t2 = EM("tensor", lambda e: e.transpose(
                ps_e[0:128, 0:128], mnn, id_f32[:]), St_, 1)
            WT("vector", St_, v_t2)
            v_e2 = EM("vector", lambda e: e.tensor_copy(
                out=mnnT[:], in_=ps_e[0:128, 0:128]), Sv, 1)
            for istr in range(8):
                WT("tensor", Sv, v_e2)
                if ot == 0 and istr == 0:
                    WT("tensor", Sgp, Sgp.n)
                WT("tensor", Sv, v_q3_rd)
                EM("tensor", lambda e, istr=istr: e.matmul(
                    ps_c[:], sclT[:], P0[:, istr * 512:(istr + 1) * 512],
                    start=True, stop=True))
                v_mm = EM("tensor", lambda e, istr=istr: e.matmul(
                    ps_d[:], mnnT[:], P0[:, istr * 512:(istr + 1) * 512],
                    start=True, stop=True), St_, 1)
                WT("sync", Sv, v_q3_rd)
                v_q3 = EM("sync", lambda e, ot=ot, istr=istr: e.dma_start(
                    out=q3i, in_=q3c[ot * 128:(ot + 1) * 128, istr * 512:(istr + 1) * 512]),
                    Sld, 16)
                CERT("sync")
                WT("vector", Slp, Slp.n)
                EM("vector", lambda e: e.tensor_scalar(
                    out=q3f, in0=q3i, scalar1=1.0 / 7.0, scalar2=None, op0=ALU.mult))
                WT("vector", St_, v_mm)
                EM("vector", lambda e: e.tensor_tensor(
                    out=q3f, in0=q3f, in1=ps_c[:], op=ALU.mult))
                WT("vector", Sgp, Sgp.n)
                v_wn = EM("vector", lambda e: e.tensor_tensor(
                    out=ab[0:128, WNAT_O:WNAT_O + 512], in0=q3f, in1=ps_d[:],
                    op=ALU.add), Sv, 1)
                v_q3_rd = v_wn
                WT("tensor", Sv, v_wn)
                for j in range(4):
                    v_tt = EM("tensor", lambda e, j=j: e.transpose(
                        ps_f[0:128, 0:64].bitcast(BF16), wnat[j], id_bf[:]), St_, 1)
                    WT("scalar", St_, v_tt)
                    v_ev = EM("scalar", lambda e, j=j: e.activation(
                        out=wttile[j], in_=ps_f[0:128, 0:64].bitcast(BF16), func=AF.Copy), Ss, 1)
                    WT("tensor", Ss, v_ev)
                    WT("gpsimd", Ss, v_ev)
                    v_wq_st = EM("gpsimd", lambda e, ot=ot, istr=istr, j=j: e.dma_start(
                        out=wffn_part.ap().rearrange(
                            "p (o k f) -> p o k f", o=4, k=32)[:, ot, istr * 4 + j, :],
                        in_=wttile[j]), Sg, 16)
                    CERT("gpsimd")
        WT("gpsimd", Sg, Sg.n)
        v_ag_wffn = EM("gpsimd", lambda e: e.collective_compute(
            "AllGather", ALU.bypass, replica_groups=RG,
            ins=[wffn_part.ap()], outs=[wffn_all.ap()]), Scc, 1)
        WT("gpsimd", Scc, Scc.n)
        v_deq_done = Sg.n

        WT("tensor", Sgp, Sgp.n)
        for psw in (ps_m0, ps_m1):
            for kk in range(4):
                EM("tensor", lambda e, psw=psw, kk=kk: e.matmul(
                    psw[0:128, 0:128], id_bf[:], id_bf[:],
                    start=(kk == 0), stop=(kk == 3)))
        hsv = hsln_all.ap().rearrange("(blk h) t -> blk h t", h=H)
        winx_v = winx_bf.ap().rearrange("(m p) f -> m p f", p=128)
        wing_v = wing_bf.ap().rearrange("(m p) f -> m p f", p=128)
        rhs_t = [abf(RHS_O + kk * 512, 512) for kk in range(32)]
        wm = [abf(WM_O, 4096), abf(WM_O + 4096, 4096)]
        xbf_pp = [[abf(XBF_O + d * 512, 512) for d in range(NDT)],
                  [abf(40960 + d * 512, 512) for d in range(NDT)]]
        gbf_pp = [[abf(GBF_O + d * 512, 512) for d in range(NDT)],
                  [abf(45056 + d * 512, 512) for d in range(NDT)]]
        tsn = [abf(TSN_O, 512), abf(TSN_O + 512, 512)]
        yst = [abf(YST_O, 512), abf(YST_O + 512, 512)]
        bN = ab[0:16, BN_O:BN_O + 512]
        cN = ab[0:16, BN_O + 512:BN_O + 1024]
        dt_t = [af32(DT_O + d * 512, 512) for d in range(NDT)]
        dtu_t = [af32(DTU_O + d * 512, 512) for d in range(NDT)]
        ya_t = [af32(YA_O + d * 512, 512) for d in range(NDT)]
        tb_t = [af32(TB_O, 512), af32(TB_O + 512, 512)]
        dAb_t = [abf(DQ_O + j * 512, 512) for j in range(4)]
        tbb_t = [abf(DQ_O + 2048, 512), abf(DQ_O + 2560, 512)]
        tsb_t = [abf(DQ_O + 3072, 512), abf(DQ_O + 3584, 512)]
        tcb_t = [abf(HST_O, 512), abf(HST_O + 512, 512)]
        dub_t = [af[0:128, DTU_O:DTU_O + 2048].bitcast(BF16)[:, d * 512:(d + 1) * 512]
                 for d in range(NDT)]
        bcb_t = [af[0:128, DA_O:DA_O + 1024].bitcast(BF16)[:, j * 512:(j + 1) * 512]
                 for j in range(4)]
        ts0 = af32(T0_O, 512)
        ts1 = af32(T0_O + 512, 512)
        accv = [af32(ACC_O, 512), af32(ACC_O + 512, 512)]
        xcat = [af32(XCAT_O, 515), af32(XCAT_O + 515, 515)]
        stg = [af32(STG_O, 512), af32(STG_O + 512, 512), af32(STG_O + 1024, 512, 32)]
        braw = af[0:16, BCR_O:BCR_O + 512]
        craw = af[0:16, BCR_O + 512:BCR_O + 1024]
        bsq = af[0:16, 5120:5632]
        csq = af[0:16, 5632:6144]

        last_rhs_use = [0] * 32
        v_wm_free = [0, 0]
        v_ssm_store = [0, 0, 0]
        v_dt = [0] * NDT
        v_yst = [0, 0]
        v_scan_use = {}
        v_bc_read = [0, 0]

        v_dtu_prev = 0
        v_evac_prev = [0, 0]
        v_ssm_ev = [0, 0, 0]
        v_ar_of = {}
        v_conv_prev = [0, 0]

        def emit_A(k):
            nonlocal v_evac_prev, v_conv_prev
            blk, half = k // 2, k % 2
            col = half * 512
            xbf = xbf_pp[k % 2]
            gbf = gbf_pp[k % 2]
            if k in (0, 8):
                WT("scalar", Sv, Sv.n)
                EM("scalar", lambda e: e.memzero(halo[:]))

            if k == 0:
                WT("sync", Scc, v_ag_hsln)
                WT("sync", Sgp, Sgp.n)
            v_rhs = [0] * 32
            for kk in range(32):
                WT("sync", St_, last_rhs_use[kk])
                v_rhs[kk] = EM("sync", lambda e, kk=kk, blk=blk, col=col: e.dma_start(
                    out=rhs_t[kk], in_=hsv[blk, kk * 128:(kk + 1) * 128, col:col + 512]),
                    Sld, 16)
            CERT("sync")

            if k == 0:
                WT("sync", Sgp, Sgp.n)
                WT("sync", Sgp, Sgp.n)
            v_mm_done = [0] * 16
            v_evac = [0] * 16
            v_silu = [0] * 16
            v_conv_done = [0] * 8
            for m in range(16):
                sl = m % 2
                wv = winx_v if m < 8 else wing_v
                mi = m if m < 8 else m - 8
                WT("sync", St_, v_wm_free[sl])
                v_wm = EM("sync", lambda e, wv=wv, mi=mi, sl=sl: e.dma_start(
                    out=wm[sl], in_=wv[mi]), Sld, 16)
                CERT("sync")
                psm = ps_m0 if sl == 0 else ps_m1
                WT("tensor", Slp, Slp.n)
                if m >= 2:
                    WT("tensor", Ss, v_evac[m - 2])
                else:
                    WT("tensor", Ss, v_evac_prev[m])
                for kk in range(32):
                    if m == 0:
                        WT("tensor", Slp, Slp.n)
                    last = (kk == 31)
                    v = EM("tensor", lambda e, psm=psm, sl=sl, kk=kk: e.matmul(
                        psm[:], ab[0:128, WM_O + sl * 4096 + kk * 128:WM_O + sl * 4096 + (kk + 1) * 128],
                        rhs_t[kk], start=(kk == 0), stop=(kk == 31)),
                        St_ if last else None, 1)
                    if last:
                        v_mm_done[m] = v
                v_wm_free[sl] = v_mm_done[m]
                if debug and k == 0 and m == 0:
                    WT("gpsimd", St_, v_mm_done[0])
                    for kk in range(32):
                        EM("gpsimd", lambda e, kk=kk: e.dma_start(
                            out=dbgr.ap()[kk * 128:(kk + 1) * 128, :], in_=rhs_t[kk]), Sg, 16)
                    EM("gpsimd", lambda e: e.dma_start(
                        out=dbgw.ap(), in_=wm[0]), Sg, 16)
                WT("scalar", St_, v_mm_done[m])
                if m < 8:
                    xc = xcat[m % 2]
                    if m >= 2:
                        WT("scalar", Sv, v_conv_done[m - 2])
                    else:
                        WT("scalar", Sv, max(v_conv_prev[m], v_dtu_prev if m == 0 else 0))
                    v_evac[m] = EM("scalar", lambda e, psm=psm, xc=xc: e.activation(
                        out=xc[:, 3:515], in_=psm[:], func=AF.Copy), Ss, 1)
                    if debug and k < 2:
                        WT("gpsimd", Ss, v_evac[m])
                        EM("gpsimd", lambda e, xc=xc, m=m, k=k: e.dma_start(
                            out=dbgp.ap()[k, m * 128:(m + 1) * 128, :], in_=xc[:, 3:515]),
                            Sg, 16)
                        CERT("gpsimd")
                        WT("scalar", Sgp, Sgp.n)
                    WT("vector", Ss, v_evac[m])
                    if m >= 2:
                        WT("vector", Ss, v_silu[m - 2])
                    EM("vector", lambda e, xc=xc, m=m: e.tensor_copy(
                        out=xc[:, 0:3], in_=halo[:, m * 3:(m + 1) * 3]))
                    av = accv[m % 2]
                    EM("vector", lambda e, xc=xc, av=av, m=m: e.tensor_scalar(
                        out=av, in0=xc[:, 3:515],
                        scalar1=convw_sb[:, m * 4 + 3:m * 4 + 4],
                        scalar2=convb_sb[:, m:m + 1], op0=ALU.mult, op1=ALU.add))
                    for j in range(3):
                        EM("vector", lambda e, xc=xc, av=av, m=m, j=j: e.scalar_tensor_tensor(
                            out=av, in0=xc[:, j:j + 512],
                            scalar=convw_sb[:, m * 4 + j:m * 4 + j + 1],
                            in1=av, op0=ALU.mult, op1=ALU.add))
                    v_conv_done[m] = EM("vector", lambda e, xc=xc, m=m: e.tensor_copy(
                        out=halo[:, m * 3:(m + 1) * 3], in_=xc[:, 512:515]), Sv, 1)
                    WT("scalar", Sv, v_conv_done[m])
                    v_silu[m] = EM("scalar", lambda e, av=av, m=m: e.activation(
                        out=xbf[m], in_=av, func=AF.Silu), Ss, 1)
                    v_evac[m] = v_silu[m]
                else:
                    dtl = m - 8
                    v_evac[m] = EM("scalar", lambda e, psm=psm, dtl=dtl: e.activation(
                        out=gbf[dtl], in_=psm[:], func=AF.Silu), Ss, 1)
                    v_silu[m] = v_evac[m]
            for kk in range(32):
                last_rhs_use[kk] = v_mm_done[15]
            v_evac_prev = [v_evac[14], v_evac[15]]
            v_conv_prev = [v_conv_done[6], v_conv_done[7]]
            v_x_ready = Ss.n
            if debug:
                WT("gpsimd", Ss, v_x_ready)
                for d in range(NDT):
                    EM("gpsimd", lambda e, d=d, k=k: e.dma_start(
                        out=dbgu.ap()[k, d * 128:(d + 1) * 128, :], in_=xbf[d]), Sg, 16)
                    EM("gpsimd", lambda e, d=d, k=k: e.dma_start(
                        out=dbgg.ap()[k, d * 128:(d + 1) * 128, :], in_=gbf[d]), Sg, 16)

            if k == 0:
                WT("tensor", Sgp, Sgp.n)
            psl = [ps_a, ps_b, ps_c]
            v_wx = [0, 0, 0]
            for ms in range(3):
                msz = 128 if ms < 2 else 32
                for kk in range(NDT):
                    if kk == 0:
                        WT("tensor", Ss, v_x_ready)
                        WT("tensor", Ss, v_ssm_ev[ms])
                    last = (kk == NDT - 1)
                    v = EM("tensor", lambda e, ms=ms, msz=msz, kk=kk: e.matmul(
                        psl[ms][0:msz, :],
                        wxT_sb[:, kk * 288 + ms * 128:kk * 288 + ms * 128 + msz],
                        xbf[kk], start=(kk == 0), stop=(kk == NDT - 1)),
                        St_ if last else None, 1)
                    if last:
                        v_wx[ms] = v
            for ms in range(3):
                msz = 128 if ms < 2 else 32
                WT("scalar", St_, v_wx[ms])
                WT("scalar", Sgp, Sgp.n)
                v_ssm_ev[ms] = EM("scalar", lambda e, ms=ms, msz=msz: e.activation(
                    out=stg[ms] if ms == 2 else stg[ms][0:msz, :],
                    in_=psl[ms][0:msz, :], func=AF.Copy), Ss, 1)
                WT("gpsimd", Ss, v_ssm_ev[ms])
                v_ssm_store[ms] = EM("gpsimd", lambda e, ms=ms, msz=msz, k=k: e.dma_start(
                    out=ssmpart.ap()[k, ms * 128:ms * 128 + msz, :],
                    in_=stg[ms] if ms == 2 else stg[ms][0:msz, :]), Sg, 16)
            CERT("gpsimd")
            WT("gpsimd", Sg, Sg.n)
            WT("gpsimd", Scc, Scc.n)
            v_ar_of[k] = EM("gpsimd", lambda e, k=k: e.collective_compute(
                "AllReduce", ALU.add, replica_groups=RG,
                ins=[ssmpart.ap()[k]], outs=[ssmred.ap()[k]]), Scc, 1)

        def emit_C(k):
            nonlocal v_dtu_prev
            blk, half = k // 2, k % 2
            col = half * 512
            xbf = xbf_pp[k % 2]
            gbf = gbf_pp[k % 2]
            if k in (0, 8):
                WT("scalar", Sv, Sv.n)
                EM("scalar", lambda e: e.memzero(carry[:]))
            WT("sync", Scc, v_ar_of[k])
            EM("sync", lambda e, k=k: e.dma_start(out=ts0, in_=ssmred.ap()[k, 0:128, :]), Sld, 16)
            v_l1 = EM("sync", lambda e, k=k: e.dma_start(out=ts1, in_=ssmred.ap()[k, 128:256, :]), Sld, 16)
            EM("sync", lambda e, k=k: e.dma_start(out=braw, in_=ssmred.ap()[k, 256:272, :]), Sld, 16)
            v_l3 = EM("sync", lambda e, k=k: e.dma_start(out=craw, in_=ssmred.ap()[k, 272:288, :]), Sld, 16)
            CERT("sync")

            WT("scalar", Slp, Slp.n)
            EM("scalar", lambda e: e.activation(out=sqp[0], in_=ts0, func=AF.Square))
            EM("scalar", lambda e: e.activation(out=sqp[1], in_=ts1, func=AF.Square))
            WT("scalar", Slp, Slp.n)
            EM("scalar", lambda e: e.activation(out=bsq, in_=braw, func=AF.Square))
            v_sqbc = EM("scalar", lambda e: e.activation(out=csq, in_=craw, func=AF.Square), Ss, 1)
            WT("tensor", Ss, v_sqbc)
            EM("tensor", lambda e: e.matmul(ps_d[0:1, :], ones128[:], sqp[0], start=True, stop=False))
            EM("tensor", lambda e: e.matmul(ps_d[0:1, :], ones128[:], sqp[1], start=False, stop=True))
            EM("tensor", lambda e: e.matmul(ps_e[0:1, :], ones16[:], bsq, start=True, stop=True))
            v_strms = EM("tensor", lambda e: e.matmul(ps_f[0:1, :], ones16[:], csq, start=True, stop=True), St_, 1)
            WT("scalar", St_, v_strms)
            EM("scalar", lambda e: e.activation(out=rowd[:], in_=ps_d[0:1, :], func=AF.Ln, scale=1.0 / RDT, bias=epsc[0:1, 1:2]))
            EM("scalar", lambda e: e.activation(out=rowa[:], in_=rowd[:], func=AF.Exp, scale=-0.5))
            EM("scalar", lambda e: e.activation(out=rowd[:], in_=ps_e[0:1, :], func=AF.Ln, scale=1.0 / NST, bias=epsc[0:1, 1:2]))
            EM("scalar", lambda e: e.activation(out=rowb[:], in_=rowd[:], func=AF.Exp, scale=-0.5))
            EM("scalar", lambda e: e.activation(out=rowd[:], in_=ps_f[0:1, :], func=AF.Ln, scale=1.0 / NST, bias=epsc[0:1, 1:2]))
            v_rst = EM("scalar", lambda e: e.activation(out=rowc[:], in_=rowd[:], func=AF.Exp, scale=-0.5), Ss, 1)
            WT("tensor", Ss, v_rst)
            v_tb = EM("tensor", lambda e: e.matmul(
                ps_d[:], ones1r[:], rowa[:], start=True, stop=True), St_, 1)
            WT("vector", St_, v_tb)
            EM("vector", lambda e: e.tensor_tensor(out=tsn[0], in0=ts0, in1=ps_d[:], op=ALU.mult))
            v_tsn = EM("vector", lambda e: e.tensor_tensor(out=tsn[1], in0=ts1, in1=ps_d[:], op=ALU.mult), Sv, 1)
            EM("tensor", lambda e: e.matmul(
                ps_e[0:16, :], ones16r[:], rowb[:], start=True, stop=True))
            v_cbm = EM("tensor", lambda e: e.matmul(
                ps_f[0:16, :], ones16r[:], rowc[:], start=True, stop=True), St_, 1)
            WT("vector", St_, v_cbm)
            EM("vector", lambda e: e.tensor_tensor(out=bN, in0=braw, in1=ps_e[0:16, :], op=ALU.mult))
            v_bcn = EM("vector", lambda e: e.tensor_tensor(out=cN, in0=craw, in1=ps_f[0:16, :], op=ALU.mult), Sv, 1)

            if k == 0:
                WT("tensor", Sgp, Sgp.n)
            WT("scalar", Sgp, Sgp.n)
            v_dtadd = [0] * NDT
            for d in range(NDT):
                psd = ps_a if (d % 2 == 0) else ps_b
                if d == 0:
                    WT("tensor", Sv, v_tsn)
                if d >= 2:
                    WT("tensor", Ss, v_dt[d - 2])
                for rr in range(2):
                    last = (rr == 1)
                    v = EM("tensor", lambda e, psd=psd, d=d, rr=rr: e.matmul(
                        psd[:], wdtT_sb[:, rr * 1024 + d * 128:rr * 1024 + (d + 1) * 128],
                        tsn[rr], start=(rr == 0), stop=(rr == 1)),
                        St_ if last else None, 1)
                WT("scalar", St_, v)
                if d >= 2:
                    WT("scalar", Sv, v_dtadd[d - 2])
                else:
                    WT("scalar", Sv, v_dtu_prev)
                sa = af32(SAB_O + (d % 2) * 1024, 512)
                sb = af32(SAB_O + (d % 2) * 1024 + 512, 512)
                EM("scalar", lambda e, psd=psd, d=d, sa=sa: e.activation(
                    out=sa, in_=psd[:], func=AF.Abs, bias=bdt_sb[:, d:d + 1]))
                EM("scalar", lambda e, sa=sa, sb=sb: e.activation(
                    out=sb, in_=sa, func=AF.Exp, scale=-1.0))
                EM("scalar", lambda e, sa=sa, sb=sb: e.activation(
                    out=sa, in_=sb, func=AF.Ln, bias=one128[:]))
                v_dt[d] = EM("scalar", lambda e, psd=psd, d=d, sb=sb: e.activation(
                    out=sb, in_=psd[:], func=AF.Relu, bias=bdt_sb[:, d:d + 1]), Ss, 1)
                WT("vector", Ss, v_dt[d])
                EM("vector", lambda e, d=d, sa=sa, sb=sb: e.tensor_tensor(
                    out=dt_t[d], in0=sa, in1=sb, op=ALU.add))
                v_dtadd[d] = EM("vector", lambda e, d=d, xb=xbf[d]: e.tensor_tensor(
                    out=dub_t[d], in0=dt_t[d], in1=xb, op=ALU.mult), Sv, 1)
                if d == NDT - 1:
                    v_dtu_prev = v_dtadd[d]

            v_bc_ev = [0, 0]
            for n in range(NST):
                par = n % 2
                psB = ps_c if par == 0 else ps_e
                psC = ps_d if par == 0 else ps_f
                bcB = bcb_t[2 * par]
                bcC = bcb_t[2 * par + 1]
                if n == 0:
                    WT("tensor", Sv, v_bcn)
                else:
                    WT("tensor", Sv, v_bc_read[par])
                EM("tensor", lambda e, n=n, psB=psB: e.matmul(
                    psB[:], sel_b[:, n * 128:(n + 1) * 128], bN, start=True, stop=True))
                v_bcast = EM("tensor", lambda e, n=n, psC=psC: e.matmul(
                    psC[:], sel_b[:, n * 128:(n + 1) * 128], cN, start=True, stop=True), St_, 1)
                WT("scalar", St_, v_bcast)
                if n >= 2:
                    WT("scalar", Sv, v_bc_read[par])
                else:
                    WT("scalar", Sv, v_bc_read[par])
                EM("scalar", lambda e, psB=psB, bcB=bcB: e.activation(
                    out=bcB, in_=psB[:], func=AF.Copy))
                v_bc_ev[par] = EM("scalar", lambda e, psC=psC, bcC=bcC: e.activation(
                    out=bcC, in_=psC[:], func=AF.Copy), Ss, 1)
                WT("vector", Ss, v_bc_ev[par])
                for d in range(NDT):
                    g = n * NDT + d
                    j = g % 4
                    pr = g % 2
                    if g >= 4:
                        WT("scalar", Sv, v_scan_use.get((k, g - 4), 0))
                    else:
                        WT("scalar", Sv, v_dtadd[NDT - 1])
                    v_da = EM("scalar", lambda e, d=d, n=n, j=j: e.activation(
                        out=dAb_t[j], in_=dt_t[d], func=AF.Exp,
                        scale=A_sb[:, d * 16 + n:d * 16 + n + 1]), Ss, 1)
                    EM("vector", lambda e, d=d, pr=pr, bcB=bcB: e.tensor_tensor(
                        out=tbb_t[pr], in0=dub_t[d], in1=bcB, op=ALU.mult))
                    WT("vector", Ss, v_da)
                    cidx = d * 16 + n
                    EM("vector", lambda e, j=j, pr=pr, cidx=cidx: e.tensor_tensor_scan(
                        out=tsb_t[pr], data0=dAb_t[j], data1=tbb_t[pr],
                        initial=carry[:, cidx:cidx + 1], op0=ALU.mult, op1=ALU.add))
                    v_cy = EM("vector", lambda e, pr=pr, bcC=bcC: e.tensor_tensor(
                        out=tcb_t[pr], in0=tsb_t[pr], in1=bcC, op=ALU.mult), Sv, 1)
                    v_scan_use[(k, g)] = v_cy
                    if n == 0:
                        EM("vector", lambda e, d=d, pr=pr: e.tensor_copy(
                            out=ya_t[d], in_=tcb_t[pr]))
                    else:
                        EM("vector", lambda e, d=d, pr=pr: e.tensor_tensor(
                            out=ya_t[d], in0=ya_t[d], in1=tcb_t[pr], op=ALU.add))
                    EM("vector", lambda e, pr=pr, cidx=cidx: e.tensor_copy(
                        out=carry[:, cidx:cidx + 1], in_=tsb_t[pr][:, 511:512]))
                    if d == NDT - 1:
                        v_bc_read[par] = v_cy

            for d in range(NDT):
                pr = d % 2
                EM("vector", lambda e, d=d, pr=pr: e.tensor_scalar(
                    out=tb_t[pr], in0=xbf[d], scalar1=dvec_sb[:, d:d + 1],
                    scalar2=None, op0=ALU.mult))
                EM("vector", lambda e, d=d, pr=pr: e.tensor_tensor(
                    out=ya_t[d], in0=ya_t[d], in1=tb_t[pr], op=ALU.add))
                WT("vector", Sgp, Sgp.n)
                v_y3 = EM("vector", lambda e, d=d, pr=pr: e.tensor_tensor(
                    out=yst[pr], in0=ya_t[d], in1=gbf[d], op=ALU.mult), Sv, 1)
                WT("gpsimd", Sv, v_y3)
                ybf_t = ybf_e if k % 2 == 0 else ybf_o
                v_yst[pr] = EM("gpsimd", lambda e, d=d, k=k, pr=pr, yb=ybf_t: e.dma_start(
                    out=yb.ap()[k // 2, d * 128:(d + 1) * 128, :], in_=yst[pr]), Sg, 16)
                CERT("gpsimd")
        emit_A(0)
        for k in range(1, NCHUNK):
            emit_A(k)
            emit_C(k - 1)
            if k - 1 == 14:
                WT("gpsimd", Sg, Sg.n)
                WT("gpsimd", Scc, Scc.n)
                v_agy_e = EM("gpsimd", lambda e: e.collective_compute(
                    "AllGather", ALU.bypass, replica_groups=RG,
                    ins=[ybf_e.ap()], outs=[yag_e.ap()]), Scc, 1)
        emit_C(NCHUNK - 1)
        WT("gpsimd", Sg, Sg.n)
        WT("gpsimd", Scc, Scc.n)
        v_agy_o = EM("gpsimd", lambda e: e.collective_compute(
            "AllGather", ALU.bypass, replica_groups=RG,
            ins=[ybf_o.ap()], outs=[yag_o.ap()]), Scc, 1)

        y_t = [abf(Y_O + kk * 512, 512) for kk in range(64)]
        wo_sl = [abf(WOUT_O, 4096), abf(WOUT_O + 4096, 4096)]
        xln_t = [abf(XLN_O + kk * 512, 512) for kk in range(32)]
        wf_sl = [abf(WFF_O, 4096), abf(WFF_O + 4096, 4096)]
        x1p = [af32(X1_O, 512), af32(X1_O + 512, 512)]
        scr = af32(SCR_O, 512)
        x0p = [af32(XI_O, 512), af32(XI_O + 512, 512)]
        otp = [af32(ACC_O, 512), af32(ACC_O + 512, 512)]
        yagv_e = yag_e.ap().rearrange("(r k) ch t -> r k ch t", k=8)
        yagv_o = yag_o.ap().rearrange("(r k) ch t -> r k ch t", k=8)
        wov = wout_all.ap().rearrange("(r p) (m k f) -> r p m k f", p=128, m=32, k=8)
        wfv = wffn_all.ap().rearrange("(r p) (o k f) -> r p o k f", p=128, o=4, k=32)


        v_wo_free = [0, 0]
        v_wf_free = [0, 0]
        v_ost = [0, 0]
        v_x1s_arr = [0] * 32
        v_sq_arr = [0] * 32
        for sub in range(2):
            col = sub * 512
            v_y = [0] * 8
            WT("sync", Scc, v_agy_e if sub == 0 else v_agy_o)
            if sub == 1:
                WT("sync", St_, St_.n)
            for g in range(8):
                yv = yagv_e if sub == 0 else yagv_o
                def _yld(e, g=g, yv=yv):
                    return e.dma_start(
                        out=ab[0:128, Y_O + g * 4096:Y_O + (g + 1) * 4096]
                        .rearrange("p (kk t) -> p kk t", t=512),
                        in_=yv[g, ds(hold["rank"], 1), :, :]
                        .squeeze(0).rearrange("(kk p) t -> p kk t", p=128))
                v_y[g] = EM("sync", _yld, Sld, 16)
            CERT("sync")
            if sub == 0:
                WT("sync", Sv, Sv.n)
                WT("sync", Ss, Ss.n)
                WT("sync", Sgp, Sgp.n)
            v_amm = [0] * 32
            v_ev = [0] * 32
            v_half_done = [0, 0]
            for m in range(32):
                sl = m % 2
                v_wo = [0, 0]
                for hf2 in range(2):
                    WT("sync", St_, v_half_done[hf2])
                    v_wo[hf2] = EM("sync", lambda e, m=m, hf2=hf2: e.dma_start(
                        out=ab[0:128, WOUT_O + hf2 * 4096:WOUT_O + (hf2 + 1) * 4096]
                        .rearrange("p (r q) -> p r q", q=1024),
                        in_=wov[4 * hf2:4 * (hf2 + 1), :, m].rearrange("r p k f -> p r (k f)")),
                        Sld, 16)
                if m >= 2:
                    WT("sync", Sv, v_ev[m - 2])
                v_x0 = EM("sync", lambda e, m=m, col=col, sl=sl: e.dma_start(
                    out=x0p[sl], in_=x0T[m * 128:(m + 1) * 128, col:col + 512]), Sld, 16)
                CERT("sync")
                psm = ps_m0 if sl == 0 else ps_m1
                if m >= 2:
                    WT("tensor", Sv, v_ev[m - 2])
                for kk in range(64):
                    if kk == 0:
                        WT("tensor", Slp, Slp.n)
                    if kk == 32:
                        WT("tensor", Slp, Slp.n)
                    if m == 0 and kk % 8 == 0:
                        WT("tensor", Slp, Slp.n)
                    last = (kk == 63)
                    v = EM("tensor", lambda e, psm=psm, kk=kk: e.matmul(
                        psm[:], ab[0:128, WOUT_O + (kk // 32) * 4096 + (kk % 32) * 128:WOUT_O + (kk // 32) * 4096 + (kk % 32) * 128 + 128],
                        y_t[kk], start=(kk == 0), stop=(kk == 63)),
                        St_ if (last or kk == 31) else None, 1)
                    if kk == 31:
                        v_half_done[0] = v
                    if last:
                        v_amm[m] = v
                v_half_done[1] = v_amm[m]
                WT("vector", St_, v_amm[m])
                WT("vector", Slp, Slp.n)
                pr = m % 2
                if m >= 2:
                    WT("vector", Ss, v_sq_arr[m - 2])
                    WT("vector", Sgp, Sgp.n)
                v_x1 = EM("vector", lambda e, psm=psm, pr=pr: e.tensor_tensor(
                    out=x1p[pr], in0=x0p[pr], in1=psm[:], op=ALU.add), Sv, 1)
                v_ev[m] = v_x1
                WT("scalar", Sv, v_x1)
                v_sq_arr[m] = EM("scalar", lambda e, pr=pr: e.activation(
                    out=scr, in_=x1p[pr], func=AF.Square), Ss, 1)
                WT("tensor", Sv, v_x1)
                WT("tensor", Ss, v_sq_arr[m])
                EM("tensor", lambda e, pr=pr, m=m: e.matmul(
                    ps_d[0:1, :], ones128[:], x1p[pr], start=(m == 0), stop=(m == 31),
                    skip_group_check=True))
                v = EM("tensor", lambda e, m=m: e.matmul(
                    ps_e[0:1, :], ones128[:], scr, start=(m == 0), stop=(m == 31),
                    skip_group_check=True), St_, 1)
                if m == 31:
                    v_ln2stats = v
                WT("gpsimd", Sv, v_x1)
                v_x1s_arr[m] = EM("gpsimd", lambda e, m=m, sub=sub, pr=pr: e.dma_start(
                    out=x1d.ap()[sub, m * 128:(m + 1) * 128, :], in_=x1p[pr]), Sg, 16)
                CERT("gpsimd")

            WT("scalar", St_, v_ln2stats)
            EM("scalar", lambda e: e.activation(out=rowa[:], in_=ps_d[0:1, :], func=AF.Copy, scale=1.0 / H))
            v_m2 = EM("scalar", lambda e: e.activation(out=rowd[:], in_=rowa[:], func=AF.Square), Ss, 1)
            WT("vector", Ss, v_m2)
            v_v2 = EM("vector", lambda e: e.scalar_tensor_tensor(
                out=rowe[:], in0=ps_e[0:1, :], scalar=1.0 / H,
                in1=rowd[:], op0=ALU.mult, op1=ALU.subtract), Sv, 1)
            WT("scalar", Sv, v_v2)
            EM("scalar", lambda e: e.activation(out=rowd[:], in_=rowe[:], func=AF.Ln, bias=epsc[0:1, 0:1]))
            v_r2 = EM("scalar", lambda e: e.activation(out=rowb[:], in_=rowd[:], func=AF.Exp, scale=-0.5), Ss, 1)
            WT("tensor", Ss, v_r2)
            EM("tensor", lambda e: e.matmul(ps_c[:], ones1r[:], rowa[:], start=True, stop=True))
            v_bc2 = EM("tensor", lambda e: e.matmul(ps_d[:], ones1r[:], rowb[:], start=True, stop=True), St_, 1)

            v_xln = [0] * 32
            WT("sync", Sgp, Sgp.n)
            WT("sync", Ss, v_sq_arr[31])
            for kk in range(32):
                pr = kk % 2
                if kk >= 2:
                    WT("sync", Sv, v_xln[kk - 2])
                v_lx = EM("sync", lambda e, kk=kk, sub=sub, pr=pr: e.dma_start(
                    out=x1p[pr], in_=x1d.ap()[sub, kk * 128:(kk + 1) * 128, :]), Sld, 16)
                CERT("sync")
                WT("vector", Slp, Slp.n)
                if kk == 0:
                    WT("vector", St_, v_bc2)
                EM("vector", lambda e, pr=pr: e.tensor_tensor(
                    out=scr, in0=x1p[pr], in1=ps_c[:], op=ALU.subtract))
                v_xln[kk] = EM("vector", lambda e, kk=kk: e.tensor_tensor(
                    out=xln_t[kk], in0=scr, in1=ps_d[:], op=ALU.mult), Sv, 1)

            if sub == 0:
                WT("sync", Scc, v_ag_wffn)
                WT("sync", Sgp, Sgp.n)
            v_fmm = [0] * 32
            v_fev = [0] * 32
            for m in range(32):
                sl = m % 2
                if m == 0:
                    WT("sync", St_, v_amm[31])
                WT("sync", St_, v_wf_free[sl])
                v_wf = EM("sync", lambda e, m=m, sl=sl: e.dma_start(
                    out=wf_sl[sl],
                    in_=wfv[m // 4, :, m % 4].rearrange("p k f -> p (k f)")), Sld, 16)
                if m >= 2:
                    WT("sync", Sv, v_fev[m - 2])
                else:
                    WT("sync", Sv, v_xln[31])
                v_x1r = EM("sync", lambda e, m=m, sub=sub, sl=sl: e.dma_start(
                    out=x1p[sl], in_=x1d.ap()[sub, m * 128:(m + 1) * 128, :]), Sld, 16)
                CERT("sync")
                psm = ps_m0 if sl == 0 else ps_m1
                WT("tensor", Slp, Slp.n)
                if m >= 2:
                    WT("tensor", Sv, v_fev[m - 2])
                for kk in range(32):
                    if m == 0:
                        WT("tensor", Sv, v_xln[kk])
                    last = (kk == 31)
                    v = EM("tensor", lambda e, psm=psm, sl=sl, kk=kk: e.matmul(
                        psm[:], ab[0:128, WFF_O + sl * 4096 + kk * 128:WFF_O + sl * 4096 + (kk + 1) * 128],
                        xln_t[kk], start=(kk == 0), stop=(kk == 31)),
                        St_ if last else None, 1)
                    if last:
                        v_fmm[m] = v
                v_wf_free[sl] = v_fmm[m]
                WT("vector", St_, v_fmm[m])
                WT("vector", Slp, Slp.n)
                pr = m % 2
                WT("vector", Sgp, Sgp.n)
                v_x2 = EM("vector", lambda e, psm=psm, pr=pr, m=m: e.scalar_tensor_tensor(
                    out=otp[pr], in0=psm[:], scalar=biasc_sb[:, m:m + 1],
                    in1=x1p[pr], op0=ALU.add, op1=ALU.add), Sv, 1)
                v_fev[m] = v_x2
                WT("gpsimd", Sv, v_x2)
                v_ost[pr] = EM("gpsimd", lambda e, m=m, col=col, pr=pr: e.dma_start(
                    out=oT[m * 128:(m + 1) * 128, col:col + 512], in_=otp[pr]), Sg, 16)
                CERT("gpsimd")

        if debug:
            dbg = {
                "dbg_hsln": (hsln_all, [8 * H, 1024], BF16),
                "dbg_ssm": (ssmred, [NCHUNK, 288, CT], F32),
                "dbg_ssmp": (ssmpart, [NCHUNK, 288, CT], F32),
                "dbg_u": (dbgu, [NCHUNK, 1024, CT], BF16),
                "dbg_p": (dbgp, [2, 1024, CT], F32),
                "dbg_r": (dbgr, [4096, CT], BF16),
                "dbg_w": (dbgw, [128, 4096], BF16),
                "dbg_g": (dbgg, [NCHUNK, 1024, CT], BF16),
                "dbg_ye": (ybf_e, [8, 1024, CT], BF16),
                "dbg_yage": (yag_e, [64, 1024, CT], BF16),
                "dbg_yo": (ybf_o, [8, 1024, CT], BF16),
                "dbg_x1": (x1d, [2, H, CT], F32),
                "dbg_wffn": (wffn_all, [8 * 128, 16384], BF16),
                "dbg_wout": (wout_all, [8 * 128, 32768], BF16),
            }
            for name, (t, shp, dt_) in dbg.items():
                o = nc.dram_tensor(name, shp, dt_, kind="ExternalOutput").ap()
                EM("gpsimd", lambda e, o=o, t=t: e.dma_start(out=o, in_=t.ap()), Sg, 16)
        WT("gpsimd", Sg, Sg.n)

        block = es.enter_context(nc.Block())
        block.sync(lambda eng: [f(eng) for f in prog["sync"]])
        block.vector(lambda eng: [f(eng) for f in prog["vector"]])
        block.scalar(lambda eng: [f(eng) for f in prog["scalar"]])
        block.tensor(lambda eng: [f(eng) for f in prog["tensor"]])
        block.gpsimd(lambda eng: [f(eng) for f in prog["gpsimd"]])

    return nc


_NC_CACHE = {}


def _host_layout(inputs):
    x = np.asarray(inputs["L0_attn_hidden_states"])
    W_in = np.asarray(inputs["L0_attn_W_in"])
    conv_w = np.asarray(inputs["L0_attn_conv_w"])
    conv_b = np.asarray(inputs["L0_attn_conv_b"])
    W_x = np.asarray(inputs["L0_attn_W_x"])
    W_dt = np.asarray(inputs["L0_attn_W_dt"])
    b_dt = np.asarray(inputs["L0_attn_b_dt"])
    A_log = np.asarray(inputs["L0_attn_A_log"])
    Dv = np.asarray(inputs["L0_attn_D"])
    W_out = np.asarray(inputs["L0_attn_W_out"])
    q3 = np.asarray(inputs["L0_ffn_weight_q3"])
    rng = np.asarray(inputs["L0_ffn_weight_ranges"])
    bias = np.asarray(inputs["L0_ffn_bias"])

    xf = x.reshape(8192, 4096)
    q3_2d = q3.reshape(4096, 4096)
    in_maps = []
    for c in range(NCORE):
        cs, ce = c * 1024, (c + 1) * 1024
        d = {}
        d["x0T"] = np.ascontiguousarray(xf[cs:ce].T)
        d["winx"] = np.ascontiguousarray(
            W_in[cs:ce].reshape(8, 128, 32, 128).transpose(0, 3, 2, 1)).reshape(1024, 4096)
        d["wing"] = np.ascontiguousarray(
            W_in[DIN + cs:DIN + ce].reshape(8, 128, 32, 128).transpose(0, 3, 2, 1)).reshape(1024, 4096)
        d["wxT"] = np.ascontiguousarray(
            W_x[:, cs:ce].T.reshape(8, 128, 288).transpose(1, 0, 2)).reshape(128, 8 * 288)
        d["wdtT"] = np.ascontiguousarray(
            W_dt[cs:ce].T.reshape(2, 128, 1024).transpose(1, 0, 2)).reshape(128, 2048)
        d["wout"] = np.ascontiguousarray(
            W_out[:, cs:ce].reshape(32, 128, 8, 128).transpose(3, 0, 2, 1)).reshape(128, 32768)
        d["q3c"] = np.ascontiguousarray(q3_2d[c * 512:(c + 1) * 512])
        d["rngc"] = np.ascontiguousarray(rng[c * 65536:(c + 1) * 65536].reshape(512, 256))
        d["biasc"] = np.ascontiguousarray(bias.reshape(32, 128).T)
        d["alog"] = np.ascontiguousarray(
            A_log[cs:ce].reshape(8, 128, 16).transpose(1, 0, 2)).reshape(128, 128)
        d["convw"] = np.ascontiguousarray(
            conv_w[cs:ce].reshape(8, 128, 4).transpose(1, 0, 2)).reshape(128, 32)
        d["convb"] = np.ascontiguousarray(conv_b[cs:ce].reshape(8, 128).T)
        d["bdt"] = np.ascontiguousarray(b_dt[cs:ce].reshape(8, 128).T)
        d["dvec"] = np.ascontiguousarray(Dv[cs:ce].reshape(8, 128).T)
        in_maps.append(d)
    return in_maps


def kernel(**inputs):
    in_maps = _host_layout(inputs)
    if "nc" not in _NC_CACHE:
        _NC_CACHE["nc"] = build_program()
    nc = _NC_CACHE["nc"]
    res = run_bass_kernel_spmd(nc, in_maps, core_ids=list(range(NCORE)))
    out = np.empty((8192, 4096), dtype=np.float32)
    for c in range(NCORE):
        out[c * 1024:(c + 1) * 1024] = res.results[c]["oT"].T
    return out.reshape(2, 4096, 4096)


# version 18
# speedup vs baseline: 1.0245x; 1.0077x over previous
import numpy as np
import contextlib
import concourse.bass as bass
import concourse.mybir as mybir
from concourse.bass_utils import run_bass_kernel_spmd

F32 = mybir.dt.float32
BF16 = mybir.dt.bfloat16
F16 = mybir.dt.float16
I32 = mybir.dt.int32
AF = mybir.ActivationFunctionType
ALU = mybir.AluOpType
ds = bass.ds

H = 4096
DIN = 8192
NST = 16
RDT = 256
NCORE = 8
CT = 512
NCHUNK = 16
NDT = 8
LN_EPS = 1e-5
RMS_EPS = 1e-6
RG = [list(range(NCORE))]


def build_program(debug=False):
    nc = bass.Bass("TRN2", target_bir_lowering=False)

    x0T = nc.dram_tensor("x0T", [H, 1024], F32, kind="ExternalInput").ap()
    winx = nc.dram_tensor("winx", [1024, 4096], F32, kind="ExternalInput").ap()
    wing = nc.dram_tensor("wing", [1024, 4096], F32, kind="ExternalInput").ap()
    wxT = nc.dram_tensor("wxT", [128, 8 * 288], F32, kind="ExternalInput").ap()
    wdtT = nc.dram_tensor("wdtT", [128, 2048], F32, kind="ExternalInput").ap()
    wout = nc.dram_tensor("wout", [128, 32768], F32, kind="ExternalInput").ap()
    q3c = nc.dram_tensor("q3c", [512, 4096], I32, kind="ExternalInput").ap()
    rngc = nc.dram_tensor("rngc", [512, 256], F16, kind="ExternalInput").ap()
    biasc = nc.dram_tensor("biasc", [128, 32], F16, kind="ExternalInput").ap()
    alog = nc.dram_tensor("alog", [128, 128], F32, kind="ExternalInput").ap()
    convw = nc.dram_tensor("convw", [128, 32], F32, kind="ExternalInput").ap()
    convb = nc.dram_tensor("convb", [128, 8], F32, kind="ExternalInput").ap()
    bdt = nc.dram_tensor("bdt", [128, 8], F32, kind="ExternalInput").ap()
    dvec = nc.dram_tensor("dvec", [128, 8], F32, kind="ExternalInput").ap()
    oT = nc.dram_tensor("oT", [H, 1024], F32, kind="ExternalOutput").ap()

    winx_bf = nc.dram_tensor("winx_bf", [1024, 4096], BF16)
    wing_bf = nc.dram_tensor("wing_bf", [1024, 4096], BF16)
    woutp_bf = nc.dram_tensor("woutp_bf", [128, 32768], BF16)
    wout_all = nc.dram_tensor("wout_all", [8 * 128, 32768], BF16)
    hsln_part = nc.dram_tensor("hsln_part", [H, 1024], BF16)
    hsln_all = nc.dram_tensor("hsln_all", [8 * H, 1024], BF16)
    ssmpart = nc.dram_tensor("ssmpart", [NCHUNK, 288, CT], F32)
    ssmred = nc.dram_tensor("ssmred", [NCHUNK, 288, CT], F32)
    wffn_part = nc.dram_tensor("wffn_part", [128, 16384], BF16)
    wffn_all = nc.dram_tensor("wffn_all", [8 * 128, 16384], BF16)

    yag_e = nc.dram_tensor("yag_e", [8 * 8, 1024, CT], BF16)
    yag_o = nc.dram_tensor("yag_o", [8 * 8, 1024, CT], BF16)
    ybf_e = nc.dram_tensor("ybf_e", [8, 1024, CT], BF16)
    ybf_o = nc.dram_tensor("ybf_o", [8, 1024, CT], BF16)
    x1d = nc.dram_tensor("x1d", [2, H, CT], F32)
    dbgu = nc.dram_tensor("dbgu", [NCHUNK, 1024, CT], BF16)
    dbgg = nc.dram_tensor("dbgg", [NCHUNK, 1024, CT], BF16)
    dbgp = nc.dram_tensor("dbgp", [2, 1024, CT], F32)
    dbgr = nc.dram_tensor("dbgr", [4096, CT], BF16)
    dbgw = nc.dram_tensor("dbgw", [128, 4096], BF16)

    es = contextlib.ExitStack()
    with es:
        ab = es.enter_context(nc.sbuf_tensor("ab", [128, 49152], BF16))
        af = es.enter_context(nc.sbuf_tensor("af", [128, 20992], F32))
        id_bf = es.enter_context(nc.sbuf_tensor("id_bf", [128, 128], BF16))
        id_f32 = es.enter_context(nc.sbuf_tensor("id_f32", [128, 128], F32))
        sel_b = es.enter_context(nc.sbuf_tensor("sel_b", [16, 2048], BF16))
        rowa = es.enter_context(nc.sbuf_tensor("rowa", [1, 512], F32))
        rowb = es.enter_context(nc.sbuf_tensor("rowb", [1, 512], F32))
        rowc = es.enter_context(nc.sbuf_tensor("rowc", [1, 512], F32))
        rowd = es.enter_context(nc.sbuf_tensor("rowd", [1, 512], F32))
        rowe = es.enter_context(nc.sbuf_tensor("rowe", [1, 512], F32))
        ones1r = es.enter_context(nc.sbuf_tensor("ones1r", [1, 128], F32))
        ones16r = es.enter_context(nc.sbuf_tensor("ones16r", [1, 16], F32))
        ones128 = es.enter_context(nc.sbuf_tensor("ones128", [128, 1], F32))
        ones16 = es.enter_context(nc.sbuf_tensor("ones16", [16, 1], F32))
        wxT_sb = es.enter_context(nc.sbuf_tensor("wxT_sb", [128, 8 * 288], BF16))
        wdtT_sb = es.enter_context(nc.sbuf_tensor("wdtT_sb", [128, 2048], BF16))
        A_sb = es.enter_context(nc.sbuf_tensor("A_sb", [128, 128], F32))

        convw_sb = es.enter_context(nc.sbuf_tensor("convw_sb", [128, 32], F32))
        convb_sb = es.enter_context(nc.sbuf_tensor("convb_sb", [128, 8], F32))
        bdt_sb = es.enter_context(nc.sbuf_tensor("bdt_sb", [128, 8], F32))
        dvec_sb = es.enter_context(nc.sbuf_tensor("dvec_sb", [128, 8], F32))
        biasc_sb = es.enter_context(nc.sbuf_tensor("biasc_sb", [128, 32], F32))
        carry = es.enter_context(nc.sbuf_tensor("carry", [128, 128], F32))
        halo = es.enter_context(nc.sbuf_tensor("halo", [128, 24], F32))
        epsc = es.enter_context(nc.sbuf_tensor("epsc", [1, 4], F32))
        one128 = es.enter_context(nc.sbuf_tensor("one128", [128, 1], F32))
        sclT = es.enter_context(nc.sbuf_tensor("sclT", [128, 128], BF16))
        mnnT = es.enter_context(nc.sbuf_tensor("mnnT", [128, 128], BF16))

        ps_m0 = es.enter_context(nc.psum_tensor("ps_m0", [128, 512], F32))
        ps_m1 = es.enter_context(nc.psum_tensor("ps_m1", [128, 512], F32))
        ps_a = es.enter_context(nc.psum_tensor("ps_a", [128, 512], F32))
        ps_b = es.enter_context(nc.psum_tensor("ps_b", [128, 512], F32))
        ps_c = es.enter_context(nc.psum_tensor("ps_c", [128, 512], F32))
        ps_d = es.enter_context(nc.psum_tensor("ps_d", [128, 512], F32))
        ps_e = es.enter_context(nc.psum_tensor("ps_e", [128, 512], F32))
        ps_f = es.enter_context(nc.psum_tensor("ps_f", [128, 512], F32))

        sem_ld = es.enter_context(nc.semaphore("sem_ld"))
        sem_g = es.enter_context(nc.semaphore("sem_g"))
        sem_cc = es.enter_context(nc.semaphore("sem_cc"))
        sem_t = es.enter_context(nc.semaphore("sem_t"))
        sem_v = es.enter_context(nc.semaphore("sem_v"))
        sem_s = es.enter_context(nc.semaphore("sem_s"))
        sem_lp = es.enter_context(nc.semaphore("sem_lp"))
        sem_gp = es.enter_context(nc.semaphore("sem_gp"))

        RHS_O = 0
        WM_O = 16384
        XBF_O = 24576
        GBF_O = 28672
        TSN_O = 32768
        YST_O = 33792
        HST_O = 34816
        BN_O = 35840
        DQ_O = 36864
        Y_O = 0
        XLN_O = 0
        WOUT_O = 32768
        WFF_O = 32768
        WNAT_O = 16384

        DT_O = 0
        DA_O = 4096
        DTU_O = 6144
        SAB_O = 8192
        YA_O = 10240
        TB_O = 14336
        STG_O = 15360
        SQ_O = 16896
        T0_O = 17920
        ACC_O = 18944
        BCR_O = 19968
        XCAT_O = 0
        XI_O = 0
        X1_O = 14336
        SCR_O = 15360

        def abf(off, n, p=128):
            return ab[0:p, off:off + n]

        def af32(off, n, p=128):
            return af[0:p, off:off + n]

        prog = {k: [] for k in ("sync", "vector", "scalar", "tensor", "gpsimd")}

        class S:
            __slots__ = ("h", "n")

            def __init__(self, h):
                self.h = h
                self.n = 0

        Sld, Sg, Scc, St_, Sv, Ss = (S(sem_ld), S(sem_g), S(sem_cc),
                                     S(sem_t), S(sem_v), S(sem_s))
        Slp, Sgp = S(sem_lp), S(sem_gp)

        def EM(e, fn, sem=None, k=1):
            if sem is None:
                prog[e].append(fn)
                return None
            sem.n += k
            prog[e].append(lambda eng, fn=fn, s=sem, k=k: fn(eng).then_inc(s.h, k))
            return sem.n

        def WT(e, sem, v):
            if v > 0:
                prog[e].append(lambda eng, s=sem, v=v: eng.wait_ge(s.h, v))

        def CERT(queue):
            qs, ps = (Sld, Slp) if queue == "sync" else (Sg, Sgp)
            n = qs.n
            ps.n += 1
            prog[queue].append(
                lambda eng, n=n, q=qs, p=ps: eng.wait_ge(q.h, n).then_inc(p.h, 1))
            return ps.n

        hold = {}

        EM("gpsimd", lambda e: e.memset(id_bf[:], 0.0))
        EM("gpsimd", lambda e: e.affine_select(
            out=id_bf[:], in_=id_bf[:], compare_op=ALU.not_equal, fill=1.0,
            base=0, pattern=[[-1, 128]], channel_multiplier=1))
        EM("gpsimd", lambda e: e.memset(id_f32[:], 0.0))
        v_ident = EM("gpsimd", lambda e: e.affine_select(
            out=id_f32[:], in_=id_f32[:], compare_op=ALU.not_equal, fill=1.0,
            base=0, pattern=[[-1, 128]], channel_multiplier=1), Sgp, 1)
        P0 = abf(DQ_O, 4096)
        EM("gpsimd", lambda e: e.memset(P0, 1.0))
        EM("gpsimd", lambda e: e.affine_select(
            out=P0, in_=P0, compare_op=ALU.is_ge, fill=0.0,
            base=0, pattern=[[1, 4096]], channel_multiplier=-32))
        v_p0 = EM("gpsimd", lambda e: e.affine_select(
            out=P0, in_=P0, compare_op=ALU.is_ge, fill=0.0,
            base=31, pattern=[[-1, 4096]], channel_multiplier=32), Sgp, 1)

        EM("gpsimd", lambda e: e.memset(sel_b[:], 0.0))
        v_sel = EM("gpsimd", lambda e: e.affine_select(
            out=sel_b[:].rearrange("g (j m) -> g j m", m=128),
            in_=sel_b[:].rearrange("g (j m) -> g j m", m=128),
            compare_op=ALU.not_equal, fill=1.0, base=0,
            pattern=[[1, 16], [0, 128]], channel_multiplier=-1), Sgp, 1)
        EM("vector", lambda e: e.memset(ones128[:], 1.0))
        EM("vector", lambda e: e.memset(ones16[:], 1.0))
        EM("vector", lambda e: e.memset(ones1r[:], 1.0))
        EM("vector", lambda e: e.memset(ones16r[:], 1.0))
        EM("vector", lambda e: e.memset(carry[:], 0.0))
        EM("vector", lambda e: e.memset(epsc[0:1, 0:1], 1e-5))
        EM("vector", lambda e: e.memset(epsc[0:1, 1:2], 1e-6))
        EM("vector", lambda e: e.memset(epsc[0:1, 2:3], 1.0))
        EM("vector", lambda e: e.memset(one128[:], 1.0))
        v_pre_v = EM("vector", lambda e: e.memset(halo[:], 0.0), Sv, 1)
        WT("scalar", Sv, v_pre_v)

        for (dst, src) in ((convw_sb[:], convw), (convb_sb[:], convb),
                           (bdt_sb[:], bdt), (dvec_sb[:], dvec), (af32(1408, 128), alog)):
            EM("sync", lambda e, d=dst, s=src: e.dma_start(out=d, in_=s), Sld, 16)
        CERT("sync")
        v_consts = Sld.n
        WT("scalar", Slp, Slp.n)
        v_expA = EM("scalar", lambda e: e.activation(
            out=A_sb[:], in_=af32(1408, 128), func=AF.Exp), Ss, 1)
        WT("vector", Ss, v_expA)
        v_negA = EM("vector", lambda e: e.tensor_scalar(
            out=A_sb[:], in0=A_sb[:], scalar1=-1.0, scalar2=None, op0=ALU.mult), Sv, 1)
        WT("scalar", Sv, v_negA)

        v_cwx = EM("gpsimd", lambda e: e.dma_start(out=wxT_sb[:], in_=wxT), Sg, 16)
        v_cwdt = EM("gpsimd", lambda e: e.dma_start(out=wdtT_sb[:], in_=wdtT), Sg, 16)
        v_cbias = EM("gpsimd", lambda e: e.dma_start(out=biasc_sb[:], in_=biasc), Sg, 16)
        v_cwinx = EM("gpsimd", lambda e: e.dma_start(out=winx_bf.ap(), in_=winx), Sg, 16)
        v_cwing = EM("gpsimd", lambda e: e.dma_start(out=wing_bf.ap(), in_=wing), Sg, 16)
        v_cwout = EM("gpsimd", lambda e: e.dma_start(out=woutp_bf.ap(), in_=wout), Sg, 16)
        CERT("gpsimd")

        def _rank(e):
            hold["rank"] = e.cc_rank(replica_groups=RG)
        EM("sync", _rank)

        xi = [af32(XI_O, 512), af32(XI_O + 512, 512)]
        sqp = [af32(SQ_O, 512), af32(SQ_O + 512, 512)]
        hst = [abf(HST_O, 512), abf(HST_O + 512, 512)]
        v_hst_st = [0, 0]
        v_ln_tile = [0] * 32
        v_nm_last = 0
        for hf in range(2):
            col = hf * 512
            for kk in range(32):
                sl = kk % 2
                if hf > 0 or kk >= 2:
                    WT("sync", St_, v_ln_tile[kk - 2] if kk >= 2 else v_ln_tile[31])
                if hf == 1 and kk < 2:
                    WT("sync", Sv, v_nm_last)
                v_ld = EM("sync", lambda e, s=xi[sl], kk=kk, col=col: e.dma_start(
                    out=s, in_=x0T[kk * 128:(kk + 1) * 128, col:col + 512]), Sld, 16)
                CERT("sync")
                WT("scalar", Slp, Slp.n)
                if hf > 0 or kk >= 2:
                    WT("scalar", St_, v_ln_tile[kk - 2] if kk >= 2 else v_ln_tile[31])
                v_sq = EM("scalar", lambda e, s=xi[sl], q=sqp[sl]: e.activation(
                    out=q, in_=s, func=AF.Square), Ss, 1)
                WT("tensor", Slp, Slp.n)
                EM("tensor", lambda e, s=xi[sl], kk=kk: e.matmul(
                    ps_d[0:1, :], ones128[:], s, start=(kk == 0), stop=(kk == 31),
                    skip_group_check=True))
                WT("tensor", Ss, v_sq)
                v_ln_tile[kk] = EM("tensor", lambda e, q=sqp[sl], kk=kk: e.matmul(
                    ps_e[0:1, :], ones128[:], q, start=(kk == 0), stop=(kk == 31),
                    skip_group_check=True), St_, 1)
            WT("scalar", St_, v_ln_tile[31])
            EM("scalar", lambda e: e.activation(
                out=rowa[:], in_=ps_d[0:1, :], func=AF.Copy, scale=1.0 / H))
            v_mean = EM("scalar", lambda e: e.activation(
                out=rowd[:], in_=rowa[:], func=AF.Square), Ss, 1)
            WT("vector", Ss, v_mean)
            v_var = EM("vector", lambda e: e.scalar_tensor_tensor(
                out=rowe[:], in0=ps_e[0:1, :], scalar=1.0 / H,
                in1=rowd[:], op0=ALU.mult, op1=ALU.subtract), Sv, 1)
            WT("scalar", Sv, v_var)
            EM("scalar", lambda e: e.activation(
                out=rowd[:], in_=rowe[:], func=AF.Ln, bias=epsc[0:1, 0:1]))
            v_rstd = EM("scalar", lambda e: e.activation(
                out=rowb[:], in_=rowd[:], func=AF.Exp, scale=-0.5), Ss, 1)
            WT("tensor", Ss, v_rstd)
            EM("tensor", lambda e: e.matmul(
                ps_c[:], ones1r[:], rowa[:], start=True, stop=True))
            v_bc = EM("tensor", lambda e: e.matmul(
                ps_d[:], ones1r[:], rowb[:], start=True, stop=True), St_, 1)
            WT("sync", St_, v_ln_tile[31])
            for kk in range(32):
                sl = kk % 2
                WT("sync", Sv, Sv.n)
                v_ld = EM("sync", lambda e, s=xi[sl], kk=kk, col=col: e.dma_start(
                    out=s, in_=x0T[kk * 128:(kk + 1) * 128, col:col + 512]), Sld, 16)
                CERT("sync")
                WT("vector", Slp, Slp.n)
                if kk == 0:
                    WT("vector", St_, v_bc)
                WT("vector", Sgp, Sgp.n)
                EM("vector", lambda e, s=xi[sl], q=sqp[sl]: e.tensor_tensor(
                    out=q, in0=s, in1=ps_c[:], op=ALU.subtract))
                v_nm = EM("vector", lambda e, q=sqp[sl], h=hst[sl]: e.tensor_tensor(
                    out=h, in0=q, in1=ps_d[:], op=ALU.mult), Sv, 1)
                WT("gpsimd", Sv, v_nm)
                v_hst_st[sl] = EM("gpsimd", lambda e, h=hst[sl], kk=kk, col=col: e.dma_start(
                    out=hsln_part.ap()[kk * 128:(kk + 1) * 128, col:col + 512], in_=h), Sg, 16)
                CERT("gpsimd")
            v_nm_last = Sv.n
        WT("gpsimd", Sg, Sg.n)
        v_ag_hsln = EM("gpsimd", lambda e: e.collective_compute(
            "AllGather", ALU.bypass, replica_groups=RG,
            ins=[hsln_part.ap()], outs=[hsln_all.ap()]), Scc, 1)
        WT("gpsimd", Sg, Sg.n)
        WT("gpsimd", Scc, Scc.n)
        v_ag_wout = EM("gpsimd", lambda e: e.collective_compute(
            "AllGather", ALU.bypass, replica_groups=RG,
            ins=[woutp_bf.ap()], outs=[wout_all.ap()]), Scc, 1)

        wnat = [abf(WNAT_O + j * 128, 128) for j in range(4)]
        wttile = [abf(WNAT_O + 512 + j * 128, 128) for j in range(4)]
        q3f = af32(0, 512)
        q3i = af[0:128, 512:1024].bitcast(I32)
        rngt = af[0:128, 1024:1152].bitcast(F16)
        scl = af32(1152, 128)
        mnn = af32(1280, 128)
        WT("sync", Sv, v_nm_last)
        WT("tensor", Sv, v_nm_last)
        v_wq_st = 0
        v_q3_rd = 0
        for ot in range(4):
            v_rng = EM("sync", lambda e, ot=ot: e.dma_start(
                out=rngt, in_=rngc[ot * 128:(ot + 1) * 128, :]), Sld, 16)
            CERT("sync")
            WT("vector", Slp, Slp.n)
            rv = rngt.rearrange("p (g two) -> p g two", two=2)
            EM("vector", lambda e, rv=rv: e.tensor_tensor(
                out=scl, in0=rv[:, :, 1], in1=rv[:, :, 0], op=ALU.subtract))
            v_sc = EM("vector", lambda e, rv=rv: e.tensor_copy(
                out=mnn, in_=rv[:, :, 0]), Sv, 1)
            WT("tensor", Sv, v_sc)
            if ot == 0:
                WT("tensor", Sgp, Sgp.n)
            v_t1 = EM("tensor", lambda e: e.transpose(
                ps_e[0:128, 0:128], scl, id_f32[:]), St_, 1)
            WT("vector", St_, v_t1)
            v_e1 = EM("vector", lambda e: e.tensor_copy(
                out=sclT[:], in_=ps_e[0:128, 0:128]), Sv, 1)
            WT("tensor", Sv, v_e1)
            v_t2 = EM("tensor", lambda e: e.transpose(
                ps_e[0:128, 0:128], mnn, id_f32[:]), St_, 1)
            WT("vector", St_, v_t2)
            v_e2 = EM("vector", lambda e: e.tensor_copy(
                out=mnnT[:], in_=ps_e[0:128, 0:128]), Sv, 1)
            for istr in range(8):
                WT("tensor", Sv, v_e2)
                if ot == 0 and istr == 0:
                    WT("tensor", Sgp, Sgp.n)
                WT("tensor", Sv, v_q3_rd)
                EM("tensor", lambda e, istr=istr: e.matmul(
                    ps_c[:], sclT[:], P0[:, istr * 512:(istr + 1) * 512],
                    start=True, stop=True))
                v_mm = EM("tensor", lambda e, istr=istr: e.matmul(
                    ps_d[:], mnnT[:], P0[:, istr * 512:(istr + 1) * 512],
                    start=True, stop=True), St_, 1)
                WT("sync", Sv, v_q3_rd)
                v_q3 = EM("sync", lambda e, ot=ot, istr=istr: e.dma_start(
                    out=q3i, in_=q3c[ot * 128:(ot + 1) * 128, istr * 512:(istr + 1) * 512]),
                    Sld, 16)
                CERT("sync")
                WT("vector", Slp, Slp.n)
                EM("vector", lambda e: e.tensor_scalar(
                    out=q3f, in0=q3i, scalar1=1.0 / 7.0, scalar2=None, op0=ALU.mult))
                WT("vector", St_, v_mm)
                EM("vector", lambda e: e.tensor_tensor(
                    out=q3f, in0=q3f, in1=ps_c[:], op=ALU.mult))
                WT("vector", Sgp, Sgp.n)
                v_wn = EM("vector", lambda e: e.tensor_tensor(
                    out=ab[0:128, WNAT_O:WNAT_O + 512], in0=q3f, in1=ps_d[:],
                    op=ALU.add), Sv, 1)
                v_q3_rd = v_wn
                WT("tensor", Sv, v_wn)
                for j in range(4):
                    v_tt = EM("tensor", lambda e, j=j: e.transpose(
                        ps_f[0:128, 0:64].bitcast(BF16), wnat[j], id_bf[:]), St_, 1)
                    WT("scalar", St_, v_tt)
                    v_ev = EM("scalar", lambda e, j=j: e.activation(
                        out=wttile[j], in_=ps_f[0:128, 0:64].bitcast(BF16), func=AF.Copy), Ss, 1)
                    WT("tensor", Ss, v_ev)
                    WT("gpsimd", Ss, v_ev)
                    v_wq_st = EM("gpsimd", lambda e, ot=ot, istr=istr, j=j: e.dma_start(
                        out=wffn_part.ap().rearrange(
                            "p (o k f) -> p o k f", o=4, k=32)[:, ot, istr * 4 + j, :],
                        in_=wttile[j]), Sg, 16)
                    CERT("gpsimd")
        WT("gpsimd", Sg, Sg.n)
        WT("gpsimd", Scc, Scc.n)
        v_ag_wffn = EM("gpsimd", lambda e: e.collective_compute(
            "AllGather", ALU.bypass, replica_groups=RG,
            ins=[wffn_part.ap()], outs=[wffn_all.ap()]), Scc, 1)
        v_deq_done = Sg.n

        WT("tensor", Sgp, Sgp.n)
        for psw in (ps_m0, ps_m1):
            for kk in range(4):
                EM("tensor", lambda e, psw=psw, kk=kk: e.matmul(
                    psw[0:128, 0:128], id_bf[:], id_bf[:],
                    start=(kk == 0), stop=(kk == 3)))
        hsv = hsln_all.ap().rearrange("(blk h) t -> blk h t", h=H)
        winx_v = winx_bf.ap().rearrange("(m p) f -> m p f", p=128)
        wing_v = wing_bf.ap().rearrange("(m p) f -> m p f", p=128)
        rhs_t = [abf(RHS_O + kk * 512, 512) for kk in range(32)]
        wm = [abf(WM_O, 4096), abf(WM_O + 4096, 4096)]
        xbf_pp = [[abf(XBF_O + d * 512, 512) for d in range(NDT)],
                  [abf(40960 + d * 512, 512) for d in range(NDT)]]
        gbf_pp = [[abf(GBF_O + d * 512, 512) for d in range(NDT)],
                  [abf(45056 + d * 512, 512) for d in range(NDT)]]
        tsn = [abf(TSN_O, 512), abf(TSN_O + 512, 512)]
        yst = [abf(YST_O, 512), abf(YST_O + 512, 512)]
        bN = ab[0:16, BN_O:BN_O + 512]
        cN = ab[0:16, BN_O + 512:BN_O + 1024]
        dt_t = [af32(DT_O + d * 512, 512) for d in range(NDT)]
        dtu_t = [af32(DTU_O + d * 512, 512) for d in range(NDT)]
        ya_t = [af32(YA_O + d * 512, 512) for d in range(NDT)]
        tb_t = [af32(TB_O, 512), af32(TB_O + 512, 512)]
        dAb_t = [abf(DQ_O + j * 512, 512) for j in range(4)]
        tbb_t = [abf(DQ_O + 2048, 512), abf(DQ_O + 2560, 512)]
        tsb_t = [abf(DQ_O + 3072, 512), abf(DQ_O + 3584, 512)]
        tcb_t = [abf(HST_O, 512), abf(HST_O + 512, 512)]
        dub_t = [af[0:128, DTU_O:DTU_O + 2048].bitcast(BF16)[:, d * 512:(d + 1) * 512]
                 for d in range(NDT)]
        bcb_t = [af[0:128, DA_O:DA_O + 1024].bitcast(BF16)[:, j * 512:(j + 1) * 512]
                 for j in range(4)]
        ts0 = af32(T0_O, 512)
        ts1 = af32(T0_O + 512, 512)
        accv = [af32(ACC_O, 512), af32(ACC_O + 512, 512)]
        xcat = [af32(XCAT_O, 515), af32(XCAT_O + 515, 515)]
        stg = [af32(STG_O, 512), af32(STG_O + 512, 512), af32(STG_O + 1024, 512, 32)]
        braw = af[0:16, BCR_O:BCR_O + 512]
        craw = af[0:16, BCR_O + 512:BCR_O + 1024]
        bsq = af[0:16, 5120:5632]
        csq = af[0:16, 5632:6144]

        last_rhs_use = [0] * 32
        v_wm_free = [0, 0]
        v_ssm_store = [0, 0, 0]
        v_dt = [0] * NDT
        v_yst = [0, 0]
        v_scan_use = {}
        v_bc_read = [0, 0]

        v_dtu_prev = 0
        v_evac_prev = [0, 0]
        v_ssm_ev = [0, 0, 0]
        v_ar_of = {}
        v_conv_prev = [0, 0]

        def emit_A(k):
            nonlocal v_evac_prev, v_conv_prev
            blk, half = k // 2, k % 2
            col = half * 512
            xbf = xbf_pp[k % 2]
            gbf = gbf_pp[k % 2]
            if k in (0, 8):
                WT("scalar", Sv, Sv.n)
                EM("scalar", lambda e: e.memzero(halo[:]))

            if k == 0:
                WT("sync", Scc, v_ag_hsln)
                WT("sync", Sgp, Sgp.n)
            v_rhs = [0] * 32
            for kk in range(32):
                WT("sync", St_, last_rhs_use[kk])
                v_rhs[kk] = EM("sync", lambda e, kk=kk, blk=blk, col=col: e.dma_start(
                    out=rhs_t[kk], in_=hsv[blk, kk * 128:(kk + 1) * 128, col:col + 512]),
                    Sld, 16)
            CERT("sync")

            if k == 0:
                WT("sync", Sgp, Sgp.n)
                WT("sync", Sgp, Sgp.n)
            v_mm_done = [0] * 16
            v_evac = [0] * 16
            v_psfree = [0] * 16
            v_silu = [0] * 16
            v_conv_done = [0] * 8
            for m in range(16):
                sl = m % 2
                wv = winx_v if m < 8 else wing_v
                mi = m if m < 8 else m - 8
                WT("sync", St_, v_wm_free[sl])
                v_wm = EM("sync", lambda e, wv=wv, mi=mi, sl=sl: e.dma_start(
                    out=wm[sl], in_=wv[mi]), Sld, 16)
                CERT("sync")
                psm = ps_m0 if sl == 0 else ps_m1
                WT("tensor", Slp, Slp.n)
                if m >= 2:
                    WT("tensor", Ss, v_psfree[m - 2])
                else:
                    WT("tensor", Ss, v_evac_prev[m])
                for kk in range(32):
                    if m == 0:
                        WT("tensor", Slp, Slp.n)
                    last = (kk == 31)
                    v = EM("tensor", lambda e, psm=psm, sl=sl, kk=kk: e.matmul(
                        psm[:], ab[0:128, WM_O + sl * 4096 + kk * 128:WM_O + sl * 4096 + (kk + 1) * 128],
                        rhs_t[kk], start=(kk == 0), stop=(kk == 31)),
                        St_ if last else None, 1)
                    if last:
                        v_mm_done[m] = v
                v_wm_free[sl] = v_mm_done[m]
                if debug and k == 0 and m == 0:
                    WT("gpsimd", St_, v_mm_done[0])
                    for kk in range(32):
                        EM("gpsimd", lambda e, kk=kk: e.dma_start(
                            out=dbgr.ap()[kk * 128:(kk + 1) * 128, :], in_=rhs_t[kk]), Sg, 16)
                    EM("gpsimd", lambda e: e.dma_start(
                        out=dbgw.ap(), in_=wm[0]), Sg, 16)
                WT("scalar", St_, v_mm_done[m])
                if m < 8:
                    xc = xcat[m % 2]
                    if m >= 2:
                        WT("scalar", Sv, v_conv_done[m - 2])
                    else:
                        WT("scalar", Sv, max(v_conv_prev[m], v_dtu_prev if m == 0 else 0))
                    v_evac[m] = EM("scalar", lambda e, psm=psm, xc=xc: e.activation(
                        out=xc[:, 3:515], in_=psm[:], func=AF.Copy), Ss, 1)
                    v_psfree[m] = v_evac[m]
                    if debug and k < 2:
                        WT("gpsimd", Ss, v_evac[m])
                        EM("gpsimd", lambda e, xc=xc, m=m, k=k: e.dma_start(
                            out=dbgp.ap()[k, m * 128:(m + 1) * 128, :], in_=xc[:, 3:515]),
                            Sg, 16)
                        CERT("gpsimd")
                        WT("scalar", Sgp, Sgp.n)
                    WT("vector", Ss, v_evac[m])
                    if m >= 2:
                        WT("vector", Ss, v_silu[m - 2])
                    EM("vector", lambda e, xc=xc, m=m: e.tensor_copy(
                        out=xc[:, 0:3], in_=halo[:, m * 3:(m + 1) * 3]))
                    av = accv[m % 2]
                    EM("vector", lambda e, xc=xc, av=av, m=m: e.tensor_scalar(
                        out=av, in0=xc[:, 3:515],
                        scalar1=convw_sb[:, m * 4 + 3:m * 4 + 4],
                        scalar2=convb_sb[:, m:m + 1], op0=ALU.mult, op1=ALU.add))
                    for j in range(3):
                        EM("vector", lambda e, xc=xc, av=av, m=m, j=j: e.scalar_tensor_tensor(
                            out=av, in0=xc[:, j:j + 512],
                            scalar=convw_sb[:, m * 4 + j:m * 4 + j + 1],
                            in1=av, op0=ALU.mult, op1=ALU.add))
                    v_conv_done[m] = EM("vector", lambda e, xc=xc, m=m: e.tensor_copy(
                        out=halo[:, m * 3:(m + 1) * 3], in_=xc[:, 512:515]), Sv, 1)
                    WT("scalar", Sv, v_conv_done[m])
                    v_silu[m] = EM("scalar", lambda e, av=av, m=m: e.activation(
                        out=xbf[m], in_=av, func=AF.Silu), Ss, 1)
                    v_evac[m] = v_silu[m]
                else:
                    dtl = m - 8
                    v_evac[m] = EM("scalar", lambda e, psm=psm, dtl=dtl: e.activation(
                        out=gbf[dtl], in_=psm[:], func=AF.Silu), Ss, 1)
                    v_psfree[m] = v_evac[m]
                    v_silu[m] = v_evac[m]
            for kk in range(32):
                last_rhs_use[kk] = v_mm_done[15]
            v_evac_prev = [v_psfree[14], v_psfree[15]]
            v_conv_prev = [v_conv_done[6], v_conv_done[7]]
            v_x_ready = Ss.n
            if debug:
                WT("gpsimd", Ss, v_x_ready)
                for d in range(NDT):
                    EM("gpsimd", lambda e, d=d, k=k: e.dma_start(
                        out=dbgu.ap()[k, d * 128:(d + 1) * 128, :], in_=xbf[d]), Sg, 16)
                    EM("gpsimd", lambda e, d=d, k=k: e.dma_start(
                        out=dbgg.ap()[k, d * 128:(d + 1) * 128, :], in_=gbf[d]), Sg, 16)

            if k == 0:
                WT("tensor", Sgp, Sgp.n)
            psl = [ps_a, ps_b, ps_c]
            v_wx = [0, 0, 0]
            for ms in range(3):
                msz = 128 if ms < 2 else 32
                for kk in range(NDT):
                    if kk == 0:
                        WT("tensor", Ss, v_x_ready)
                        WT("tensor", Ss, v_ssm_ev[ms])
                    last = (kk == NDT - 1)
                    v = EM("tensor", lambda e, ms=ms, msz=msz, kk=kk: e.matmul(
                        psl[ms][0:msz, :],
                        wxT_sb[:, kk * 288 + ms * 128:kk * 288 + ms * 128 + msz],
                        xbf[kk], start=(kk == 0), stop=(kk == NDT - 1)),
                        St_ if last else None, 1)
                    if last:
                        v_wx[ms] = v
            for ms in range(3):
                msz = 128 if ms < 2 else 32
                WT("scalar", St_, v_wx[ms])
                WT("scalar", Sgp, Sgp.n)
                v_ssm_ev[ms] = EM("scalar", lambda e, ms=ms, msz=msz: e.activation(
                    out=stg[ms] if ms == 2 else stg[ms][0:msz, :],
                    in_=psl[ms][0:msz, :], func=AF.Copy), Ss, 1)
                WT("gpsimd", Ss, v_ssm_ev[ms])
                v_ssm_store[ms] = EM("gpsimd", lambda e, ms=ms, msz=msz, k=k: e.dma_start(
                    out=ssmpart.ap()[k, ms * 128:ms * 128 + msz, :],
                    in_=stg[ms] if ms == 2 else stg[ms][0:msz, :]), Sg, 16)
            CERT("gpsimd")
            WT("gpsimd", Sg, Sg.n)
            WT("gpsimd", Scc, Scc.n)
            v_ar_of[k] = EM("gpsimd", lambda e, k=k: e.collective_compute(
                "AllReduce", ALU.add, replica_groups=RG,
                ins=[ssmpart.ap()[k]], outs=[ssmred.ap()[k]]), Scc, 1)

        def emit_C(k):
            nonlocal v_dtu_prev
            blk, half = k // 2, k % 2
            col = half * 512
            xbf = xbf_pp[k % 2]
            gbf = gbf_pp[k % 2]
            if k in (0, 8):
                WT("scalar", Sv, Sv.n)
                EM("scalar", lambda e: e.memzero(carry[:]))
            WT("sync", Scc, v_ar_of[k])
            EM("sync", lambda e, k=k: e.dma_start(out=ts0, in_=ssmred.ap()[k, 0:128, :]), Sld, 16)
            v_l1 = EM("sync", lambda e, k=k: e.dma_start(out=ts1, in_=ssmred.ap()[k, 128:256, :]), Sld, 16)
            EM("sync", lambda e, k=k: e.dma_start(out=braw, in_=ssmred.ap()[k, 256:272, :]), Sld, 16)
            v_l3 = EM("sync", lambda e, k=k: e.dma_start(out=craw, in_=ssmred.ap()[k, 272:288, :]), Sld, 16)
            CERT("sync")

            WT("scalar", Slp, Slp.n)
            EM("scalar", lambda e: e.activation(out=sqp[0], in_=ts0, func=AF.Square))
            EM("scalar", lambda e: e.activation(out=sqp[1], in_=ts1, func=AF.Square))
            WT("scalar", Slp, Slp.n)
            EM("scalar", lambda e: e.activation(out=bsq, in_=braw, func=AF.Square))
            v_sqbc = EM("scalar", lambda e: e.activation(out=csq, in_=craw, func=AF.Square), Ss, 1)
            WT("tensor", Ss, v_sqbc)
            EM("tensor", lambda e: e.matmul(ps_d[0:1, :], ones128[:], sqp[0], start=True, stop=False))
            EM("tensor", lambda e: e.matmul(ps_d[0:1, :], ones128[:], sqp[1], start=False, stop=True))
            EM("tensor", lambda e: e.matmul(ps_e[0:1, :], ones16[:], bsq, start=True, stop=True))
            v_strms = EM("tensor", lambda e: e.matmul(ps_f[0:1, :], ones16[:], csq, start=True, stop=True), St_, 1)
            WT("scalar", St_, v_strms)
            EM("scalar", lambda e: e.activation(out=rowd[:], in_=ps_d[0:1, :], func=AF.Ln, scale=1.0 / RDT, bias=epsc[0:1, 1:2]))
            EM("scalar", lambda e: e.activation(out=rowa[:], in_=rowd[:], func=AF.Exp, scale=-0.5))
            EM("scalar", lambda e: e.activation(out=rowd[:], in_=ps_e[0:1, :], func=AF.Ln, scale=1.0 / NST, bias=epsc[0:1, 1:2]))
            EM("scalar", lambda e: e.activation(out=rowb[:], in_=rowd[:], func=AF.Exp, scale=-0.5))
            EM("scalar", lambda e: e.activation(out=rowd[:], in_=ps_f[0:1, :], func=AF.Ln, scale=1.0 / NST, bias=epsc[0:1, 1:2]))
            v_rst = EM("scalar", lambda e: e.activation(out=rowc[:], in_=rowd[:], func=AF.Exp, scale=-0.5), Ss, 1)
            WT("tensor", Ss, v_rst)
            v_tb = EM("tensor", lambda e: e.matmul(
                ps_d[:], ones1r[:], rowa[:], start=True, stop=True), St_, 1)
            WT("vector", St_, v_tb)
            EM("vector", lambda e: e.tensor_tensor(out=tsn[0], in0=ts0, in1=ps_d[:], op=ALU.mult))
            v_tsn = EM("vector", lambda e: e.tensor_tensor(out=tsn[1], in0=ts1, in1=ps_d[:], op=ALU.mult), Sv, 1)
            EM("tensor", lambda e: e.matmul(
                ps_e[0:16, :], ones16r[:], rowb[:], start=True, stop=True))
            v_cbm = EM("tensor", lambda e: e.matmul(
                ps_f[0:16, :], ones16r[:], rowc[:], start=True, stop=True), St_, 1)
            WT("vector", St_, v_cbm)
            EM("vector", lambda e: e.tensor_tensor(out=bN, in0=braw, in1=ps_e[0:16, :], op=ALU.mult))
            v_bcn = EM("vector", lambda e: e.tensor_tensor(out=cN, in0=craw, in1=ps_f[0:16, :], op=ALU.mult), Sv, 1)

            if k == 0:
                WT("tensor", Sgp, Sgp.n)
            WT("scalar", Sgp, Sgp.n)
            v_dtadd = [0] * NDT
            for d in range(NDT):
                psd = ps_a if (d % 2 == 0) else ps_b
                if d == 0:
                    WT("tensor", Sv, v_tsn)
                if d >= 2:
                    WT("tensor", Ss, v_dt[d - 2])
                for rr in range(2):
                    last = (rr == 1)
                    v = EM("tensor", lambda e, psd=psd, d=d, rr=rr: e.matmul(
                        psd[:], wdtT_sb[:, rr * 1024 + d * 128:rr * 1024 + (d + 1) * 128],
                        tsn[rr], start=(rr == 0), stop=(rr == 1)),
                        St_ if last else None, 1)
                WT("scalar", St_, v)
                if d >= 2:
                    WT("scalar", Sv, v_dtadd[d - 2])
                else:
                    WT("scalar", Sv, v_dtu_prev)
                sa = af32(SAB_O + (d % 2) * 1024, 512)
                sb = af32(SAB_O + (d % 2) * 1024 + 512, 512)
                EM("scalar", lambda e, psd=psd, d=d, sa=sa: e.activation(
                    out=sa, in_=psd[:], func=AF.Abs, bias=bdt_sb[:, d:d + 1]))
                EM("scalar", lambda e, sa=sa, sb=sb: e.activation(
                    out=sb, in_=sa, func=AF.Exp, scale=-1.0))
                EM("scalar", lambda e, sa=sa, sb=sb: e.activation(
                    out=sa, in_=sb, func=AF.Ln, bias=one128[:]))
                v_dt[d] = EM("scalar", lambda e, psd=psd, d=d, sb=sb: e.activation(
                    out=sb, in_=psd[:], func=AF.Relu, bias=bdt_sb[:, d:d + 1]), Ss, 1)
                WT("vector", Ss, v_dt[d])
                EM("vector", lambda e, d=d, sa=sa, sb=sb: e.tensor_tensor(
                    out=dt_t[d], in0=sa, in1=sb, op=ALU.add))
                v_dtadd[d] = EM("vector", lambda e, d=d, xb=xbf[d]: e.tensor_tensor(
                    out=dub_t[d], in0=dt_t[d], in1=xb, op=ALU.mult), Sv, 1)
                if d == NDT - 1:
                    v_dtu_prev = v_dtadd[d]

            v_bc_ev = [0, 0]
            for n in range(NST):
                par = n % 2
                psB = ps_c if par == 0 else ps_e
                psC = ps_d if par == 0 else ps_f
                bcB = bcb_t[2 * par]
                bcC = bcb_t[2 * par + 1]
                if n == 0:
                    WT("tensor", Sv, v_bcn)
                else:
                    WT("tensor", Sv, v_bc_read[par])
                EM("tensor", lambda e, n=n, psB=psB: e.matmul(
                    psB[:], sel_b[:, n * 128:(n + 1) * 128], bN, start=True, stop=True))
                v_bcast = EM("tensor", lambda e, n=n, psC=psC: e.matmul(
                    psC[:], sel_b[:, n * 128:(n + 1) * 128], cN, start=True, stop=True), St_, 1)
                WT("scalar", St_, v_bcast)
                if n >= 2:
                    WT("scalar", Sv, v_bc_read[par])
                else:
                    WT("scalar", Sv, v_bc_read[par])
                EM("scalar", lambda e, psB=psB, bcB=bcB: e.activation(
                    out=bcB, in_=psB[:], func=AF.Copy))
                v_bc_ev[par] = EM("scalar", lambda e, psC=psC, bcC=bcC: e.activation(
                    out=bcC, in_=psC[:], func=AF.Copy), Ss, 1)
                WT("vector", Ss, v_bc_ev[par])
                for d in range(NDT):
                    g = n * NDT + d
                    j = g % 4
                    pr = g % 2
                    if g >= 4:
                        WT("scalar", Sv, v_scan_use.get((k, g - 4), 0))
                    else:
                        WT("scalar", Sv, v_dtadd[NDT - 1])
                    v_da = EM("scalar", lambda e, d=d, n=n, j=j: e.activation(
                        out=dAb_t[j], in_=dt_t[d], func=AF.Exp,
                        scale=A_sb[:, d * 16 + n:d * 16 + n + 1]), Ss, 1)
                    EM("vector", lambda e, d=d, pr=pr, bcB=bcB: e.tensor_tensor(
                        out=tbb_t[pr], in0=dub_t[d], in1=bcB, op=ALU.mult))
                    WT("vector", Ss, v_da)
                    cidx = d * 16 + n
                    EM("vector", lambda e, j=j, pr=pr, cidx=cidx: e.tensor_tensor_scan(
                        out=tsb_t[pr], data0=dAb_t[j], data1=tbb_t[pr],
                        initial=carry[:, cidx:cidx + 1], op0=ALU.mult, op1=ALU.add))
                    v_cy = EM("vector", lambda e, pr=pr, bcC=bcC: e.tensor_tensor(
                        out=tcb_t[pr], in0=tsb_t[pr], in1=bcC, op=ALU.mult), Sv, 1)
                    v_scan_use[(k, g)] = v_cy
                    if n == 0:
                        EM("vector", lambda e, d=d, pr=pr: e.tensor_copy(
                            out=ya_t[d], in_=tcb_t[pr]))
                    else:
                        EM("vector", lambda e, d=d, pr=pr: e.tensor_tensor(
                            out=ya_t[d], in0=ya_t[d], in1=tcb_t[pr], op=ALU.add))
                    EM("vector", lambda e, pr=pr, cidx=cidx: e.tensor_copy(
                        out=carry[:, cidx:cidx + 1], in_=tsb_t[pr][:, 511:512]))
                    if d == NDT - 1:
                        v_bc_read[par] = v_cy

            for d in range(NDT):
                pr = d % 2
                EM("vector", lambda e, d=d, pr=pr: e.tensor_scalar(
                    out=tb_t[pr], in0=xbf[d], scalar1=dvec_sb[:, d:d + 1],
                    scalar2=None, op0=ALU.mult))
                EM("vector", lambda e, d=d, pr=pr: e.tensor_tensor(
                    out=ya_t[d], in0=ya_t[d], in1=tb_t[pr], op=ALU.add))
                WT("vector", Sgp, Sgp.n)
                v_y3 = EM("vector", lambda e, d=d, pr=pr: e.tensor_tensor(
                    out=yst[pr], in0=ya_t[d], in1=gbf[d], op=ALU.mult), Sv, 1)
                WT("gpsimd", Sv, v_y3)
                ybf_t = ybf_e if k % 2 == 0 else ybf_o
                v_yst[pr] = EM("gpsimd", lambda e, d=d, k=k, pr=pr, yb=ybf_t: e.dma_start(
                    out=yb.ap()[k // 2, d * 128:(d + 1) * 128, :], in_=yst[pr]), Sg, 16)
                CERT("gpsimd")
        emit_A(0)
        for k in range(1, NCHUNK):
            emit_A(k)
            emit_C(k - 1)
            if k - 1 == 14:
                WT("gpsimd", Sg, Sg.n)
                WT("gpsimd", Scc, Scc.n)
                v_agy_e = EM("gpsimd", lambda e: e.collective_compute(
                    "AllGather", ALU.bypass, replica_groups=RG,
                    ins=[ybf_e.ap()], outs=[yag_e.ap()]), Scc, 1)
        emit_C(NCHUNK - 1)
        WT("gpsimd", Sg, Sg.n)
        WT("gpsimd", Scc, Scc.n)
        v_agy_o = EM("gpsimd", lambda e: e.collective_compute(
            "AllGather", ALU.bypass, replica_groups=RG,
            ins=[ybf_o.ap()], outs=[yag_o.ap()]), Scc, 1)

        y_t = [abf(Y_O + kk * 512, 512) for kk in range(64)]
        wo_sl = [abf(WOUT_O, 4096), abf(WOUT_O + 4096, 4096)]
        xln_t = [abf(XLN_O + kk * 512, 512) for kk in range(32)]
        wf_sl = [abf(WFF_O, 4096), abf(WFF_O + 4096, 4096)]
        x1p = [af32(X1_O, 512), af32(X1_O + 512, 512)]
        scr = af32(SCR_O, 512)
        x0p = [af32(XI_O, 512), af32(XI_O + 512, 512)]
        otp = [af32(ACC_O, 512), af32(ACC_O + 512, 512)]
        yagv_e = yag_e.ap().rearrange("(r k) ch t -> r k ch t", k=8)
        yagv_o = yag_o.ap().rearrange("(r k) ch t -> r k ch t", k=8)
        wov = wout_all.ap().rearrange("(r p) (m k f) -> r p m k f", p=128, m=32, k=8)
        wfv = wffn_all.ap().rearrange("(r p) (o k f) -> r p o k f", p=128, o=4, k=32)


        v_wo_free = [0, 0]
        v_wf_free = [0, 0]
        v_ost = [0, 0]
        v_x1s_arr = [0] * 32
        v_sq_arr = [0] * 32
        for sub in range(2):
            col = sub * 512
            v_y = [0] * 8
            WT("sync", Scc, v_agy_e if sub == 0 else v_agy_o)
            if sub == 1:
                WT("sync", St_, St_.n)
            for g in range(8):
                yv = yagv_e if sub == 0 else yagv_o
                def _yld(e, g=g, yv=yv):
                    return e.dma_start(
                        out=ab[0:128, Y_O + g * 4096:Y_O + (g + 1) * 4096]
                        .rearrange("p (kk t) -> p kk t", t=512),
                        in_=yv[g, ds(hold["rank"], 1), :, :]
                        .squeeze(0).rearrange("(kk p) t -> p kk t", p=128))
                v_y[g] = EM("sync", _yld, Sld, 16)
            CERT("sync")
            if sub == 0:
                WT("sync", Sv, Sv.n)
                WT("sync", Ss, Ss.n)
                WT("sync", Sgp, Sgp.n)
            v_amm = [0] * 32
            v_ev = [0] * 32
            v_half_done = [0, 0]
            for m in range(32):
                sl = m % 2
                v_wo = [0, 0]
                for hf2 in range(2):
                    WT("sync", St_, v_half_done[hf2])
                    v_wo[hf2] = EM("sync", lambda e, m=m, hf2=hf2: e.dma_start(
                        out=ab[0:128, WOUT_O + hf2 * 4096:WOUT_O + (hf2 + 1) * 4096]
                        .rearrange("p (r q) -> p r q", q=1024),
                        in_=wov[4 * hf2:4 * (hf2 + 1), :, m].rearrange("r p k f -> p r (k f)")),
                        Sld, 16)
                if m >= 2:
                    WT("sync", Sv, v_ev[m - 2])
                v_x0 = EM("sync", lambda e, m=m, col=col, sl=sl: e.dma_start(
                    out=x0p[sl], in_=x0T[m * 128:(m + 1) * 128, col:col + 512]), Sld, 16)
                CERT("sync")
                psm = ps_m0 if sl == 0 else ps_m1
                if m >= 2:
                    WT("tensor", Sv, v_ev[m - 2])
                for kk in range(64):
                    if kk == 0:
                        WT("tensor", Slp, Slp.n)
                    if kk == 32:
                        WT("tensor", Slp, Slp.n)
                    if m == 0 and kk % 8 == 0:
                        WT("tensor", Slp, Slp.n)
                    last = (kk == 63)
                    v = EM("tensor", lambda e, psm=psm, kk=kk: e.matmul(
                        psm[:], ab[0:128, WOUT_O + (kk // 32) * 4096 + (kk % 32) * 128:WOUT_O + (kk // 32) * 4096 + (kk % 32) * 128 + 128],
                        y_t[kk], start=(kk == 0), stop=(kk == 63)),
                        St_ if (last or kk == 31) else None, 1)
                    if kk == 31:
                        v_half_done[0] = v
                    if last:
                        v_amm[m] = v
                v_half_done[1] = v_amm[m]
                WT("vector", St_, v_amm[m])
                WT("vector", Slp, Slp.n)
                pr = m % 2
                if m >= 2:
                    WT("vector", Ss, v_sq_arr[m - 2])
                    WT("vector", Sgp, Sgp.n)
                v_x1 = EM("vector", lambda e, psm=psm, pr=pr: e.tensor_tensor(
                    out=x1p[pr], in0=x0p[pr], in1=psm[:], op=ALU.add), Sv, 1)
                v_ev[m] = v_x1
                WT("scalar", Sv, v_x1)
                v_sq_arr[m] = EM("scalar", lambda e, pr=pr: e.activation(
                    out=scr, in_=x1p[pr], func=AF.Square), Ss, 1)
                WT("tensor", Sv, v_x1)
                WT("tensor", Ss, v_sq_arr[m])
                EM("tensor", lambda e, pr=pr, m=m: e.matmul(
                    ps_d[0:1, :], ones128[:], x1p[pr], start=(m == 0), stop=(m == 31),
                    skip_group_check=True))
                v = EM("tensor", lambda e, m=m: e.matmul(
                    ps_e[0:1, :], ones128[:], scr, start=(m == 0), stop=(m == 31),
                    skip_group_check=True), St_, 1)
                if m == 31:
                    v_ln2stats = v
                WT("gpsimd", Sv, v_x1)
                v_x1s_arr[m] = EM("gpsimd", lambda e, m=m, sub=sub, pr=pr: e.dma_start(
                    out=x1d.ap()[sub, m * 128:(m + 1) * 128, :], in_=x1p[pr]), Sg, 16)
                CERT("gpsimd")

            WT("scalar", St_, v_ln2stats)
            EM("scalar", lambda e: e.activation(out=rowa[:], in_=ps_d[0:1, :], func=AF.Copy, scale=1.0 / H))
            v_m2 = EM("scalar", lambda e: e.activation(out=rowd[:], in_=rowa[:], func=AF.Square), Ss, 1)
            WT("vector", Ss, v_m2)
            v_v2 = EM("vector", lambda e: e.scalar_tensor_tensor(
                out=rowe[:], in0=ps_e[0:1, :], scalar=1.0 / H,
                in1=rowd[:], op0=ALU.mult, op1=ALU.subtract), Sv, 1)
            WT("scalar", Sv, v_v2)
            EM("scalar", lambda e: e.activation(out=rowd[:], in_=rowe[:], func=AF.Ln, bias=epsc[0:1, 0:1]))
            v_r2 = EM("scalar", lambda e: e.activation(out=rowb[:], in_=rowd[:], func=AF.Exp, scale=-0.5), Ss, 1)
            WT("tensor", Ss, v_r2)
            EM("tensor", lambda e: e.matmul(ps_c[:], ones1r[:], rowa[:], start=True, stop=True))
            v_bc2 = EM("tensor", lambda e: e.matmul(ps_d[:], ones1r[:], rowb[:], start=True, stop=True), St_, 1)

            v_xln = [0] * 32
            WT("sync", Sgp, Sgp.n)
            WT("sync", Ss, v_sq_arr[31])
            for kk in range(32):
                pr = kk % 2
                if kk >= 2:
                    WT("sync", Sv, v_xln[kk - 2])
                v_lx = EM("sync", lambda e, kk=kk, sub=sub, pr=pr: e.dma_start(
                    out=x1p[pr], in_=x1d.ap()[sub, kk * 128:(kk + 1) * 128, :]), Sld, 16)
                CERT("sync")
                WT("vector", Slp, Slp.n)
                if kk == 0:
                    WT("vector", St_, v_bc2)
                EM("vector", lambda e, pr=pr: e.tensor_tensor(
                    out=scr, in0=x1p[pr], in1=ps_c[:], op=ALU.subtract))
                v_xln[kk] = EM("vector", lambda e, kk=kk: e.tensor_tensor(
                    out=xln_t[kk], in0=scr, in1=ps_d[:], op=ALU.mult), Sv, 1)

            if sub == 0:
                WT("sync", Scc, v_ag_wffn)
                WT("sync", Sgp, Sgp.n)
            v_fmm = [0] * 32
            v_fev = [0] * 32
            for m in range(32):
                sl = m % 2
                if m == 0:
                    WT("sync", St_, v_amm[31])
                WT("sync", St_, v_wf_free[sl])
                v_wf = EM("sync", lambda e, m=m, sl=sl: e.dma_start(
                    out=wf_sl[sl],
                    in_=wfv[m // 4, :, m % 4].rearrange("p k f -> p (k f)")), Sld, 16)
                if m >= 2:
                    WT("sync", Sv, v_fev[m - 2])
                else:
                    WT("sync", Sv, v_xln[31])
                v_x1r = EM("sync", lambda e, m=m, sub=sub, sl=sl: e.dma_start(
                    out=x1p[sl], in_=x1d.ap()[sub, m * 128:(m + 1) * 128, :]), Sld, 16)
                CERT("sync")
                psm = ps_m0 if sl == 0 else ps_m1
                WT("tensor", Slp, Slp.n)
                if m >= 2:
                    WT("tensor", Sv, v_fev[m - 2])
                for kk in range(32):
                    if m == 0:
                        WT("tensor", Sv, v_xln[kk])
                    last = (kk == 31)
                    v = EM("tensor", lambda e, psm=psm, sl=sl, kk=kk: e.matmul(
                        psm[:], ab[0:128, WFF_O + sl * 4096 + kk * 128:WFF_O + sl * 4096 + (kk + 1) * 128],
                        xln_t[kk], start=(kk == 0), stop=(kk == 31)),
                        St_ if last else None, 1)
                    if last:
                        v_fmm[m] = v
                v_wf_free[sl] = v_fmm[m]
                WT("vector", St_, v_fmm[m])
                WT("vector", Slp, Slp.n)
                pr = m % 2
                WT("vector", Sgp, Sgp.n)
                v_x2 = EM("vector", lambda e, psm=psm, pr=pr, m=m: e.scalar_tensor_tensor(
                    out=otp[pr], in0=psm[:], scalar=biasc_sb[:, m:m + 1],
                    in1=x1p[pr], op0=ALU.add, op1=ALU.add), Sv, 1)
                v_fev[m] = v_x2
                WT("gpsimd", Sv, v_x2)
                v_ost[pr] = EM("gpsimd", lambda e, m=m, col=col, pr=pr: e.dma_start(
                    out=oT[m * 128:(m + 1) * 128, col:col + 512], in_=otp[pr]), Sg, 16)
                CERT("gpsimd")

        if debug:
            dbg = {
                "dbg_hsln": (hsln_all, [8 * H, 1024], BF16),
                "dbg_ssm": (ssmred, [NCHUNK, 288, CT], F32),
                "dbg_ssmp": (ssmpart, [NCHUNK, 288, CT], F32),
                "dbg_u": (dbgu, [NCHUNK, 1024, CT], BF16),
                "dbg_p": (dbgp, [2, 1024, CT], F32),
                "dbg_r": (dbgr, [4096, CT], BF16),
                "dbg_w": (dbgw, [128, 4096], BF16),
                "dbg_g": (dbgg, [NCHUNK, 1024, CT], BF16),
                "dbg_ye": (ybf_e, [8, 1024, CT], BF16),
                "dbg_yage": (yag_e, [64, 1024, CT], BF16),
                "dbg_yo": (ybf_o, [8, 1024, CT], BF16),
                "dbg_x1": (x1d, [2, H, CT], F32),
                "dbg_wffn": (wffn_all, [8 * 128, 16384], BF16),
                "dbg_wout": (wout_all, [8 * 128, 32768], BF16),
            }
            for name, (t, shp, dt_) in dbg.items():
                o = nc.dram_tensor(name, shp, dt_, kind="ExternalOutput").ap()
                EM("gpsimd", lambda e, o=o, t=t: e.dma_start(out=o, in_=t.ap()), Sg, 16)
        WT("gpsimd", Sg, Sg.n)

        block = es.enter_context(nc.Block())
        block.sync(lambda eng: [f(eng) for f in prog["sync"]])
        block.vector(lambda eng: [f(eng) for f in prog["vector"]])
        block.scalar(lambda eng: [f(eng) for f in prog["scalar"]])
        block.tensor(lambda eng: [f(eng) for f in prog["tensor"]])
        block.gpsimd(lambda eng: [f(eng) for f in prog["gpsimd"]])

    return nc


_NC_CACHE = {}


def _host_layout(inputs):
    x = np.asarray(inputs["L0_attn_hidden_states"])
    W_in = np.asarray(inputs["L0_attn_W_in"])
    conv_w = np.asarray(inputs["L0_attn_conv_w"])
    conv_b = np.asarray(inputs["L0_attn_conv_b"])
    W_x = np.asarray(inputs["L0_attn_W_x"])
    W_dt = np.asarray(inputs["L0_attn_W_dt"])
    b_dt = np.asarray(inputs["L0_attn_b_dt"])
    A_log = np.asarray(inputs["L0_attn_A_log"])
    Dv = np.asarray(inputs["L0_attn_D"])
    W_out = np.asarray(inputs["L0_attn_W_out"])
    q3 = np.asarray(inputs["L0_ffn_weight_q3"])
    rng = np.asarray(inputs["L0_ffn_weight_ranges"])
    bias = np.asarray(inputs["L0_ffn_bias"])

    xf = x.reshape(8192, 4096)
    q3_2d = q3.reshape(4096, 4096)
    in_maps = []
    for c in range(NCORE):
        cs, ce = c * 1024, (c + 1) * 1024
        d = {}
        d["x0T"] = np.ascontiguousarray(xf[cs:ce].T)
        d["winx"] = np.ascontiguousarray(
            W_in[cs:ce].reshape(8, 128, 32, 128).transpose(0, 3, 2, 1)).reshape(1024, 4096)
        d["wing"] = np.ascontiguousarray(
            W_in[DIN + cs:DIN + ce].reshape(8, 128, 32, 128).transpose(0, 3, 2, 1)).reshape(1024, 4096)
        d["wxT"] = np.ascontiguousarray(
            W_x[:, cs:ce].T.reshape(8, 128, 288).transpose(1, 0, 2)).reshape(128, 8 * 288)
        d["wdtT"] = np.ascontiguousarray(
            W_dt[cs:ce].T.reshape(2, 128, 1024).transpose(1, 0, 2)).reshape(128, 2048)
        d["wout"] = np.ascontiguousarray(
            W_out[:, cs:ce].reshape(32, 128, 8, 128).transpose(3, 0, 2, 1)).reshape(128, 32768)
        d["q3c"] = np.ascontiguousarray(q3_2d[c * 512:(c + 1) * 512])
        d["rngc"] = np.ascontiguousarray(rng[c * 65536:(c + 1) * 65536].reshape(512, 256))
        d["biasc"] = np.ascontiguousarray(bias.reshape(32, 128).T)
        d["alog"] = np.ascontiguousarray(
            A_log[cs:ce].reshape(8, 128, 16).transpose(1, 0, 2)).reshape(128, 128)
        d["convw"] = np.ascontiguousarray(
            conv_w[cs:ce].reshape(8, 128, 4).transpose(1, 0, 2)).reshape(128, 32)
        d["convb"] = np.ascontiguousarray(conv_b[cs:ce].reshape(8, 128).T)
        d["bdt"] = np.ascontiguousarray(b_dt[cs:ce].reshape(8, 128).T)
        d["dvec"] = np.ascontiguousarray(Dv[cs:ce].reshape(8, 128).T)
        in_maps.append(d)
    return in_maps


def kernel(**inputs):
    in_maps = _host_layout(inputs)
    if "nc" not in _NC_CACHE:
        _NC_CACHE["nc"] = build_program()
    nc = _NC_CACHE["nc"]
    res = run_bass_kernel_spmd(nc, in_maps, core_ids=list(range(NCORE)))
    out = np.empty((8192, 4096), dtype=np.float32)
    for c in range(NCORE):
        out[c * 1024:(c + 1) * 1024] = res.results[c]["oT"].T
    return out.reshape(2, 4096, 4096)


# version 19
# speedup vs baseline: 1.0347x; 1.0100x over previous
import numpy as np
import contextlib
import concourse.bass as bass
import concourse.mybir as mybir
from concourse.bass_utils import run_bass_kernel_spmd

F32 = mybir.dt.float32
BF16 = mybir.dt.bfloat16
F16 = mybir.dt.float16
I32 = mybir.dt.int32
AF = mybir.ActivationFunctionType
ALU = mybir.AluOpType
ds = bass.ds

H = 4096
DIN = 8192
NST = 16
RDT = 256
NCORE = 8
CT = 512
NCHUNK = 16
NDT = 8
LN_EPS = 1e-5
RMS_EPS = 1e-6
RG = [list(range(NCORE))]


def build_program(debug=False):
    nc = bass.Bass("TRN2", target_bir_lowering=False)

    x0T = nc.dram_tensor("x0T", [H, 1024], F32, kind="ExternalInput").ap()
    winx = nc.dram_tensor("winx", [1024, 4096], F32, kind="ExternalInput").ap()
    wing = nc.dram_tensor("wing", [1024, 4096], F32, kind="ExternalInput").ap()
    wxT = nc.dram_tensor("wxT", [128, 8 * 288], F32, kind="ExternalInput").ap()
    wdtT = nc.dram_tensor("wdtT", [128, 2048], F32, kind="ExternalInput").ap()
    wout = nc.dram_tensor("wout", [128, 32768], F32, kind="ExternalInput").ap()
    q3c = nc.dram_tensor("q3c", [512, 4096], I32, kind="ExternalInput").ap()
    rngc = nc.dram_tensor("rngc", [512, 256], F16, kind="ExternalInput").ap()
    biasc = nc.dram_tensor("biasc", [128, 32], F16, kind="ExternalInput").ap()
    alog = nc.dram_tensor("alog", [128, 128], F32, kind="ExternalInput").ap()
    convw = nc.dram_tensor("convw", [128, 32], F32, kind="ExternalInput").ap()
    convb = nc.dram_tensor("convb", [128, 8], F32, kind="ExternalInput").ap()
    bdt = nc.dram_tensor("bdt", [128, 8], F32, kind="ExternalInput").ap()
    dvec = nc.dram_tensor("dvec", [128, 8], F32, kind="ExternalInput").ap()
    oT = nc.dram_tensor("oT", [H, 1024], F32, kind="ExternalOutput").ap()

    winx_bf = nc.dram_tensor("winx_bf", [1024, 4096], BF16)
    wing_bf = nc.dram_tensor("wing_bf", [1024, 4096], BF16)
    woutp_bf = nc.dram_tensor("woutp_bf", [128, 32768], BF16)
    wout_all = nc.dram_tensor("wout_all", [8 * 128, 32768], BF16)
    hsln_part = nc.dram_tensor("hsln_part", [H, 1024], BF16)
    hsln_all = nc.dram_tensor("hsln_all", [8 * H, 1024], BF16)
    ssmpart = nc.dram_tensor("ssmpart", [NCHUNK, 288, CT], F32)
    ssmred = nc.dram_tensor("ssmred", [NCHUNK, 288, CT], F32)
    wffn_part = nc.dram_tensor("wffn_part", [128, 16384], BF16)
    wffn_all = nc.dram_tensor("wffn_all", [8 * 128, 16384], BF16)

    yag_e = nc.dram_tensor("yag_e", [8 * 8, 1024, CT], BF16)
    yag_o = nc.dram_tensor("yag_o", [8 * 8, 1024, CT], BF16)
    ybf_e = nc.dram_tensor("ybf_e", [8, 1024, CT], BF16)
    ybf_o = nc.dram_tensor("ybf_o", [8, 1024, CT], BF16)
    x1d = nc.dram_tensor("x1d", [2, H, CT], F32)
    dbgu = nc.dram_tensor("dbgu", [NCHUNK, 1024, CT], BF16)
    dbgg = nc.dram_tensor("dbgg", [NCHUNK, 1024, CT], BF16)
    dbgp = nc.dram_tensor("dbgp", [2, 1024, CT], F32)
    dbgr = nc.dram_tensor("dbgr", [4096, CT], BF16)
    dbgw = nc.dram_tensor("dbgw", [128, 4096], BF16)

    es = contextlib.ExitStack()
    with es:
        ab = es.enter_context(nc.sbuf_tensor("ab", [128, 49152], BF16))
        af = es.enter_context(nc.sbuf_tensor("af", [128, 20992], F32))
        id_bf = es.enter_context(nc.sbuf_tensor("id_bf", [128, 128], BF16))
        id_f32 = es.enter_context(nc.sbuf_tensor("id_f32", [128, 128], F32))
        sel_b = es.enter_context(nc.sbuf_tensor("sel_b", [16, 2048], BF16))
        rowa = es.enter_context(nc.sbuf_tensor("rowa", [1, 512], F32))
        rowb = es.enter_context(nc.sbuf_tensor("rowb", [1, 512], F32))
        rowc = es.enter_context(nc.sbuf_tensor("rowc", [1, 512], F32))
        rowd = es.enter_context(nc.sbuf_tensor("rowd", [1, 512], F32))
        rowe = es.enter_context(nc.sbuf_tensor("rowe", [1, 512], F32))
        ones1r = es.enter_context(nc.sbuf_tensor("ones1r", [1, 128], F32))
        ones16r = es.enter_context(nc.sbuf_tensor("ones16r", [1, 16], F32))
        ones128 = es.enter_context(nc.sbuf_tensor("ones128", [128, 1], F32))
        ones16 = es.enter_context(nc.sbuf_tensor("ones16", [16, 1], F32))
        wxT_sb = es.enter_context(nc.sbuf_tensor("wxT_sb", [128, 8 * 288], BF16))
        wdtT_sb = es.enter_context(nc.sbuf_tensor("wdtT_sb", [128, 2048], BF16))
        A_sb = es.enter_context(nc.sbuf_tensor("A_sb", [128, 128], F32))

        convw_sb = es.enter_context(nc.sbuf_tensor("convw_sb", [128, 32], F32))
        convb_sb = es.enter_context(nc.sbuf_tensor("convb_sb", [128, 8], F32))
        bdt_sb = es.enter_context(nc.sbuf_tensor("bdt_sb", [128, 8], F32))
        dvec_sb = es.enter_context(nc.sbuf_tensor("dvec_sb", [128, 8], F32))
        biasc_sb = es.enter_context(nc.sbuf_tensor("biasc_sb", [128, 32], F32))
        carry = es.enter_context(nc.sbuf_tensor("carry", [128, 128], F32))
        halo = es.enter_context(nc.sbuf_tensor("halo", [128, 24], F32))
        epsc = es.enter_context(nc.sbuf_tensor("epsc", [1, 4], F32))
        one128 = es.enter_context(nc.sbuf_tensor("one128", [128, 1], F32))
        sclT = es.enter_context(nc.sbuf_tensor("sclT", [128, 128], BF16))
        mnnT = es.enter_context(nc.sbuf_tensor("mnnT", [128, 128], BF16))

        ps_m0 = es.enter_context(nc.psum_tensor("ps_m0", [128, 512], F32))
        ps_m1 = es.enter_context(nc.psum_tensor("ps_m1", [128, 512], F32))
        ps_a = es.enter_context(nc.psum_tensor("ps_a", [128, 512], F32))
        ps_b = es.enter_context(nc.psum_tensor("ps_b", [128, 512], F32))
        ps_c = es.enter_context(nc.psum_tensor("ps_c", [128, 512], F32))
        ps_d = es.enter_context(nc.psum_tensor("ps_d", [128, 512], F32))
        ps_e = es.enter_context(nc.psum_tensor("ps_e", [128, 512], F32))
        ps_f = es.enter_context(nc.psum_tensor("ps_f", [128, 512], F32))

        sem_ld = es.enter_context(nc.semaphore("sem_ld"))
        sem_g = es.enter_context(nc.semaphore("sem_g"))
        sem_cc = es.enter_context(nc.semaphore("sem_cc"))
        sem_t = es.enter_context(nc.semaphore("sem_t"))
        sem_v = es.enter_context(nc.semaphore("sem_v"))
        sem_s = es.enter_context(nc.semaphore("sem_s"))
        sem_lp = es.enter_context(nc.semaphore("sem_lp"))
        sem_gp = es.enter_context(nc.semaphore("sem_gp"))

        RHS_O = 0
        WM_O = 16384
        XBF_O = 24576
        GBF_O = 28672
        TSN_O = 32768
        YST_O = 33792
        HST_O = 34816
        BN_O = 35840
        DQ_O = 36864
        Y_O = 0
        XLN_O = 0
        WOUT_O = 32768
        WFF_O = 32768
        WNAT_O = 16384

        DT_O = 0
        DA_O = 4096
        DTU_O = 6144
        SAB_O = 8192
        YA_O = 10240
        TB_O = 14336
        STG_O = 15360
        SQ_O = 16896
        T0_O = 17920
        ACC_O = 18944
        BCR_O = 19968
        XCAT_O = 0
        XI_O = 0
        X1_O = 14336
        SCR_O = 15360

        def abf(off, n, p=128):
            return ab[0:p, off:off + n]

        def af32(off, n, p=128):
            return af[0:p, off:off + n]

        prog = {k: [] for k in ("sync", "vector", "scalar", "tensor", "gpsimd")}

        class S:
            __slots__ = ("h", "n")

            def __init__(self, h):
                self.h = h
                self.n = 0

        Sld, Sg, Scc, St_, Sv, Ss = (S(sem_ld), S(sem_g), S(sem_cc),
                                     S(sem_t), S(sem_v), S(sem_s))
        Slp, Sgp = S(sem_lp), S(sem_gp)

        def EM(e, fn, sem=None, k=1):
            if sem is None:
                prog[e].append(fn)
                return None
            sem.n += k
            prog[e].append(lambda eng, fn=fn, s=sem, k=k: fn(eng).then_inc(s.h, k))
            return sem.n

        def WT(e, sem, v):
            if v > 0:
                prog[e].append(lambda eng, s=sem, v=v: eng.wait_ge(s.h, v))

        def CERT(queue):
            qs, ps = (Sld, Slp) if queue == "sync" else (Sg, Sgp)
            n = qs.n
            ps.n += 1
            prog[queue].append(
                lambda eng, n=n, q=qs, p=ps: eng.wait_ge(q.h, n).then_inc(p.h, 1))
            return ps.n

        hold = {}

        EM("gpsimd", lambda e: e.memset(id_bf[:], 0.0))
        EM("gpsimd", lambda e: e.affine_select(
            out=id_bf[:], in_=id_bf[:], compare_op=ALU.not_equal, fill=1.0,
            base=0, pattern=[[-1, 128]], channel_multiplier=1))
        EM("gpsimd", lambda e: e.memset(id_f32[:], 0.0))
        v_ident = EM("gpsimd", lambda e: e.affine_select(
            out=id_f32[:], in_=id_f32[:], compare_op=ALU.not_equal, fill=1.0,
            base=0, pattern=[[-1, 128]], channel_multiplier=1), Sgp, 1)
        P0 = abf(DQ_O, 4096)
        EM("gpsimd", lambda e: e.memset(P0, 1.0))
        EM("gpsimd", lambda e: e.affine_select(
            out=P0, in_=P0, compare_op=ALU.is_ge, fill=0.0,
            base=0, pattern=[[1, 4096]], channel_multiplier=-32))
        v_p0 = EM("gpsimd", lambda e: e.affine_select(
            out=P0, in_=P0, compare_op=ALU.is_ge, fill=0.0,
            base=31, pattern=[[-1, 4096]], channel_multiplier=32), Sgp, 1)

        EM("gpsimd", lambda e: e.memset(sel_b[:], 0.0))
        v_sel = EM("gpsimd", lambda e: e.affine_select(
            out=sel_b[:].rearrange("g (j m) -> g j m", m=128),
            in_=sel_b[:].rearrange("g (j m) -> g j m", m=128),
            compare_op=ALU.not_equal, fill=1.0, base=0,
            pattern=[[1, 16], [0, 128]], channel_multiplier=-1), Sgp, 1)
        EM("vector", lambda e: e.memset(ones128[:], 1.0))
        EM("vector", lambda e: e.memset(ones16[:], 1.0))
        EM("vector", lambda e: e.memset(ones1r[:], 1.0))
        EM("vector", lambda e: e.memset(ones16r[:], 1.0))
        EM("vector", lambda e: e.memset(carry[:], 0.0))
        EM("vector", lambda e: e.memset(epsc[0:1, 0:1], 1e-5))
        EM("vector", lambda e: e.memset(epsc[0:1, 1:2], 1e-6))
        EM("vector", lambda e: e.memset(epsc[0:1, 2:3], 1.0))
        EM("vector", lambda e: e.memset(one128[:], 1.0))
        v_pre_v = EM("vector", lambda e: e.memset(halo[:], 0.0), Sv, 1)
        WT("scalar", Sv, v_pre_v)

        for (dst, src) in ((convw_sb[:], convw), (convb_sb[:], convb),
                           (bdt_sb[:], bdt), (dvec_sb[:], dvec), (af32(1408, 128), alog)):
            EM("sync", lambda e, d=dst, s=src: e.dma_start(out=d, in_=s), Sld, 16)
        CERT("sync")
        v_consts = Sld.n
        WT("scalar", Slp, Slp.n)
        v_expA = EM("scalar", lambda e: e.activation(
            out=A_sb[:], in_=af32(1408, 128), func=AF.Exp), Ss, 1)
        WT("vector", Ss, v_expA)
        v_negA = EM("vector", lambda e: e.tensor_scalar(
            out=A_sb[:], in0=A_sb[:], scalar1=-1.0, scalar2=None, op0=ALU.mult), Sv, 1)
        WT("scalar", Sv, v_negA)

        v_cwx = EM("gpsimd", lambda e: e.dma_start(out=wxT_sb[:], in_=wxT), Sg, 16)
        v_cwdt = EM("gpsimd", lambda e: e.dma_start(out=wdtT_sb[:], in_=wdtT), Sg, 16)
        v_cbias = EM("gpsimd", lambda e: e.dma_start(out=biasc_sb[:], in_=biasc), Sg, 16)
        v_cwinx = EM("gpsimd", lambda e: e.dma_start(out=winx_bf.ap(), in_=winx), Sg, 16)
        v_cwing = EM("gpsimd", lambda e: e.dma_start(out=wing_bf.ap(), in_=wing), Sg, 16)
        v_cwout = EM("gpsimd", lambda e: e.dma_start(out=woutp_bf.ap(), in_=wout), Sg, 16)
        CERT("gpsimd")

        def _rank(e):
            hold["rank"] = e.cc_rank(replica_groups=RG)
        EM("sync", _rank)

        xi = [af32(XI_O, 512), af32(XI_O + 512, 512)]
        sqp = [af32(SQ_O, 512), af32(SQ_O + 512, 512)]
        hst = [abf(HST_O, 512), abf(HST_O + 512, 512)]
        v_hst_st = [0, 0]
        v_ln_tile = [0] * 32
        v_nm_last = 0
        for hf in range(2):
            col = hf * 512
            for kk in range(32):
                sl = kk % 2
                if hf > 0 or kk >= 2:
                    WT("sync", St_, v_ln_tile[kk - 2] if kk >= 2 else v_ln_tile[31])
                if hf == 1 and kk < 2:
                    WT("sync", Sv, v_nm_last)
                v_ld = EM("sync", lambda e, s=xi[sl], kk=kk, col=col: e.dma_start(
                    out=s, in_=x0T[kk * 128:(kk + 1) * 128, col:col + 512]), Sld, 16)
                CERT("sync")
                WT("scalar", Slp, Slp.n)
                if hf > 0 or kk >= 2:
                    WT("scalar", St_, v_ln_tile[kk - 2] if kk >= 2 else v_ln_tile[31])
                v_sq = EM("scalar", lambda e, s=xi[sl], q=sqp[sl]: e.activation(
                    out=q, in_=s, func=AF.Square), Ss, 1)
                WT("tensor", Slp, Slp.n)
                EM("tensor", lambda e, s=xi[sl], kk=kk: e.matmul(
                    ps_d[0:1, :], ones128[:], s, start=(kk == 0), stop=(kk == 31),
                    skip_group_check=True))
                WT("tensor", Ss, v_sq)
                v_ln_tile[kk] = EM("tensor", lambda e, q=sqp[sl], kk=kk: e.matmul(
                    ps_e[0:1, :], ones128[:], q, start=(kk == 0), stop=(kk == 31),
                    skip_group_check=True), St_, 1)
            WT("scalar", St_, v_ln_tile[31])
            EM("scalar", lambda e: e.activation(
                out=rowa[:], in_=ps_d[0:1, :], func=AF.Copy, scale=1.0 / H))
            v_mean = EM("scalar", lambda e: e.activation(
                out=rowd[:], in_=rowa[:], func=AF.Square), Ss, 1)
            WT("vector", Ss, v_mean)
            v_var = EM("vector", lambda e: e.scalar_tensor_tensor(
                out=rowe[:], in0=ps_e[0:1, :], scalar=1.0 / H,
                in1=rowd[:], op0=ALU.mult, op1=ALU.subtract), Sv, 1)
            WT("scalar", Sv, v_var)
            EM("scalar", lambda e: e.activation(
                out=rowd[:], in_=rowe[:], func=AF.Ln, bias=epsc[0:1, 0:1]))
            v_rstd = EM("scalar", lambda e: e.activation(
                out=rowb[:], in_=rowd[:], func=AF.Exp, scale=-0.5), Ss, 1)
            WT("tensor", Ss, v_rstd)
            EM("tensor", lambda e: e.matmul(
                ps_c[:], ones1r[:], rowa[:], start=True, stop=True))
            v_bc = EM("tensor", lambda e: e.matmul(
                ps_d[:], ones1r[:], rowb[:], start=True, stop=True), St_, 1)
            WT("sync", St_, v_ln_tile[31])
            for kk in range(32):
                sl = kk % 2
                WT("sync", Sv, Sv.n)
                v_ld = EM("sync", lambda e, s=xi[sl], kk=kk, col=col: e.dma_start(
                    out=s, in_=x0T[kk * 128:(kk + 1) * 128, col:col + 512]), Sld, 16)
                CERT("sync")
                WT("vector", Slp, Slp.n)
                if kk == 0:
                    WT("vector", St_, v_bc)
                WT("vector", Sgp, Sgp.n)
                EM("vector", lambda e, s=xi[sl], q=sqp[sl]: e.tensor_tensor(
                    out=q, in0=s, in1=ps_c[:], op=ALU.subtract))
                v_nm = EM("vector", lambda e, q=sqp[sl], h=hst[sl]: e.tensor_tensor(
                    out=h, in0=q, in1=ps_d[:], op=ALU.mult), Sv, 1)
                WT("gpsimd", Sv, v_nm)
                v_hst_st[sl] = EM("gpsimd", lambda e, h=hst[sl], kk=kk, col=col: e.dma_start(
                    out=hsln_part.ap()[kk * 128:(kk + 1) * 128, col:col + 512], in_=h), Sg, 16)
                CERT("gpsimd")
            v_nm_last = Sv.n
        WT("gpsimd", Sg, Sg.n)
        v_ag_hsln = EM("gpsimd", lambda e: e.collective_compute(
            "AllGather", ALU.bypass, replica_groups=RG,
            ins=[hsln_part.ap()], outs=[hsln_all.ap()]), Scc, 1)
        WT("gpsimd", Sg, Sg.n)
        WT("gpsimd", Scc, Scc.n)
        v_ag_wout = EM("gpsimd", lambda e: e.collective_compute(
            "AllGather", ALU.bypass, replica_groups=RG,
            ins=[woutp_bf.ap()], outs=[wout_all.ap()]), Scc, 1)

        wnat = [abf(WNAT_O + j * 128, 128) for j in range(4)]
        wttile = [abf(WNAT_O + 512 + j * 128, 128) for j in range(4)]
        q3f = af32(0, 512)
        q3i = af[0:128, 512:1024].bitcast(I32)
        rngt = af[0:128, 1024:1152].bitcast(F16)
        scl = af32(1152, 128)
        mnn = af32(1280, 128)
        WT("sync", Sv, v_nm_last)
        WT("tensor", Sv, v_nm_last)
        v_wq_st = 0
        v_q3_rd = 0
        for ot in range(4):
            v_rng = EM("sync", lambda e, ot=ot: e.dma_start(
                out=rngt, in_=rngc[ot * 128:(ot + 1) * 128, :]), Sld, 16)
            CERT("sync")
            WT("vector", Slp, Slp.n)
            rv = rngt.rearrange("p (g two) -> p g two", two=2)
            EM("vector", lambda e, rv=rv: e.tensor_tensor(
                out=scl, in0=rv[:, :, 1], in1=rv[:, :, 0], op=ALU.subtract))
            v_sc = EM("vector", lambda e, rv=rv: e.tensor_copy(
                out=mnn, in_=rv[:, :, 0]), Sv, 1)
            WT("tensor", Sv, v_sc)
            if ot == 0:
                WT("tensor", Sgp, Sgp.n)
            v_t1 = EM("tensor", lambda e: e.transpose(
                ps_e[0:128, 0:128], scl, id_f32[:]), St_, 1)
            WT("vector", St_, v_t1)
            v_e1 = EM("vector", lambda e: e.tensor_copy(
                out=sclT[:], in_=ps_e[0:128, 0:128]), Sv, 1)
            WT("tensor", Sv, v_e1)
            v_t2 = EM("tensor", lambda e: e.transpose(
                ps_e[0:128, 0:128], mnn, id_f32[:]), St_, 1)
            WT("vector", St_, v_t2)
            v_e2 = EM("vector", lambda e: e.tensor_copy(
                out=mnnT[:], in_=ps_e[0:128, 0:128]), Sv, 1)
            for istr in range(8):
                WT("tensor", Sv, v_e2)
                if ot == 0 and istr == 0:
                    WT("tensor", Sgp, Sgp.n)
                WT("tensor", Sv, v_q3_rd)
                EM("tensor", lambda e, istr=istr: e.matmul(
                    ps_c[:], sclT[:], P0[:, istr * 512:(istr + 1) * 512],
                    start=True, stop=True))
                v_mm = EM("tensor", lambda e, istr=istr: e.matmul(
                    ps_d[:], mnnT[:], P0[:, istr * 512:(istr + 1) * 512],
                    start=True, stop=True), St_, 1)
                WT("sync", Sv, v_q3_rd)
                v_q3 = EM("sync", lambda e, ot=ot, istr=istr: e.dma_start(
                    out=q3i, in_=q3c[ot * 128:(ot + 1) * 128, istr * 512:(istr + 1) * 512]),
                    Sld, 16)
                CERT("sync")
                WT("vector", Slp, Slp.n)
                EM("vector", lambda e: e.tensor_scalar(
                    out=q3f, in0=q3i, scalar1=1.0 / 7.0, scalar2=None, op0=ALU.mult))
                WT("vector", St_, v_mm)
                EM("vector", lambda e: e.tensor_tensor(
                    out=q3f, in0=q3f, in1=ps_c[:], op=ALU.mult))
                WT("vector", Sgp, Sgp.n)
                v_wn = EM("vector", lambda e: e.tensor_tensor(
                    out=ab[0:128, WNAT_O:WNAT_O + 512], in0=q3f, in1=ps_d[:],
                    op=ALU.add), Sv, 1)
                v_q3_rd = v_wn
                WT("tensor", Sv, v_wn)
                for j in range(4):
                    v_tt = EM("tensor", lambda e, j=j: e.transpose(
                        ps_f[0:128, 0:64].bitcast(BF16), wnat[j], id_bf[:]), St_, 1)
                    WT("scalar", St_, v_tt)
                    v_ev = EM("scalar", lambda e, j=j: e.activation(
                        out=wttile[j], in_=ps_f[0:128, 0:64].bitcast(BF16), func=AF.Copy), Ss, 1)
                    WT("tensor", Ss, v_ev)
                    WT("gpsimd", Ss, v_ev)
                    v_wq_st = EM("gpsimd", lambda e, ot=ot, istr=istr, j=j: e.dma_start(
                        out=wffn_part.ap().rearrange(
                            "p (o k f) -> p o k f", o=4, k=32)[:, ot, istr * 4 + j, :],
                        in_=wttile[j]), Sg, 16)
                    CERT("gpsimd")
        WT("gpsimd", Sg, Sg.n)
        WT("gpsimd", Scc, Scc.n)
        v_ag_wffn = EM("gpsimd", lambda e: e.collective_compute(
            "AllGather", ALU.bypass, replica_groups=RG,
            ins=[wffn_part.ap()], outs=[wffn_all.ap()]), Scc, 1)
        v_deq_done = Sg.n

        WT("tensor", Sgp, Sgp.n)
        for psw in (ps_m0, ps_m1):
            for kk in range(4):
                EM("tensor", lambda e, psw=psw, kk=kk: e.matmul(
                    psw[0:128, 0:128], id_bf[:], id_bf[:],
                    start=(kk == 0), stop=(kk == 3)))
        hsv = hsln_all.ap().rearrange("(blk h) t -> blk h t", h=H)
        winx_v = winx_bf.ap().rearrange("(m p) f -> m p f", p=128)
        wing_v = wing_bf.ap().rearrange("(m p) f -> m p f", p=128)
        rhs_t = [abf(RHS_O + kk * 512, 512) for kk in range(32)]
        wm = [abf(WM_O, 4096), abf(WM_O + 4096, 4096)]
        xbf_pp = [[abf(XBF_O + d * 512, 512) for d in range(NDT)],
                  [abf(40960 + d * 512, 512) for d in range(NDT)]]
        gbf_pp = [[abf(GBF_O + d * 512, 512) for d in range(NDT)],
                  [abf(45056 + d * 512, 512) for d in range(NDT)]]
        tsn = [abf(TSN_O, 512), abf(TSN_O + 512, 512)]
        yst = [abf(YST_O, 512), abf(YST_O + 512, 512)]
        bN = ab[0:16, BN_O:BN_O + 512]
        cN = ab[0:16, BN_O + 512:BN_O + 1024]
        dt_t = [af32(DT_O + d * 512, 512) for d in range(NDT)]
        dtu_t = [af32(DTU_O + d * 512, 512) for d in range(NDT)]
        ya_t = [af32(YA_O + d * 512, 512) for d in range(NDT)]
        tb_t = [af32(TB_O, 512), af32(TB_O + 512, 512)]
        dAb_t = [abf(DQ_O + j * 512, 512) for j in range(4)]
        tbb_t = [abf(DQ_O + 2048, 512), abf(DQ_O + 2560, 512)]
        tsb_t = [abf(DQ_O + 3072, 512), abf(DQ_O + 3584, 512)]
        tcb_t = [abf(HST_O, 512), abf(HST_O + 512, 512)]
        dub_t = [af[0:128, DTU_O:DTU_O + 2048].bitcast(BF16)[:, d * 512:(d + 1) * 512]
                 for d in range(NDT)]
        bcb_t = [af[0:128, DA_O:DA_O + 1024].bitcast(BF16)[:, j * 512:(j + 1) * 512]
                 for j in range(4)]
        ts0 = af32(T0_O, 512)
        ts1 = af32(T0_O + 512, 512)
        accv = [af32(ACC_O, 512), af32(ACC_O + 512, 512)]
        xcat = [af32(XCAT_O, 515), af32(XCAT_O + 515, 515)]
        stg = [af32(STG_O, 512), af32(STG_O + 512, 512), af32(STG_O + 1024, 512, 32)]
        braw = af[0:16, BCR_O:BCR_O + 512]
        craw = af[0:16, BCR_O + 512:BCR_O + 1024]
        bsq = af[0:16, 5120:5632]
        csq = af[0:16, 5632:6144]

        last_rhs_use = [0] * 32
        v_wm_free = [0, 0]
        v_ssm_store = [0, 0, 0]
        v_dt = [0] * NDT
        v_yst = [0, 0]
        v_scan_use = {}
        v_bc_read = [0, 0]

        v_dtu_prev = 0
        v_evac_prev = [0, 0]
        v_ssm_ev = [0, 0, 0]
        v_ar_of = {}
        v_conv_prev = [0, 0]

        def emit_A(k):
            nonlocal v_evac_prev, v_conv_prev
            blk, half = k // 2, k % 2
            col = half * 512
            xbf = xbf_pp[k % 2]
            gbf = gbf_pp[k % 2]
            if k in (0, 8):
                WT("scalar", Sv, Sv.n)
                EM("scalar", lambda e: e.memzero(halo[:]))

            if k == 0:
                WT("sync", Scc, v_ag_hsln)
                WT("sync", Sgp, Sgp.n)
            v_rhs = [0] * 32
            for kk in range(32):
                WT("sync", St_, last_rhs_use[kk])
                v_rhs[kk] = EM("sync", lambda e, kk=kk, blk=blk, col=col: e.dma_start(
                    out=rhs_t[kk], in_=hsv[blk, kk * 128:(kk + 1) * 128, col:col + 512]),
                    Sld, 16)
            CERT("sync")

            if k == 0:
                WT("sync", Sgp, Sgp.n)
                WT("sync", Sgp, Sgp.n)
            v_mm_done = [0] * 16
            v_evac = [0] * 16
            v_psfree = [0] * 16
            v_silu = [0] * 16
            v_conv_done = [0] * 8
            for m in range(16):
                sl = m % 2
                wv = winx_v if m < 8 else wing_v
                mi = m if m < 8 else m - 8
                WT("sync", St_, v_wm_free[sl])
                v_wm = EM("sync", lambda e, wv=wv, mi=mi, sl=sl: e.dma_start(
                    out=wm[sl], in_=wv[mi]), Sld, 16)
                CERT("sync")
                psm = ps_m0 if sl == 0 else ps_m1
                WT("tensor", Slp, Slp.n)
                if m >= 2:
                    WT("tensor", Ss, v_psfree[m - 2])
                else:
                    WT("tensor", Ss, v_evac_prev[m])
                for kk in range(32):
                    if m == 0:
                        WT("tensor", Slp, Slp.n)
                    last = (kk == 31)
                    v = EM("tensor", lambda e, psm=psm, sl=sl, kk=kk: e.matmul(
                        psm[:], ab[0:128, WM_O + sl * 4096 + kk * 128:WM_O + sl * 4096 + (kk + 1) * 128],
                        rhs_t[kk], start=(kk == 0), stop=(kk == 31)),
                        St_ if last else None, 1)
                    if last:
                        v_mm_done[m] = v
                v_wm_free[sl] = v_mm_done[m]
                if debug and k == 0 and m == 0:
                    WT("gpsimd", St_, v_mm_done[0])
                    for kk in range(32):
                        EM("gpsimd", lambda e, kk=kk: e.dma_start(
                            out=dbgr.ap()[kk * 128:(kk + 1) * 128, :], in_=rhs_t[kk]), Sg, 16)
                    EM("gpsimd", lambda e: e.dma_start(
                        out=dbgw.ap(), in_=wm[0]), Sg, 16)
                WT("scalar", St_, v_mm_done[m])
                if m < 8:
                    xc = xcat[m % 2]
                    if m >= 2:
                        WT("scalar", Sv, v_conv_done[m - 2])
                    else:
                        WT("scalar", Sv, max(v_conv_prev[m], v_dtu_prev if m == 0 else 0))
                    v_evac[m] = EM("scalar", lambda e, psm=psm, xc=xc: e.activation(
                        out=xc[:, 3:515], in_=psm[:], func=AF.Copy), Ss, 1)
                    v_psfree[m] = v_evac[m]
                    if debug and k < 2:
                        WT("gpsimd", Ss, v_evac[m])
                        EM("gpsimd", lambda e, xc=xc, m=m, k=k: e.dma_start(
                            out=dbgp.ap()[k, m * 128:(m + 1) * 128, :], in_=xc[:, 3:515]),
                            Sg, 16)
                        CERT("gpsimd")
                        WT("scalar", Sgp, Sgp.n)
                    WT("vector", Ss, v_evac[m])
                    if m >= 2:
                        WT("vector", Ss, v_silu[m - 2])
                    EM("vector", lambda e, xc=xc, m=m: e.tensor_copy(
                        out=xc[:, 0:3], in_=halo[:, m * 3:(m + 1) * 3]))
                    av = accv[m % 2]
                    EM("vector", lambda e, xc=xc, av=av, m=m: e.tensor_scalar(
                        out=av, in0=xc[:, 3:515],
                        scalar1=convw_sb[:, m * 4 + 3:m * 4 + 4],
                        scalar2=convb_sb[:, m:m + 1], op0=ALU.mult, op1=ALU.add))
                    for j in range(3):
                        EM("vector", lambda e, xc=xc, av=av, m=m, j=j: e.scalar_tensor_tensor(
                            out=av, in0=xc[:, j:j + 512],
                            scalar=convw_sb[:, m * 4 + j:m * 4 + j + 1],
                            in1=av, op0=ALU.mult, op1=ALU.add))
                    v_conv_done[m] = EM("vector", lambda e, xc=xc, m=m: e.tensor_copy(
                        out=halo[:, m * 3:(m + 1) * 3], in_=xc[:, 512:515]), Sv, 1)
                    WT("scalar", Sv, v_conv_done[m])
                    v_silu[m] = EM("scalar", lambda e, av=av, m=m: e.activation(
                        out=xbf[m], in_=av, func=AF.Silu), Ss, 1)
                    v_evac[m] = v_silu[m]
                else:
                    dtl = m - 8
                    v_evac[m] = EM("scalar", lambda e, psm=psm, dtl=dtl: e.activation(
                        out=gbf[dtl], in_=psm[:], func=AF.Silu), Ss, 1)
                    v_psfree[m] = v_evac[m]
                    v_silu[m] = v_evac[m]
            for kk in range(32):
                last_rhs_use[kk] = v_mm_done[15]
            v_evac_prev = [v_psfree[14], v_psfree[15]]
            v_conv_prev = [v_conv_done[6], v_conv_done[7]]
            v_x_ready = Ss.n
            if debug:
                WT("gpsimd", Ss, v_x_ready)
                for d in range(NDT):
                    EM("gpsimd", lambda e, d=d, k=k: e.dma_start(
                        out=dbgu.ap()[k, d * 128:(d + 1) * 128, :], in_=xbf[d]), Sg, 16)
                    EM("gpsimd", lambda e, d=d, k=k: e.dma_start(
                        out=dbgg.ap()[k, d * 128:(d + 1) * 128, :], in_=gbf[d]), Sg, 16)

            if k == 0:
                WT("tensor", Sgp, Sgp.n)
            psl = [ps_a, ps_b, ps_c]
            v_wx = [0, 0, 0]
            for ms in range(3):
                msz = 128 if ms < 2 else 32
                for kk in range(NDT):
                    if kk == 0:
                        WT("tensor", Ss, v_x_ready)
                        WT("tensor", Ss, v_ssm_ev[ms])
                    last = (kk == NDT - 1)
                    v = EM("tensor", lambda e, ms=ms, msz=msz, kk=kk: e.matmul(
                        psl[ms][0:msz, :],
                        wxT_sb[:, kk * 288 + ms * 128:kk * 288 + ms * 128 + msz],
                        xbf[kk], start=(kk == 0), stop=(kk == NDT - 1)),
                        St_ if last else None, 1)
                    if last:
                        v_wx[ms] = v
            for ms in range(3):
                msz = 128 if ms < 2 else 32
                WT("scalar", St_, v_wx[ms])
                WT("scalar", Sgp, Sgp.n)
                v_ssm_ev[ms] = EM("scalar", lambda e, ms=ms, msz=msz: e.activation(
                    out=stg[ms] if ms == 2 else stg[ms][0:msz, :],
                    in_=psl[ms][0:msz, :], func=AF.Copy), Ss, 1)
                WT("gpsimd", Ss, v_ssm_ev[ms])
                v_ssm_store[ms] = EM("gpsimd", lambda e, ms=ms, msz=msz, k=k: e.dma_start(
                    out=ssmpart.ap()[k, ms * 128:ms * 128 + msz, :],
                    in_=stg[ms] if ms == 2 else stg[ms][0:msz, :]), Sg, 16)
            CERT("gpsimd")
            WT("gpsimd", Sg, Sg.n)
            WT("gpsimd", Scc, Scc.n)
            v_ar_of[k] = EM("gpsimd", lambda e, k=k: e.collective_compute(
                "AllReduce", ALU.add, replica_groups=RG,
                ins=[ssmpart.ap()[k]], outs=[ssmred.ap()[k]]), Scc, 1)

        def emit_C(k):
            nonlocal v_dtu_prev
            blk, half = k // 2, k % 2
            col = half * 512
            xbf = xbf_pp[k % 2]
            gbf = gbf_pp[k % 2]
            if k in (0, 8):
                WT("scalar", Sv, Sv.n)
                EM("scalar", lambda e: e.memzero(carry[:]))
            WT("sync", Scc, v_ar_of[k])
            EM("sync", lambda e, k=k: e.dma_start(out=ts0, in_=ssmred.ap()[k, 0:128, :]), Sld, 16)
            v_l1 = EM("sync", lambda e, k=k: e.dma_start(out=ts1, in_=ssmred.ap()[k, 128:256, :]), Sld, 16)
            EM("sync", lambda e, k=k: e.dma_start(out=braw, in_=ssmred.ap()[k, 256:272, :]), Sld, 16)
            v_l3 = EM("sync", lambda e, k=k: e.dma_start(out=craw, in_=ssmred.ap()[k, 272:288, :]), Sld, 16)
            CERT("sync")

            WT("scalar", Slp, Slp.n)
            EM("scalar", lambda e: e.activation(out=sqp[0], in_=ts0, func=AF.Square))
            EM("scalar", lambda e: e.activation(out=sqp[1], in_=ts1, func=AF.Square))
            WT("scalar", Slp, Slp.n)
            EM("scalar", lambda e: e.activation(out=bsq, in_=braw, func=AF.Square))
            v_sqbc = EM("scalar", lambda e: e.activation(out=csq, in_=craw, func=AF.Square), Ss, 1)
            WT("tensor", Ss, v_sqbc)
            EM("tensor", lambda e: e.matmul(ps_d[0:1, :], ones128[:], sqp[0], start=True, stop=False))
            EM("tensor", lambda e: e.matmul(ps_d[0:1, :], ones128[:], sqp[1], start=False, stop=True))
            EM("tensor", lambda e: e.matmul(ps_e[0:1, :], ones16[:], bsq, start=True, stop=True))
            v_strms = EM("tensor", lambda e: e.matmul(ps_f[0:1, :], ones16[:], csq, start=True, stop=True), St_, 1)
            WT("scalar", St_, v_strms)
            EM("scalar", lambda e: e.activation(out=rowd[:], in_=ps_d[0:1, :], func=AF.Ln, scale=1.0 / RDT, bias=epsc[0:1, 1:2]))
            EM("scalar", lambda e: e.activation(out=rowa[:], in_=rowd[:], func=AF.Exp, scale=-0.5))
            EM("scalar", lambda e: e.activation(out=rowd[:], in_=ps_e[0:1, :], func=AF.Ln, scale=1.0 / NST, bias=epsc[0:1, 1:2]))
            EM("scalar", lambda e: e.activation(out=rowb[:], in_=rowd[:], func=AF.Exp, scale=-0.5))
            EM("scalar", lambda e: e.activation(out=rowd[:], in_=ps_f[0:1, :], func=AF.Ln, scale=1.0 / NST, bias=epsc[0:1, 1:2]))
            v_rst = EM("scalar", lambda e: e.activation(out=rowc[:], in_=rowd[:], func=AF.Exp, scale=-0.5), Ss, 1)
            WT("tensor", Ss, v_rst)
            v_tb = EM("tensor", lambda e: e.matmul(
                ps_d[:], ones1r[:], rowa[:], start=True, stop=True), St_, 1)
            WT("vector", St_, v_tb)
            EM("vector", lambda e: e.tensor_tensor(out=tsn[0], in0=ts0, in1=ps_d[:], op=ALU.mult))
            v_tsn = EM("vector", lambda e: e.tensor_tensor(out=tsn[1], in0=ts1, in1=ps_d[:], op=ALU.mult), Sv, 1)
            EM("tensor", lambda e: e.matmul(
                ps_e[0:16, :], ones16r[:], rowb[:], start=True, stop=True))
            v_cbm = EM("tensor", lambda e: e.matmul(
                ps_f[0:16, :], ones16r[:], rowc[:], start=True, stop=True), St_, 1)
            WT("vector", St_, v_cbm)
            EM("vector", lambda e: e.tensor_tensor(out=bN, in0=braw, in1=ps_e[0:16, :], op=ALU.mult))
            v_bcn = EM("vector", lambda e: e.tensor_tensor(out=cN, in0=craw, in1=ps_f[0:16, :], op=ALU.mult), Sv, 1)

            if k == 0:
                WT("tensor", Sgp, Sgp.n)
            WT("scalar", Sgp, Sgp.n)
            v_dtadd = [0] * NDT
            for d in range(NDT):
                psd = ps_a if (d % 2 == 0) else ps_b
                if d == 0:
                    WT("tensor", Sv, v_tsn)
                if d >= 2:
                    WT("tensor", Ss, v_dt[d - 2])
                for rr in range(2):
                    last = (rr == 1)
                    v = EM("tensor", lambda e, psd=psd, d=d, rr=rr: e.matmul(
                        psd[:], wdtT_sb[:, rr * 1024 + d * 128:rr * 1024 + (d + 1) * 128],
                        tsn[rr], start=(rr == 0), stop=(rr == 1)),
                        St_ if last else None, 1)
                WT("scalar", St_, v)
                if d >= 2:
                    WT("scalar", Sv, v_dtadd[d - 2])
                else:
                    WT("scalar", Sv, v_dtu_prev)
                sa = af32(SAB_O + (d % 2) * 1024, 512)
                sb = af32(SAB_O + (d % 2) * 1024 + 512, 512)
                EM("scalar", lambda e, psd=psd, d=d, sa=sa: e.activation(
                    out=sa, in_=psd[:], func=AF.Abs, bias=bdt_sb[:, d:d + 1]))
                EM("scalar", lambda e, sa=sa, sb=sb: e.activation(
                    out=sb, in_=sa, func=AF.Exp, scale=-1.0))
                EM("scalar", lambda e, sa=sa, sb=sb: e.activation(
                    out=sa, in_=sb, func=AF.Ln, bias=one128[:]))
                v_dt[d] = EM("scalar", lambda e, psd=psd, d=d, sb=sb: e.activation(
                    out=sb, in_=psd[:], func=AF.Relu, bias=bdt_sb[:, d:d + 1]), Ss, 1)
                WT("vector", Ss, v_dt[d])
                EM("vector", lambda e, d=d, sa=sa, sb=sb: e.tensor_tensor(
                    out=dt_t[d], in0=sa, in1=sb, op=ALU.add))
                v_dtadd[d] = EM("vector", lambda e, d=d, xb=xbf[d]: e.tensor_tensor(
                    out=dub_t[d], in0=dt_t[d], in1=xb, op=ALU.mult), Sv, 1)
                if d == NDT - 1:
                    v_dtu_prev = v_dtadd[d]

            v_bc_ev = [0, 0]
            for n in range(NST):
                par = n % 2
                psB = ps_c if par == 0 else ps_e
                psC = ps_d if par == 0 else ps_f
                bcB = bcb_t[2 * par]
                bcC = bcb_t[2 * par + 1]
                if n == 0:
                    WT("tensor", Sv, v_bcn)
                else:
                    WT("tensor", Sv, v_bc_read[par])
                EM("tensor", lambda e, n=n, psB=psB: e.matmul(
                    psB[:], sel_b[:, n * 128:(n + 1) * 128], bN, start=True, stop=True))
                v_bcast = EM("tensor", lambda e, n=n, psC=psC: e.matmul(
                    psC[:], sel_b[:, n * 128:(n + 1) * 128], cN, start=True, stop=True), St_, 1)
                WT("scalar", St_, v_bcast)
                if n >= 2:
                    WT("scalar", Sv, v_bc_read[par])
                else:
                    WT("scalar", Sv, v_bc_read[par])
                EM("scalar", lambda e, psB=psB, bcB=bcB: e.activation(
                    out=bcB, in_=psB[:], func=AF.Copy))
                v_bc_ev[par] = EM("scalar", lambda e, psC=psC, bcC=bcC: e.activation(
                    out=bcC, in_=psC[:], func=AF.Copy), Ss, 1)
                WT("vector", Ss, v_bc_ev[par])
                for d in range(NDT):
                    g = n * NDT + d
                    j = g % 4
                    pr = g % 2
                    if g >= 4:
                        WT("scalar", Sv, v_scan_use.get((k, g - 4), 0))
                    else:
                        WT("scalar", Sv, v_dtadd[NDT - 1])
                    v_da = EM("scalar", lambda e, d=d, n=n, j=j: e.activation(
                        out=dAb_t[j], in_=dt_t[d], func=AF.Exp,
                        scale=A_sb[:, d * 16 + n:d * 16 + n + 1]), Ss, 1)
                    EM("vector", lambda e, d=d, pr=pr, bcB=bcB: e.tensor_tensor(
                        out=tbb_t[pr], in0=dub_t[d], in1=bcB, op=ALU.mult))
                    WT("vector", Ss, v_da)
                    cidx = d * 16 + n
                    EM("vector", lambda e, j=j, pr=pr, cidx=cidx: e.tensor_tensor_scan(
                        out=tsb_t[pr], data0=dAb_t[j], data1=tbb_t[pr],
                        initial=carry[:, cidx:cidx + 1], op0=ALU.mult, op1=ALU.add))
                    v_cy = EM("vector", lambda e, pr=pr, bcC=bcC: e.tensor_tensor(
                        out=tcb_t[pr], in0=tsb_t[pr], in1=bcC, op=ALU.mult), Sv, 1)
                    v_scan_use[(k, g)] = v_cy
                    if n == 0:
                        EM("vector", lambda e, d=d, pr=pr: e.tensor_copy(
                            out=ya_t[d], in_=tcb_t[pr]))
                    else:
                        EM("vector", lambda e, d=d, pr=pr: e.tensor_tensor(
                            out=ya_t[d], in0=ya_t[d], in1=tcb_t[pr], op=ALU.add))
                    EM("vector", lambda e, pr=pr, cidx=cidx: e.tensor_copy(
                        out=carry[:, cidx:cidx + 1], in_=tsb_t[pr][:, 511:512]))
                    if d == NDT - 1:
                        v_bc_read[par] = v_cy

            for d in range(NDT):
                pr = d % 2
                EM("vector", lambda e, d=d, xb=xbf[d]: e.scalar_tensor_tensor(
                    out=ya_t[d], in0=xb, scalar=dvec_sb[:, d:d + 1],
                    in1=ya_t[d], op0=ALU.mult, op1=ALU.add))
                WT("vector", Sgp, Sgp.n)
                v_y3 = EM("vector", lambda e, d=d, pr=pr: e.tensor_tensor(
                    out=yst[pr], in0=ya_t[d], in1=gbf[d], op=ALU.mult), Sv, 1)
                WT("gpsimd", Sv, v_y3)
                ybf_t = ybf_e if k % 2 == 0 else ybf_o
                v_yst[pr] = EM("gpsimd", lambda e, d=d, k=k, pr=pr, yb=ybf_t: e.dma_start(
                    out=yb.ap()[k // 2, d * 128:(d + 1) * 128, :], in_=yst[pr]), Sg, 16)
                CERT("gpsimd")
        emit_A(0)
        for k in range(1, NCHUNK):
            emit_A(k)
            emit_C(k - 1)
            if k - 1 == 14:
                WT("gpsimd", Sg, Sg.n)
                WT("gpsimd", Scc, Scc.n)
                v_agy_e = EM("gpsimd", lambda e: e.collective_compute(
                    "AllGather", ALU.bypass, replica_groups=RG,
                    ins=[ybf_e.ap()], outs=[yag_e.ap()]), Scc, 1)
        emit_C(NCHUNK - 1)
        WT("gpsimd", Sg, Sg.n)
        WT("gpsimd", Scc, Scc.n)
        v_agy_o = EM("gpsimd", lambda e: e.collective_compute(
            "AllGather", ALU.bypass, replica_groups=RG,
            ins=[ybf_o.ap()], outs=[yag_o.ap()]), Scc, 1)

        y_t = [abf(Y_O + kk * 512, 512) for kk in range(64)]
        wo_sl = [abf(WOUT_O, 4096), abf(WOUT_O + 4096, 4096)]
        xln_t = [abf(XLN_O + kk * 512, 512) for kk in range(32)]
        wf_sl = [abf(WFF_O, 4096), abf(WFF_O + 4096, 4096)]
        x1p = [af32(X1_O, 512), af32(X1_O + 512, 512)]
        scr = af32(SCR_O, 512)
        x0p = [af32(XI_O, 512), af32(XI_O + 512, 512)]
        otp = [af32(ACC_O, 512), af32(ACC_O + 512, 512)]
        yagv_e = yag_e.ap().rearrange("(r k) ch t -> r k ch t", k=8)
        yagv_o = yag_o.ap().rearrange("(r k) ch t -> r k ch t", k=8)
        wov = wout_all.ap().rearrange("(r p) (m k f) -> r p m k f", p=128, m=32, k=8)
        wfv = wffn_all.ap().rearrange("(r p) (o k f) -> r p o k f", p=128, o=4, k=32)


        v_wo_free = [0, 0]
        v_wf_free = [0, 0]
        v_ost = [0, 0]
        v_x1s_arr = [0] * 32
        v_sq_arr = [0] * 32
        for sub in range(2):
            col = sub * 512
            v_y = [0] * 8
            WT("sync", Scc, v_agy_e if sub == 0 else v_agy_o)
            if sub == 1:
                WT("sync", St_, St_.n)
            for g in range(8):
                yv = yagv_e if sub == 0 else yagv_o
                def _yld(e, g=g, yv=yv):
                    return e.dma_start(
                        out=ab[0:128, Y_O + g * 4096:Y_O + (g + 1) * 4096]
                        .rearrange("p (kk t) -> p kk t", t=512),
                        in_=yv[g, ds(hold["rank"], 1), :, :]
                        .squeeze(0).rearrange("(kk p) t -> p kk t", p=128))
                v_y[g] = EM("sync", _yld, Sld, 16)
            CERT("sync")
            if sub == 0:
                WT("sync", Sv, Sv.n)
                WT("sync", Ss, Ss.n)
                WT("sync", Sgp, Sgp.n)
            v_amm = [0] * 32
            v_ev = [0] * 32
            v_half_done = [0, 0]
            for m in range(32):
                sl = m % 2
                v_wo = [0, 0]
                for hf2 in range(2):
                    WT("sync", St_, v_half_done[hf2])
                    v_wo[hf2] = EM("sync", lambda e, m=m, hf2=hf2: e.dma_start(
                        out=ab[0:128, WOUT_O + hf2 * 4096:WOUT_O + (hf2 + 1) * 4096]
                        .rearrange("p (r q) -> p r q", q=1024),
                        in_=wov[4 * hf2:4 * (hf2 + 1), :, m].rearrange("r p k f -> p r (k f)")),
                        Sld, 16)
                if m >= 2:
                    WT("sync", Sv, v_ev[m - 2])
                v_x0 = EM("sync", lambda e, m=m, col=col, sl=sl: e.dma_start(
                    out=x0p[sl], in_=x0T[m * 128:(m + 1) * 128, col:col + 512]), Sld, 16)
                CERT("sync")
                psm = ps_m0 if sl == 0 else ps_m1
                if m >= 2:
                    WT("tensor", Sv, v_ev[m - 2])
                for kk in range(64):
                    if kk == 0:
                        WT("tensor", Slp, Slp.n)
                    if kk == 32:
                        WT("tensor", Slp, Slp.n)
                    if m == 0 and kk % 8 == 0:
                        WT("tensor", Slp, Slp.n)
                    last = (kk == 63)
                    v = EM("tensor", lambda e, psm=psm, kk=kk: e.matmul(
                        psm[:], ab[0:128, WOUT_O + (kk // 32) * 4096 + (kk % 32) * 128:WOUT_O + (kk // 32) * 4096 + (kk % 32) * 128 + 128],
                        y_t[kk], start=(kk == 0), stop=(kk == 63)),
                        St_ if (last or kk == 31) else None, 1)
                    if kk == 31:
                        v_half_done[0] = v
                    if last:
                        v_amm[m] = v
                v_half_done[1] = v_amm[m]
                WT("vector", St_, v_amm[m])
                WT("vector", Slp, Slp.n)
                pr = m % 2
                if m >= 2:
                    WT("vector", Ss, v_sq_arr[m - 2])
                    WT("vector", Sgp, Sgp.n)
                v_x1 = EM("vector", lambda e, psm=psm, pr=pr: e.tensor_tensor(
                    out=x1p[pr], in0=x0p[pr], in1=psm[:], op=ALU.add), Sv, 1)
                v_ev[m] = v_x1
                WT("scalar", Sv, v_x1)
                v_sq_arr[m] = EM("scalar", lambda e, pr=pr: e.activation(
                    out=scr, in_=x1p[pr], func=AF.Square), Ss, 1)
                WT("tensor", Sv, v_x1)
                WT("tensor", Ss, v_sq_arr[m])
                EM("tensor", lambda e, pr=pr, m=m: e.matmul(
                    ps_d[0:1, :], ones128[:], x1p[pr], start=(m == 0), stop=(m == 31),
                    skip_group_check=True))
                v = EM("tensor", lambda e, m=m: e.matmul(
                    ps_e[0:1, :], ones128[:], scr, start=(m == 0), stop=(m == 31),
                    skip_group_check=True), St_, 1)
                if m == 31:
                    v_ln2stats = v
                WT("gpsimd", Sv, v_x1)
                v_x1s_arr[m] = EM("gpsimd", lambda e, m=m, sub=sub, pr=pr: e.dma_start(
                    out=x1d.ap()[sub, m * 128:(m + 1) * 128, :], in_=x1p[pr]), Sg, 16)
                CERT("gpsimd")

            WT("scalar", St_, v_ln2stats)
            EM("scalar", lambda e: e.activation(out=rowa[:], in_=ps_d[0:1, :], func=AF.Copy, scale=1.0 / H))
            v_m2 = EM("scalar", lambda e: e.activation(out=rowd[:], in_=rowa[:], func=AF.Square), Ss, 1)
            WT("vector", Ss, v_m2)
            v_v2 = EM("vector", lambda e: e.scalar_tensor_tensor(
                out=rowe[:], in0=ps_e[0:1, :], scalar=1.0 / H,
                in1=rowd[:], op0=ALU.mult, op1=ALU.subtract), Sv, 1)
            WT("scalar", Sv, v_v2)
            EM("scalar", lambda e: e.activation(out=rowd[:], in_=rowe[:], func=AF.Ln, bias=epsc[0:1, 0:1]))
            v_r2 = EM("scalar", lambda e: e.activation(out=rowb[:], in_=rowd[:], func=AF.Exp, scale=-0.5), Ss, 1)
            WT("tensor", Ss, v_r2)
            EM("tensor", lambda e: e.matmul(ps_c[:], ones1r[:], rowa[:], start=True, stop=True))
            v_bc2 = EM("tensor", lambda e: e.matmul(ps_d[:], ones1r[:], rowb[:], start=True, stop=True), St_, 1)

            v_xln = [0] * 32
            WT("sync", Sgp, Sgp.n)
            WT("sync", Ss, v_sq_arr[31])
            for kk in range(32):
                pr = kk % 2
                if kk >= 2:
                    WT("sync", Sv, v_xln[kk - 2])
                v_lx = EM("sync", lambda e, kk=kk, sub=sub, pr=pr: e.dma_start(
                    out=x1p[pr], in_=x1d.ap()[sub, kk * 128:(kk + 1) * 128, :]), Sld, 16)
                CERT("sync")
                WT("vector", Slp, Slp.n)
                if kk == 0:
                    WT("vector", St_, v_bc2)
                EM("vector", lambda e, pr=pr: e.tensor_tensor(
                    out=scr, in0=x1p[pr], in1=ps_c[:], op=ALU.subtract))
                v_xln[kk] = EM("vector", lambda e, kk=kk: e.tensor_tensor(
                    out=xln_t[kk], in0=scr, in1=ps_d[:], op=ALU.mult), Sv, 1)

            if sub == 0:
                WT("sync", Scc, v_ag_wffn)
                WT("sync", Sgp, Sgp.n)
            v_fmm = [0] * 32
            v_fev = [0] * 32
            for m in range(32):
                sl = m % 2
                if m == 0:
                    WT("sync", St_, v_amm[31])
                WT("sync", St_, v_wf_free[sl])
                v_wf = EM("sync", lambda e, m=m, sl=sl: e.dma_start(
                    out=wf_sl[sl],
                    in_=wfv[m // 4, :, m % 4].rearrange("p k f -> p (k f)")), Sld, 16)
                if m >= 2:
                    WT("sync", Sv, v_fev[m - 2])
                else:
                    WT("sync", Sv, v_xln[31])
                v_x1r = EM("sync", lambda e, m=m, sub=sub, sl=sl: e.dma_start(
                    out=x1p[sl], in_=x1d.ap()[sub, m * 128:(m + 1) * 128, :]), Sld, 16)
                CERT("sync")
                psm = ps_m0 if sl == 0 else ps_m1
                WT("tensor", Slp, Slp.n)
                if m >= 2:
                    WT("tensor", Sv, v_fev[m - 2])
                for kk in range(32):
                    if m == 0:
                        WT("tensor", Sv, v_xln[kk])
                    last = (kk == 31)
                    v = EM("tensor", lambda e, psm=psm, sl=sl, kk=kk: e.matmul(
                        psm[:], ab[0:128, WFF_O + sl * 4096 + kk * 128:WFF_O + sl * 4096 + (kk + 1) * 128],
                        xln_t[kk], start=(kk == 0), stop=(kk == 31)),
                        St_ if last else None, 1)
                    if last:
                        v_fmm[m] = v
                v_wf_free[sl] = v_fmm[m]
                WT("vector", St_, v_fmm[m])
                WT("vector", Slp, Slp.n)
                pr = m % 2
                WT("vector", Sgp, Sgp.n)
                v_x2 = EM("vector", lambda e, psm=psm, pr=pr, m=m: e.scalar_tensor_tensor(
                    out=otp[pr], in0=psm[:], scalar=biasc_sb[:, m:m + 1],
                    in1=x1p[pr], op0=ALU.add, op1=ALU.add), Sv, 1)
                v_fev[m] = v_x2
                WT("gpsimd", Sv, v_x2)
                v_ost[pr] = EM("gpsimd", lambda e, m=m, col=col, pr=pr: e.dma_start(
                    out=oT[m * 128:(m + 1) * 128, col:col + 512], in_=otp[pr]), Sg, 16)
                CERT("gpsimd")

        if debug:
            dbg = {
                "dbg_hsln": (hsln_all, [8 * H, 1024], BF16),
                "dbg_ssm": (ssmred, [NCHUNK, 288, CT], F32),
                "dbg_ssmp": (ssmpart, [NCHUNK, 288, CT], F32),
                "dbg_u": (dbgu, [NCHUNK, 1024, CT], BF16),
                "dbg_p": (dbgp, [2, 1024, CT], F32),
                "dbg_r": (dbgr, [4096, CT], BF16),
                "dbg_w": (dbgw, [128, 4096], BF16),
                "dbg_g": (dbgg, [NCHUNK, 1024, CT], BF16),
                "dbg_ye": (ybf_e, [8, 1024, CT], BF16),
                "dbg_yage": (yag_e, [64, 1024, CT], BF16),
                "dbg_yo": (ybf_o, [8, 1024, CT], BF16),
                "dbg_x1": (x1d, [2, H, CT], F32),
                "dbg_wffn": (wffn_all, [8 * 128, 16384], BF16),
                "dbg_wout": (wout_all, [8 * 128, 32768], BF16),
            }
            for name, (t, shp, dt_) in dbg.items():
                o = nc.dram_tensor(name, shp, dt_, kind="ExternalOutput").ap()
                EM("gpsimd", lambda e, o=o, t=t: e.dma_start(out=o, in_=t.ap()), Sg, 16)
        WT("gpsimd", Sg, Sg.n)

        block = es.enter_context(nc.Block())
        block.sync(lambda eng: [f(eng) for f in prog["sync"]])
        block.vector(lambda eng: [f(eng) for f in prog["vector"]])
        block.scalar(lambda eng: [f(eng) for f in prog["scalar"]])
        block.tensor(lambda eng: [f(eng) for f in prog["tensor"]])
        block.gpsimd(lambda eng: [f(eng) for f in prog["gpsimd"]])

    return nc


_NC_CACHE = {}


def _host_layout(inputs):
    x = np.asarray(inputs["L0_attn_hidden_states"])
    W_in = np.asarray(inputs["L0_attn_W_in"])
    conv_w = np.asarray(inputs["L0_attn_conv_w"])
    conv_b = np.asarray(inputs["L0_attn_conv_b"])
    W_x = np.asarray(inputs["L0_attn_W_x"])
    W_dt = np.asarray(inputs["L0_attn_W_dt"])
    b_dt = np.asarray(inputs["L0_attn_b_dt"])
    A_log = np.asarray(inputs["L0_attn_A_log"])
    Dv = np.asarray(inputs["L0_attn_D"])
    W_out = np.asarray(inputs["L0_attn_W_out"])
    q3 = np.asarray(inputs["L0_ffn_weight_q3"])
    rng = np.asarray(inputs["L0_ffn_weight_ranges"])
    bias = np.asarray(inputs["L0_ffn_bias"])

    xf = x.reshape(8192, 4096)
    q3_2d = q3.reshape(4096, 4096)
    in_maps = []
    for c in range(NCORE):
        cs, ce = c * 1024, (c + 1) * 1024
        d = {}
        d["x0T"] = np.ascontiguousarray(xf[cs:ce].T)
        d["winx"] = np.ascontiguousarray(
            W_in[cs:ce].reshape(8, 128, 32, 128).transpose(0, 3, 2, 1)).reshape(1024, 4096)
        d["wing"] = np.ascontiguousarray(
            W_in[DIN + cs:DIN + ce].reshape(8, 128, 32, 128).transpose(0, 3, 2, 1)).reshape(1024, 4096)
        d["wxT"] = np.ascontiguousarray(
            W_x[:, cs:ce].T.reshape(8, 128, 288).transpose(1, 0, 2)).reshape(128, 8 * 288)
        d["wdtT"] = np.ascontiguousarray(
            W_dt[cs:ce].T.reshape(2, 128, 1024).transpose(1, 0, 2)).reshape(128, 2048)
        d["wout"] = np.ascontiguousarray(
            W_out[:, cs:ce].reshape(32, 128, 8, 128).transpose(3, 0, 2, 1)).reshape(128, 32768)
        d["q3c"] = np.ascontiguousarray(q3_2d[c * 512:(c + 1) * 512])
        d["rngc"] = np.ascontiguousarray(rng[c * 65536:(c + 1) * 65536].reshape(512, 256))
        d["biasc"] = np.ascontiguousarray(bias.reshape(32, 128).T)
        d["alog"] = np.ascontiguousarray(
            A_log[cs:ce].reshape(8, 128, 16).transpose(1, 0, 2)).reshape(128, 128)
        d["convw"] = np.ascontiguousarray(
            conv_w[cs:ce].reshape(8, 128, 4).transpose(1, 0, 2)).reshape(128, 32)
        d["convb"] = np.ascontiguousarray(conv_b[cs:ce].reshape(8, 128).T)
        d["bdt"] = np.ascontiguousarray(b_dt[cs:ce].reshape(8, 128).T)
        d["dvec"] = np.ascontiguousarray(Dv[cs:ce].reshape(8, 128).T)
        in_maps.append(d)
    return in_maps


def kernel(**inputs):
    in_maps = _host_layout(inputs)
    if "nc" not in _NC_CACHE:
        _NC_CACHE["nc"] = build_program()
    nc = _NC_CACHE["nc"]
    res = run_bass_kernel_spmd(nc, in_maps, core_ids=list(range(NCORE)))
    out = np.empty((8192, 4096), dtype=np.float32)
    for c in range(NCORE):
        out[c * 1024:(c + 1) * 1024] = res.results[c]["oT"].T
    return out.reshape(2, 4096, 4096)
